# version 02
# baseline (speedup 1.0000x reference)
import numpy as np
from contextlib import ExitStack

import concourse.bass as bass
import concourse.mybir as mybir
from concourse.bass import IndirectOffsetOnAxis
from concourse.bass_utils import run_bass_kernel_spmd

F32 = mybir.dt.float32
BF16 = mybir.dt.bfloat16
I32 = mybir.dt.int32
AF = mybir.ActivationFunctionType
ALU = mybir.AluOpType
AX = mybir.AxisListType

NC = 8
B = 4
T = 8192
D = 1024
SL = 512
SPB = T // SL
NBS = B * 2
TPC = B * 2 * SL
EPS = 1e-5
HUGE = 1 << 17


def stripe_of(core, slot):
    return core if slot == 0 else (SPB - 1 - core)


def owner_of(stripe):
    return (stripe, 0) if stripe < NC else (SPB - 1 - stripe, 1)


class Sem:
    def __init__(self, ctx, nc, name):
        self.h = ctx.enter_context(nc.semaphore(name))
        self.n = 0

    def inc(self, instr, k=1):
        instr.then_inc(self.h, k)
        self.n += k
        return instr

    def dma(self, instr):
        return self.inc(instr, 16)

    def wait(self, eng, n=None):
        eng.wait_ge(self.h, self.n if n is None else n)

    def chain(self, eng, instr):
        self.inc(instr, 1)
        self.wait(eng)
        return instr


class Prog:
    def __init__(self):
        self.nc = bass.Bass("TRN2", target_bir_lowering=False, debug=False)
        self.ctx = ExitStack()
        self.ctx.__enter__()
        self._sems = {}
        self._drams = {}
        self._names = set()
        self._scopes = []
        self._blk = None

    def _uniq(self, name):
        n, i = name, 0
        while n in self._names:
            i += 1
            n = f"{name}_{i}"
        self._names.add(n)
        return n

    def push_scope(self):
        st = ExitStack()
        self._scopes.append(st)
        return st

    def pop_scope(self):
        self._scopes.pop().close()

    def _stack(self):
        return self._scopes[-1] if self._scopes else self.ctx

    def sem(self, name):
        if name not in self._sems:
            self._sems[name] = Sem(self.ctx, self.nc, name)
        return self._sems[name]

    def sb(self, name, shape, dtype):
        return self._stack().enter_context(
            self.nc.sbuf_tensor(self._uniq(name), shape, dtype))

    def sb_into(self, stack, name, shape, dtype):
        return stack.enter_context(
            self.nc.sbuf_tensor(self._uniq(name), shape, dtype))

    def ps(self, name, shape, dtype):
        return self._stack().enter_context(
            self.nc.psum_tensor(self._uniq(name), shape, dtype))

    def dram(self, name, shape, dtype, kind="Internal"):
        if name in self._drams:
            ap, sh, dt = self._drams[name]
            assert list(sh) == list(shape) and dt == dtype, name
            return ap
        ap = self.nc.dram_tensor(name, shape, dtype, kind=kind).ap()
        self._drams[name] = (ap, shape, dtype)
        return ap

    def block(self):
        if self._blk is None:
            self._blk = self.ctx.enter_context(self.nc.Block())
        return self._blk

    def finish(self):
        self.ctx.close()
        return self.nc


def load_col(eng, sem, dst_col, src_row):
    sem.dma(eng.dma_start(out=dst_col, in_=src_row.rearrange("a b -> b a")))


def emit_identities(blk, s_g, idb, idf):
    @blk.gpsimd
    def _(g):
        g.memset(idb[:], 0.0)
        s_g.inc(g.memset(idf[:], 0.0))
        s_g.wait(g)
        g.affine_select(out=idb[:], in_=idb[:], compare_op=ALU.not_equal,
                        fill=1.0, base=0, pattern=[[-1, 128]], channel_multiplier=1)
        s_g.inc(g.affine_select(out=idf[:], in_=idf[:], compare_op=ALU.not_equal,
                                fill=1.0, base=0, pattern=[[-1, 128]],
                                channel_multiplier=1))


def emit_weightT(p, blk, wap, dst, dout_blocks, din_blocks, bufs, sems):
    wload, wload16, ps_t, idb = bufs
    s_ld, s_v, s_t, s_g = sems
    for i in range(dout_blocks):
        @blk.sync
        def _(s, i=i):
            s_v.wait(s)
            s_ld.dma(s.dma_start(out=wload[:, 0:128 * din_blocks],
                                 in_=wap[128 * i:128 * (i + 1), 0:128 * din_blocks]))

        @blk.vector
        def _(v, i=i):
            s_ld.wait(v)
            s_t.wait(v)
            s_v.inc(v.tensor_copy(wload16[:, 0:128 * din_blocks],
                                  wload[:, 0:128 * din_blocks]))

        for jj in range((din_blocks + 3) // 4):
            nj = min(4, din_blocks - jj * 4)

            @blk.tensor
            def _(t, i=i, jj=jj, nj=nj):
                s_v.wait(t)
                s_g.wait(t)
                for j4 in range(nj):
                    j = jj * 4 + j4
                    ins = t.transpose(ps_t[:, 128 * j4:128 * (j4 + 1)],
                                      wload16[:, 128 * j:128 * (j + 1)], idb[:])
                s_t.inc(ins)

            @blk.vector
            def _(v, i=i, jj=jj, nj=nj):
                s_t.wait(v)
                for j4 in range(nj):
                    j = jj * 4 + j4
                    ins = v.tensor_copy(dst[:, j, 128 * i:128 * (i + 1)],
                                        ps_t[:, 128 * j4:128 * (j4 + 1)])
                s_v.inc(ins)


def emit_ln_pass(p, blk, x_src, lnD, ag_in, bufs, sems, n_tiles,
                 lnD_rowmaj=True, stage_last=True):
    xt, sq, stat, ln16 = bufs
    s_ld, s_v, s_a, s_st, s_gd = sems
    for k in range(n_tiles):
        bs, ti = divmod(k, 4)
        pb = k % 2

        @blk.sync
        def _(s, k=k, pb=pb):
            s_v.wait(s)
            s_ld.dma(s.dma_start(out=xt[:, pb, :], in_=x_src[k * 128:(k + 1) * 128, :]))

        @blk.vector
        def _(v, pb=pb):
            s_ld.wait(v)
            s_a.wait(v)
            s_v.chain(v, v.tensor_reduce(out=stat[:, 0:1], in_=xt[:, pb, :],
                                         axis=AX.X, op=ALU.add))
            s_v.chain(v, v.tensor_scalar_mul(stat[:, 2:3], stat[:, 0:1], 1.0 / D))
            s_v.inc(v.tensor_scalar(out=sq[:, pb, :], in0=xt[:, pb, :], scalar1=stat[:, 2:3],
                                    scalar2=None, op0=ALU.subtract))

        @blk.scalar
        def _(sc, pb=pb):
            s_v.wait(sc)
            s_a.inc(sc.activation(out=sq[:, pb, :], in_=sq[:, pb, :], func=AF.Square,
                                  accum_out=stat[:, 1:2]))

        @blk.vector
        def _(v):
            s_a.wait(v)
            s_v.inc(v.tensor_scalar(out=stat[:, 3:4], in0=stat[:, 1:2],
                                    scalar1=1.0 / D, scalar2=EPS,
                                    op0=ALU.mult, op1=ALU.add))

        @blk.scalar
        def _(sc):
            s_v.wait(sc)
            s_a.inc(sc.activation(out=stat[:, 4:5], in_=stat[:, 3:4], func=AF.Sqrt))

        @blk.vector
        def _(v, pb=pb):
            s_a.wait(v)
            s_st.wait(v)
            s_gd.wait(v)
            s_v.chain(v, v.reciprocal(stat[:, 5:6], stat[:, 4:5]))
            s_v.chain(v, v.tensor_scalar(out=sq[:, pb, :], in0=xt[:, pb, :],
                                         scalar1=stat[:, 2:3], scalar2=None,
                                         op0=ALU.subtract))
            s_v.inc(v.tensor_scalar(out=ln16[:, pb, :], in0=sq[:, pb, :], scalar1=stat[:, 5:6],
                                    scalar2=None, op0=ALU.mult))

        @blk.scalar
        def _(sc, bs=bs, ti=ti, pb=pb):
            s_v.wait(sc)
            s_st.dma(sc.dma_start(out=lnD[bs, 1 + ti * 128:1 + (ti + 1) * 128, :],
                                  in_=ln16[:, pb, :]))

        if stage_last and ti == 3:
            @blk.gpsimd
            def _(g, bs=bs, pb=pb):
                s_v.wait(g)
                s_gd.dma(g.dma_start(out=ag_in[bs:bs + 1, :],
                                     in_=ln16[127:128, pb, :]))



def build_l0ffn(ext=None):
    p = ext["p"] if ext else Prog()
    nc = p.nc

    x_in = ext["x_in"] if ext else p.dram("x", [TPC, D], F32, kind="ExternalInput")
    w_r = p.dram("w_r", [D, D], F32, kind="ExternalInput")
    w_k = p.dram("w_k", [D, D], F32, kind="ExternalInput")
    w_v = p.dram("w_v", [D, D], F32, kind="ExternalInput")
    mu_rk = p.dram("mu_rk", [2, D], F32, kind="ExternalInput")
    bidx = p.dram("bidx", [NBS, 1], I32, kind="ExternalInput")
    y_out = ext["y_out"] if ext else p.dram("y", [TPC, D], F32, kind="ExternalOutput")

    lnD = p.dram("lnD", [NBS, 1 + SL, D], BF16)
    ag_in = p.dram("ag_in", [NBS, D], BF16)
    ag_out = p.dram("ag_out", [NC * NBS, D], BF16)

    xt = p.sb("xt", [128, 2, D], F32)
    xt2 = p.sb("xt2", [128, D], F32)
    sq = p.sb("sq", [128, 2, D], F32)
    stat = p.sb("stat", [128, 8], F32)
    ln16 = p.sb("ln16", [128, 2, D], BF16)
    wload = p.sb("wload", [128, D], F32)
    wload16 = p.sb("wload16", [128, D], BF16)
    wts = {nm: p.sb(nm + "T", [128, 8, D], BF16) for nm in ("wr", "wk", "wv")}
    mu_t = p.sb("mu_t", [128, 16], F32)
    bidx_sb = p.sb("bidx_sb", [NBS, 1], I32)
    brow = p.sb("brow", [NBS, D], BF16)
    uT = p.sb("uT", [128, 8, SL], BF16)
    upT = p.sb("upT", [128, 8, SL], BF16)
    xrT = p.sb("xrT", [128, 8, SL], BF16)
    xkT = p.sb("xkT", [128, 8, SL], BF16)
    rT = p.sb("rT", [128, 8, SL], BF16)
    kaT = p.sb("kaT", [128, 8, SL], BF16)
    vvT = p.sb("vvT", [128, 8, SL], F32)
    relu_t = p.sb("relu_t", [128, SL], F32)
    idb = p.sb("idb", [128, 128], BF16)
    idf = p.sb("idf", [128, 128], F32)

    ps_t = p.ps("ps_t", [128, 512], BF16)
    ps_tf = p.ps("ps_tf", [128, 512], F32)
    ps_m = p.ps("ps_m", [128, 512], F32)
    ps_m2 = p.ps("ps_m2", [128, 512], F32)

    s_ld = p.sem("s_ld")
    s_st = p.sem("s_st")
    s_g = p.sem("s_g")
    s_gd = p.sem("s_gd")
    s_v = p.sem("s_v")
    s_a = p.sem("s_a")
    s_t = p.sem("s_t")
    s_cc = p.sem("s_cc")
    s_out = p.sem("s_out")

    blk = p.block()

    emit_identities(blk, s_g, idb, idf)

    @blk.sync
    def _(s):
        s_ld.dma(s.dma_start(out=bidx_sb[:], in_=bidx))
        for ch in range(8):
            load_col(s, s_ld, mu_t[:, 2 * ch:2 * ch + 1],
                     mu_rk[0:1, 128 * ch:128 * (ch + 1)])
            load_col(s, s_ld, mu_t[:, 2 * ch + 1:2 * ch + 2],
                     mu_rk[1:2, 128 * ch:128 * (ch + 1)])

    for nm, wap in (("wr", w_r), ("wk", w_k), ("wv", w_v)):
        emit_weightT(p, blk, wap, wts[nm], 8, 8,
                     (wload, wload16, ps_t, idb), (s_ld, s_v, s_t, s_g))

    emit_ln_pass(p, blk, x_in, lnD, ag_in, (xt, sq, stat, ln16),
                 (s_ld, s_v, s_a, s_st, s_gd), TPC // 128)

    @blk.gpsimd
    def _(g):
        s_gd.wait(g)
        s_st.wait(g)
        s_cc.inc(g.collective_compute(
            "AllGather", ALU.bypass, replica_groups=[list(range(NC))],
            ins=[ag_in], outs=[ag_out]))
        s_cc.wait(g)
        s_g.inc(g.memset(brow[:], 0.0))
        s_g.wait(g)
        s_ld.wait(g)
        s_gd.dma(g.indirect_dma_start(
            out=brow[:], out_offset=None, in_=ag_out,
            in_offset=IndirectOffsetOnAxis(bidx_sb[:, 0:1], axis=0),
            bounds_check=NC * NBS - 1, oob_is_err=False))
        s_gd.wait(g)
        for bs in range(NBS):
            s_gd.dma(g.dma_start(out=lnD[bs, 0:1, :], in_=brow[bs:bs + 1, :]))

    for bs in range(NBS):
        for ti in range(4):
            @blk.sync
            def _(s, bs=bs, ti=ti):
                if ti == 0:
                    s_g.wait(s)
                    s_gd.wait(s)
                    s_st.wait(s)
                    s_v.wait(s)
                else:
                    s_v.wait(s)
                s_ld.dma(s.dma_start(out=ln16[:, 0, :],
                                     in_=lnD[bs, 1 + ti * 128:1 + (ti + 1) * 128, :]))
                s_ld.dma(s.dma_start(out=ln16[:, 1, :],
                                     in_=lnD[bs, ti * 128:(ti + 1) * 128, :]))

            for src, dstbuf in ((0, uT), (1, upT)):
                for jj in range(2):
                    @blk.tensor
                    def _(t, src=src, jj=jj):
                        s_ld.wait(t)
                        s_v.wait(t)
                        s_g.wait(t)
                        for j4 in range(4):
                            j = jj * 4 + j4
                            ins = t.transpose(ps_t[:, 128 * j4:128 * (j4 + 1)],
                                              ln16[:, src, 128 * j:128 * (j + 1)], idb[:])
                        s_t.inc(ins)

                    @blk.vector
                    def _(v, dstbuf=dstbuf, ti=ti, jj=jj):
                        s_t.wait(v)
                        for j4 in range(4):
                            j = jj * 4 + j4
                            ins = v.tensor_copy(dstbuf[:, j, 128 * ti:128 * (ti + 1)],
                                                ps_t[:, 128 * j4:128 * (j4 + 1)])
                        s_v.inc(ins)

        @blk.vector
        def _(v, bs=bs):
            s_v.wait(v)
            for j in range(8):
                ins = v.tensor_sub(xrT[:, j, :], uT[:, j, :], upT[:, j, :])
            s_v.chain(v, ins)
            for j in range(8):
                v.scalar_tensor_tensor(out=xkT[:, j, :], in0=xrT[:, j, :],
                                       scalar=mu_t[:, 2 * j + 1:2 * j + 2],
                                       in1=upT[:, j, :], op0=ALU.mult, op1=ALU.add)
            for j in range(8):
                ins = v.scalar_tensor_tensor(out=xrT[:, j, :], in0=xrT[:, j, :],
                                             scalar=mu_t[:, 2 * j:2 * j + 1],
                                             in1=upT[:, j, :], op0=ALU.mult, op1=ALU.add)
            s_v.inc(ins)

        for i in range(8):
            @blk.tensor
            def _(t, i=i):
                s_v.wait(t)
                s_a.wait(t)
                for j in range(8):
                    ins = t.matmul(ps_m[:, :], wts["wr"][:, j, 128 * i:128 * (i + 1)],
                                   xrT[:, j, :], start=(j == 0), stop=(j == 7))
                s_t.inc(ins)
                for j in range(8):
                    ins = t.matmul(ps_m2[:, :], wts["wk"][:, j, 128 * i:128 * (i + 1)],
                                   xkT[:, j, :], start=(j == 0), stop=(j == 7))
                s_t.inc(ins)

            @blk.scalar
            def _(sc, i=i):
                s_t.wait(sc, s_t.n - 1)
                s_a.inc(sc.activation(out=rT[:, i, :], in_=ps_m[:, :], func=AF.Sigmoid))

            @blk.vector
            def _(v, i=i):
                s_t.wait(v)
                s_a.wait(v, s_a.n - 1)
                s_v.inc(v.tensor_scalar_max(relu_t[:, :], ps_m2[:, :], 0.0))

            @blk.scalar
            def _(sc, i=i):
                s_v.wait(sc)
                s_a.inc(sc.activation(out=kaT[:, i, :], in_=relu_t[:, :], func=AF.Square))

        for i in range(8):
            @blk.tensor
            def _(t, i=i):
                s_a.wait(t)
                s_v.wait(t)
                for j in range(8):
                    ins = t.matmul(ps_m[:, :], wts["wv"][:, j, 128 * i:128 * (i + 1)],
                                   kaT[:, j, :], start=(j == 0), stop=(j == 7))
                s_t.inc(ins)

            @blk.vector
            def _(v, i=i):
                s_t.wait(v)
                s_v.inc(v.tensor_mul(vvT[:, i, :], rT[:, i, :], ps_m[:, :]))

        for ti in range(4):
            @blk.sync
            def _(s, bs=bs, ti=ti):
                s_out.wait(s)
                row0 = bs * SL + ti * 128
                s_ld.dma(s.dma_start(out=xt2[:], in_=x_in[row0:row0 + 128, :]))

            for half in range(2):
                @blk.tensor
                def _(t, ti=ti, half=half):
                    s_v.wait(t)
                    for i4 in range(4):
                        i = half * 4 + i4
                        ins = t.transpose(ps_tf[:, 128 * i4:128 * (i4 + 1)],
                                          vvT[:, i, 128 * ti:128 * (ti + 1)], idf[:])
                    s_t.inc(ins)

                @blk.vector
                def _(v, ti=ti, half=half):
                    s_t.wait(v)
                    s_ld.wait(v)
                    for i4 in range(4):
                        i = half * 4 + i4
                        ins = v.scalar_tensor_tensor(
                            out=xt2[:, 128 * i:128 * (i + 1)],
                            in0=ps_tf[:, 128 * i4:128 * (i4 + 1)],
                            scalar=0.5, in1=xt2[:, 128 * i:128 * (i + 1)],
                            op0=ALU.mult, op1=ALU.add)
                    s_v.inc(ins)

            @blk.scalar
            def _(sc, bs=bs, ti=ti):
                s_v.wait(sc)
                row0 = bs * SL + ti * 128
                s_out.dma(sc.dma_start(out=y_out[row0:row0 + 128, :], in_=xt2[:]))
                s_a.inc(sc.nop())

    if ext:
        return

    @blk.gpsimd
    def _(g):
        s_out.wait(g)

    return p.finish()



def shard_x(x):
    shards = []
    for c in range(NC):
        parts = []
        for b in range(B):
            for slot in range(2):
                st = stripe_of(c, slot)
                parts.append(x[b, st * SL:(st + 1) * SL, :])
        shards.append(np.ascontiguousarray(np.concatenate(parts, axis=0)))
    return shards


def unshard_x(shards):
    x = np.empty((B, T, D), dtype=np.asarray(shards[0]).dtype)
    for c in range(NC):
        sh = np.asarray(shards[c]).reshape(B, 2, SL, D)
        for b in range(B):
            for slot in range(2):
                st = stripe_of(c, slot)
                x[b, st * SL:(st + 1) * SL, :] = sh[b, slot]
    return x


def l0ffn_bidx(core):
    out = np.full((NBS, 1), HUGE, dtype=np.int32)
    for b in range(B):
        for slot in range(2):
            st = stripe_of(core, slot)
            if st == 0:
                continue
            pr, ps = owner_of(st - 1)
            out[b * 2 + slot, 0] = pr * NBS + b * 2 + ps
    return out


_L0FFN_NC = None


def run_l0ffn(xshards, W_r, W_k, W_v, mu_r, mu_k):
    global _L0FFN_NC
    if _L0FFN_NC is None:
        _L0FFN_NC = build_l0ffn()
    mu = np.stack([mu_r, mu_k]).astype(np.float32)
    in_maps = []
    for c in range(NC):
        in_maps.append(dict(x=xshards[c], w_r=W_r, w_k=W_k, w_v=W_v,
                            mu_rk=mu, bidx=l0ffn_bidx(c)))
    res = run_bass_kernel_spmd(_L0FFN_NC, in_maps, core_ids=list(range(NC)))
    return [np.asarray(res.results[c]["y"]) for c in range(NC)]



GAMMA = 1.0 / 32.0
MNEG = -30000.0


def attn_steps(core):
    steps = []
    for b in range(B):
        for slot in range(2):
            s = stripe_of(core, slot)
            for kb in range(s + 1):
                steps.append((b * 2 + slot, kb))
    return steps


def attn_tables(core):
    steps = attn_steps(core)
    ns = len(steps)
    qidx = np.zeros((128, ns * 8), dtype=np.int32)
    kidx = np.zeros((128, ns * 8), dtype=np.int32)
    vidx = np.zeros((128, ns * 4), dtype=np.int32)
    thr = np.zeros((128, ns * 4), dtype=np.float32)
    rf = np.zeros((128, ns), dtype=np.float32)
    rb = np.zeros((128, ns), dtype=np.float32)
    ysc = np.full((128, (ns + 1) * 4), HUGE, dtype=np.int32)
    p = np.arange(128)
    prev_bs = -1
    for k, (bs, kb) in enumerate(steps):
        b, slot = divmod(bs, 2)
        s = stripe_of(core, slot)
        for op in range(8):
            dr = op * 128 + p
            qidx[:, k * 8 + op] = dr * (TPC // SL) + bs
        rk, kslot = owner_of(kb)
        kbs = b * 2 + kslot
        for op in range(8):
            dr = rk * D + op * 128 + p
            kidx[:, k * 8 + op] = dr * (TPC // SL) + kbs
        for op in range(4):
            vidx[:, k * 4 + op] = rk * TPC + kbs * SL + op * 128 + p
        for qi in range(4):
            if kb == s:
                thr[:, k * 4 + qi] = qi * 128 + p
            else:
                thr[:, k * 4 + qi] = 1e4
        newstripe = bs != prev_bs
        rf[:, k] = 0.0 if newstripe else 1.0
        rb[:, k] = MNEG if newstripe else 0.0
        if newstripe and prev_bs >= 0:
            for qi in range(4):
                ysc[:, k * 4 + qi] = prev_bs * SL + qi * 128 + p
        prev_bs = bs
    for qi in range(4):
        ysc[:, ns * 4 + qi] = prev_bs * SL + qi * 128 + p
    return dict(qidx=qidx, kidx=kidx, vidx=vidx, thr=thr, rf=rf, rb=rb, ysc=ysc)


def build_l0attn(debug_dump=False, dbg_steps=None, ext=None):
    p = ext["p"] if ext else Prog()
    nc = p.nc
    ns = 17 * B

    x_in = ext["x_in"] if ext else p.dram("x", [TPC, D], F32, kind="ExternalInput")
    w_q = p.dram("wq_at", [D, D], F32, kind="ExternalInput")
    w_k = p.dram("wk_at", [D, D], F32, kind="ExternalInput")
    w_v = p.dram("wv_at", [D, D], F32, kind="ExternalInput")
    qidx_d = p.dram("qidx", [128, ns * 8], I32, kind="ExternalInput")
    kidx_d = p.dram("kidx", [128, ns * 8], I32, kind="ExternalInput")
    vidx_d = p.dram("vidx", [128, ns * 4], I32, kind="ExternalInput")
    thr_d = p.dram("thr", [128, ns * 4], F32, kind="ExternalInput")
    rf_d = p.dram("rf", [128, ns], F32, kind="ExternalInput")
    rb_d = p.dram("rb", [128, ns], F32, kind="ExternalInput")
    ysc_d = p.dram("ysc", [128, (ns + 1) * 4], I32, kind="ExternalInput")
    y_out = ext["y_out"] if ext else p.dram("y_at", [TPC, D], F32, kind="ExternalOutput")

    lnD = p.dram("lnD", [NBS, 1 + SL, D], BF16)
    qT = p.dram("qT", [D, TPC], BF16)
    agK_in = p.dram("agK_in", [D, TPC], BF16)
    agK = p.dram("agK", [NC * D, TPC], BF16)
    agV_in = p.dram("agV_in", [TPC, D], BF16)
    agV = p.dram("agV", [NC * TPC, D], BF16)
    yD = p.dram("yD", [TPC, D], F32)

    xt = p.sb("xt", [128, 2, D], F32)
    sq = p.sb("sq", [128, 2, D], F32)
    stat = p.sb("stat", [128, 8], F32)
    ln16 = p.sb("ln16", [128, 2, D], BF16)
    wsb = {nm: p.sb("w" + nm, [128, 8, D], BF16) for nm in ("q", "k", "v")}
    lnTw = p.sb("lnTw", [128, 8, SL], BF16)
    otile = p.sb("otile", [128, 2, D], BF16)
    qTs = p.sb("qTs", [128, 8, SL], BF16)
    kTs = p.sb("kTs", [128, 8, SL], BF16)
    vs = p.sb("vs", [128, 4, D], BF16)
    idxcols = p.sb("idxcols", [128, ns * 8], I32)
    idxcols2 = p.sb("idxcols2", [128, ns * 8], I32)
    idxcols3 = p.sb("idxcols3", [128, ns * 4], I32)
    thr_sb = p.sb("thr_sb", [128, ns * 4], F32)
    rf_sb = p.sb("rf_sb", [128, ns], F32)
    rb_sb = p.sb("rb_sb", [128, ns], F32)
    ysc_sb = p.sb("ysc_sb", [128, (ns + 1) * 4], I32)
    kpos = p.sb("kpos", [128, SL], F32)
    kpos_i = p.sb("kpos_i", [128, SL], I32)
    scf = p.sb("scf", [128, SL], F32)
    pmat = p.sb("pmat", [128, SL], BF16)
    pTs = p.sb("pTs", [128, 4, 128], BF16)
    mrow = p.sb("mrow", [128, 16], F32)
    acc = p.sb("acc", [128, 4, D], F32)
    mstate = p.sb("mstate", [128, 4], F32)
    lstate = p.sb("lstate", [128, 4], F32)
    yfin = p.sb("yfin", [128, 4, D], F32)
    idb = p.sb("idb", [128, 128], BF16)
    idf = p.sb("idf", [128, 128], F32)

    ps_t = p.ps("ps_t", [128, 512], BF16)
    ps_s = p.ps("ps_s", [128, 512], F32)
    ps_av = p.ps("ps_av", [128, 512], F32)
    ps_v = p.ps("ps_v", [128, D], F32)

    s_ld = p.sem("s_ld")
    s_st = p.sem("s_st")
    s_g = p.sem("s_g")
    s_gd = p.sem("s_gd")
    s_v = p.sem("s_v")
    s_a = p.sem("s_a")
    s_t = p.sem("s_t")
    s_cc = p.sem("s_cc")
    s_out = p.sem("s_out")

    blk = p.block()
    bregs = {}

    emit_identities(blk, s_g, idb, idf)

    @blk.sync
    def _(s):
        s_ld.dma(s.dma_start(out=idxcols[:], in_=qidx_d))
        s_ld.dma(s.dma_start(out=idxcols2[:], in_=kidx_d))
        s_ld.dma(s.dma_start(out=idxcols3[:], in_=vidx_d))
        s_ld.dma(s.dma_start(out=thr_sb[:], in_=thr_d))
        s_ld.dma(s.dma_start(out=rf_sb[:], in_=rf_d))
        s_ld.dma(s.dma_start(out=rb_sb[:], in_=rb_d))
        s_ld.dma(s.dma_start(out=ysc_sb[:], in_=ysc_d))

    @blk.gpsimd
    def _(g):
        for nm, wap in (("q", w_q), ("k", w_k), ("v", w_v)):
            for kc in range(8):
                s_gd.dma(g.dma_start(out=wsb[nm][:, kc, :],
                                     in_=wap[128 * kc:128 * (kc + 1), :]))
        s_g.chain(g, g.iota(kpos_i[:], pattern=[[1, SL]], base=0, channel_multiplier=0))
        s_g.inc(g.tensor_copy(kpos[:], kpos_i[:]))
        g.memset(acc[:], 0.0)
        g.memset(mrow[:], 0.0)
        g.memset(lstate[:], 1e-30)
        s_g.inc(g.memset(mstate[:], MNEG))
        bregs["yd"] = g.to_reg(TPC - 1)

    emit_ln_pass(p, blk, x_in, lnD, None, (xt, sq, stat, ln16),
                 (s_ld, s_v, s_a, s_st, s_gd), TPC // 128, stage_last=False)

    for bs in range(NBS):
        for ti in range(4):
            @blk.sync
            def _(s, bs=bs, ti=ti):
                if bs == 0 and ti == 0:
                    s_st.wait(s)
                s_v.wait(s)
                s_a.wait(s)
                s_ld.dma(s.dma_start(out=ln16[:, 0, :],
                                     in_=lnD[bs, 1 + ti * 128:1 + (ti + 1) * 128, :]))

            for jj in range(2):
                @blk.tensor
                def _(t, jj=jj):
                    s_ld.wait(t)
                    s_v.wait(t)
                    s_g.wait(t)
                    for j4 in range(4):
                        j = jj * 4 + j4
                        ins = t.transpose(ps_t[:, 128 * j4:128 * (j4 + 1)],
                                          ln16[:, 0, 128 * j:128 * (j + 1)], idb[:])
                    s_t.inc(ins)

                @blk.vector
                def _(v, ti=ti, jj=jj):
                    s_t.wait(v)
                    for j4 in range(4):
                        j = jj * 4 + j4
                        ins = v.tensor_copy(lnTw[:, j, 128 * ti:128 * (ti + 1)],
                                            ps_t[:, 128 * j4:128 * (j4 + 1)])
                    s_v.inc(ins)

        for nm, dstD in (("q", qT), ("k", agK_in)):
            for i in range(8):
                pb = i % 2

                @blk.tensor
                def _(t, nm=nm, i=i):
                    s_v.wait(t)
                    s_a.wait(t)
                    s_gd.wait(t)
                    for j in range(8):
                        ins = t.matmul(ps_s[:, :], wsb[nm][:, j, 128 * i:128 * (i + 1)],
                                       lnTw[:, j, :], start=(j == 0), stop=(j == 7))
                    s_t.inc(ins)

                @blk.vector
                def _(v, pb=pb):
                    s_t.wait(v)
                    s_st.wait(v)
                    s_v.inc(v.tensor_copy(otile[:, pb, 0:SL], ps_s[:, :]))

                @blk.scalar
                def _(sc, nm=nm, dstD=dstD, bs=bs, i=i, pb=pb):
                    s_v.wait(sc)
                    s_st.dma(sc.dma_start(out=dstD[128 * i:128 * (i + 1),
                                                   bs * SL:(bs + 1) * SL],
                                          in_=otile[:, pb, 0:SL]))
                    s_a.inc(sc.nop())

        for ti in range(4):
            @blk.tensor
            def _(t, ti=ti):
                s_v.wait(t)
                s_a.wait(t)
                s_gd.wait(t)
                for half in range(2):
                    for j in range(8):
                        ins = t.matmul(ps_v[:, 512 * half:512 * (half + 1)],
                                       lnTw[:, j, 128 * ti:128 * (ti + 1)],
                                       wsb["v"][:, j, 512 * half:512 * (half + 1)],
                                       start=(j == 0), stop=(j == 7))
                s_t.inc(ins)

            @blk.vector
            def _(v, ti=ti):
                s_t.wait(v)
                s_st.wait(v)
                s_v.inc(v.tensor_copy(otile[:, ti % 2, :], ps_v[:, :]))

            @blk.scalar
            def _(sc, bs=bs, ti=ti):
                s_v.wait(sc)
                s_st.dma(sc.dma_start(out=agV_in[bs * SL + ti * 128:bs * SL + (ti + 1) * 128, :],
                                      in_=otile[:, ti % 2, :]))
                s_a.inc(sc.nop())

    @blk.gpsimd
    def _(g):
        s_st.wait(g)
        s_cc.inc(g.collective_compute("AllGather", ALU.bypass,
                                      replica_groups=[list(range(NC))],
                                      ins=[agK_in], outs=[agK]))
        s_cc.inc(g.collective_compute("AllGather", ALU.bypass,
                                      replica_groups=[list(range(NC))],
                                      ins=[agV_in], outs=[agV]))
        s_cc.wait(g)

    if debug_dump:
        dbg = {nm: p.dram("dbg_" + nm, shape, F32, kind="ExternalOutput")
               for nm, shape in [("yD", [TPC, D]), ("qTd", [D, TPC]), ("agKin", [D, TPC]),
                                 ("wq", [128, 8 * D]), ("lnTw", [128, 8 * SL]),
                                 ("lnDd", [NBS * (1 + SL), D]),
                                 ("scf", [128, SL]), ("pmat", [128, SL]),
                                 ("acc", [128, 4 * D]), ("lstate", [128, 4]),
                                 ("mstate", [128, 4]), ("kTs", [128, 8 * SL]),
                                 ("qTs", [128, 8 * SL]), ("vsb", [128, 4 * D]),
                                 ("mrow", [128, 16])]}
    qT_f = qT.rearrange("a (r c) -> (a r) c", c=SL)
    agK_f = agK.rearrange("a (r c) -> (a r) c", c=SL)
    agV_f = agV
    yD_f = yD

    for k in range(ns + 1 if dbg_steps is None else dbg_steps):
        is_flush = (k == ns)

        @blk.vector
        def _(v, k=k):
            if k == 0:
                s_cc.wait(v)
                s_g.wait(v)
            else:
                s_v.wait(v)
                s_gd.wait(v)
            for qi in range(4):
                ins = v.reciprocal(mrow[:, 8 + qi:9 + qi], lstate[:, qi:qi + 1])
            s_v.chain(v, ins)
            for qi in range(4):
                ins = v.tensor_scalar(out=yfin[:, qi, :], in0=acc[:, qi, :],
                                      scalar1=mrow[:, 8 + qi:9 + qi], scalar2=None,
                                      op0=ALU.mult)
            s_v.inc(ins)

        @blk.gpsimd
        def _(g, k=k):
            s_v.wait(g)
            for qi in range(4):
                ins = g.indirect_dma_start(
                    out=yD_f, out_offset=IndirectOffsetOnAxis(
                        ysc_sb[:, 4 * k + qi:4 * k + qi + 1], axis=0),
                    in_=yfin[:, qi, :], in_offset=None,
                    bounds_check=bregs["yd"], oob_is_err=False)
                s_gd.inc(ins, 16)

        if is_flush:
            break

        @blk.gpsimd
        def _(sc, k=k):
            s_t.wait(sc)
            for op in range(8):
                s_gd.dma(sc.indirect_dma_start(
                    out=qTs[:, op, :], out_offset=None, in_=qT_f,
                    in_offset=IndirectOffsetOnAxis(
                        idxcols[:, 8 * k + op:8 * k + op + 1], axis=0)))
            for op in range(8):
                s_gd.dma(sc.indirect_dma_start(
                    out=kTs[:, op, :], out_offset=None, in_=agK_f,
                    in_offset=IndirectOffsetOnAxis(
                        idxcols2[:, 8 * k + op:8 * k + op + 1], axis=0)))
            for op in range(4):
                s_gd.dma(sc.indirect_dma_start(
                    out=vs[:, op, :], out_offset=None, in_=agV_f,
                    in_offset=IndirectOffsetOnAxis(
                        idxcols3[:, 4 * k + op:4 * k + op + 1], axis=0)))

        for qi in range(4):
            @blk.tensor
            def _(t, k=k, qi=qi):
                if qi == 0:
                    s_gd.wait(t)
                s_v.wait(t)
                for j in range(8):
                    ins = t.matmul(ps_s[:, :], qTs[:, j, 128 * qi:128 * (qi + 1)],
                                   kTs[:, j, :], start=(j == 0), stop=(j == 7))
                s_t.inc(ins)

            @blk.vector
            def _(v, k=k, qi=qi):
                s_t.wait(v)
                s_a.wait(v)
                s_v.chain(v, v.tensor_scalar(out=scf[:], in0=kpos[:],
                                             scalar1=thr_sb[:, 4 * k + qi:4 * k + qi + 1],
                                             scalar2=None, op0=ALU.is_gt))
                s_v.chain(v, v.scalar_tensor_tensor(out=scf[:], in0=scf[:], scalar=MNEG,
                                                    in1=ps_s[:, :], op0=ALU.mult, op1=ALU.add))
                v.reduce_max(mrow[:, 2:3], scf[:], axis=AX.X)
                s_v.chain(v, v.tensor_scalar(out=mrow[:, 1:2], in0=mstate[:, qi:qi + 1],
                                             scalar1=rf_sb[:, k:k + 1], scalar2=rb_sb[:, k:k + 1],
                                             op0=ALU.mult, op1=ALU.add))
                s_v.chain(v, v.tensor_tensor(out=mrow[:, 3:4], in0=mrow[:, 1:2],
                                             in1=mrow[:, 2:3], op=ALU.max))
                v.tensor_copy(mstate[:, qi:qi + 1], mrow[:, 3:4])
                s_v.chain(v, v.tensor_scalar(out=mrow[:, 7:8], in0=mrow[:, 3:4],
                                             scalar1=-GAMMA, scalar2=None, op0=ALU.mult))
                s_v.chain(v, v.tensor_sub(mrow[:, 0:1], mrow[:, 1:2], mrow[:, 3:4]))
                ins = v.tensor_scalar(out=mrow[:, 0:1], in0=mrow[:, 0:1], scalar1=GAMMA,
                                      scalar2=None, op0=ALU.mult)
                s_v.inc(ins)

            @blk.scalar
            def _(sc, k=k, qi=qi):
                s_v.wait(sc)
                sc.activation(out=pmat[:], in_=scf[:], func=AF.Exp, scale=GAMMA,
                              bias=mrow[:, 7:8], accum_out=mrow[:, 6:7])
                ins = sc.activation(out=mrow[:, 4:5], in_=mrow[:, 0:1], func=AF.Exp)
                s_a.inc(ins)

            @blk.vector
            def _(v, k=k, qi=qi):
                s_a.wait(v)
                ins = v.scalar_tensor_tensor(out=lstate[:, qi:qi + 1],
                                             in0=lstate[:, qi:qi + 1],
                                             scalar=mrow[:, 4:5], in1=mrow[:, 6:7],
                                             op0=ALU.mult, op1=ALU.add)
                s_v.inc(ins)

            for kc in range(4):
                pass

            @blk.tensor
            def _(t, k=k, qi=qi):
                s_a.wait(t)
                s_v.wait(t)
                for kc in range(4):
                    ins = t.transpose(ps_t[:, 128 * kc:128 * (kc + 1)],
                                      pmat[:, 128 * kc:128 * (kc + 1)], idb[:])
                s_t.inc(ins)

            @blk.vector
            def _(v, k=k, qi=qi):
                s_t.wait(v)
                for kc in range(4):
                    ins = v.tensor_copy(pTs[:, kc, :], ps_t[:, 128 * kc:128 * (kc + 1)])
                s_v.inc(ins)

            for half in range(2):
                @blk.tensor
                def _(t, k=k, qi=qi, half=half):
                    s_v.wait(t)
                    if half == 0:
                        pass
                    else:
                        s_v.wait(t)
                    for kc in range(4):
                        ins = t.matmul(ps_av[:, :], pTs[:, kc, :],
                                       vs[:, kc, 512 * half:512 * (half + 1)],
                                       start=(kc == 0), stop=(kc == 3))
                    s_t.inc(ins)

                @blk.vector
                def _(v, k=k, qi=qi, half=half):
                    s_t.wait(v)
                    ins = v.scalar_tensor_tensor(
                        out=acc[:, qi, 512 * half:512 * (half + 1)],
                        in0=acc[:, qi, 512 * half:512 * (half + 1)],
                        scalar=mrow[:, 4:5], in1=ps_av[:, :],
                        op0=ALU.mult, op1=ALU.add)
                    s_v.inc(ins)

    if debug_dump:
        @blk.gpsimd
        def _(g):
            s_v.wait(g)
            s_a.wait(g)
            s_t.wait(g)
            s_gd.dma(g.dma_start(out=dbg["scf"], in_=scf[:]))
            s_gd.dma(g.dma_start(out=dbg["pmat"], in_=pmat[:]))
            s_gd.dma(g.dma_start(out=dbg["acc"], in_=acc[:].rearrange("p a b -> p (a b)")))
            s_gd.dma(g.dma_start(out=dbg["lstate"], in_=lstate[:]))
            s_gd.dma(g.dma_start(out=dbg["mstate"], in_=mstate[:]))
            s_gd.dma(g.dma_start(out=dbg["kTs"], in_=kTs[:].rearrange("p a b -> p (a b)")))
            s_gd.dma(g.dma_start(out=dbg["qTs"], in_=qTs[:].rearrange("p a b -> p (a b)")))
            s_gd.dma(g.dma_start(out=dbg["vsb"], in_=vs[:].rearrange("p a b -> p (a b)")))
            s_gd.dma(g.dma_start(out=dbg["mrow"], in_=mrow[:]))
            s_gd.dma(g.dma_start(out=dbg["yD"], in_=yD))
            s_gd.dma(g.dma_start(out=dbg["qTd"], in_=qT))
            s_gd.dma(g.dma_start(out=dbg["agKin"], in_=agK_in))
            s_gd.dma(g.dma_start(out=dbg["wq"], in_=wsb["q"][:].rearrange("p a b -> p (a b)")))
            s_gd.dma(g.dma_start(out=dbg["lnTw"], in_=lnTw[:].rearrange("p a b -> p (a b)")))
            s_gd.dma(g.dma_start(out=dbg["lnDd"], in_=lnD.rearrange("a b c -> (a b) c")))
            s_gd.wait(g)

    for k in range(TPC // 128 if dbg_steps is None else 0):
        pb = k % 2

        @blk.sync
        def _(s, k=k, pb=pb):
            s_gd.wait(s)
            s_out.wait(s)
            s_ld.dma(s.dma_start(out=xt[:, pb, :], in_=x_in[128 * k:128 * (k + 1), :]))
            s_ld.dma(s.dma_start(out=sq[:, pb, :], in_=yD[128 * k:128 * (k + 1), :]))

        @blk.vector
        def _(v, pb=pb):
            s_ld.wait(v)
            s_v.inc(v.tensor_add(xt[:, pb, :], xt[:, pb, :], sq[:, pb, :]))

        @blk.scalar
        def _(sc, k=k, pb=pb):
            s_v.wait(sc)
            s_out.dma(sc.dma_start(out=y_out[128 * k:128 * (k + 1), :], in_=xt[:, pb, :]))
            s_a.inc(sc.nop())

    if ext:
        return

    @blk.gpsimd
    def _(g):
        s_out.wait(g)

    return p.finish()


_L0ATTN_NC = None


def run_l0attn(xshards, w_q, w_k, w_v):
    global _L0ATTN_NC
    if _L0ATTN_NC is None:
        _L0ATTN_NC = build_l0attn()
    in_maps = []
    for c in range(NC):
        tb = attn_tables(c)
        in_maps.append(dict(x=xshards[c], wq_at=w_q, wk_at=w_k, wv_at=w_v,
                            qidx=tb["qidx"], kidx=tb["kidx"], vidx=tb["vidx"],
                            thr=tb["thr"], rf=tb["rf"], rb=tb["rb"], ysc=tb["ysc"]))
    res = run_bass_kernel_spmd(_L0ATTN_NC, in_maps, core_ids=list(range(NC)))
    return [np.asarray(res.results[c]["y_at"]) for c in range(NC)]



DIN = 2048
NSTATE = 16
NDT = DIN // 128


def l1attn_tables(core):
    p = np.arange(128)
    cbidx = np.full((128, NBS * NDT), HUGE, dtype=np.int32)
    for b in range(B):
        for slot in range(2):
            st = stripe_of(core, slot)
            if st == 0:
                continue
            pr, ps = owner_of(st - 1)
            pbs = b * 2 + ps
            for dt in range(NDT):
                cbidx[:, (b * 2 + slot) * NDT + dt] = \
                    pr * (NBS * DIN) + pbs * DIN + dt * 128 + p
    smask = np.zeros((128, SPB * 2), dtype=np.float32)
    for slot in range(2):
        smask[:, stripe_of(core, slot) * 2 + slot] = 1.0
    return cbidx, smask


def build_l1attn(ext=None):
    p = ext["p"] if ext else Prog()
    nc = p.nc

    x_in = ext["x_in"] if ext else p.dram("x", [TPC, D], F32, kind="ExternalInput")
    w_in = p.dram("w_in", [2 * DIN, D], F32, kind="ExternalInput")
    conv_w = p.dram("conv_w", [DIN, 4], F32, kind="ExternalInput")
    conv_b = p.dram("conv_b", [1, DIN], F32, kind="ExternalInput")
    w_x = p.dram("w_x", [33, DIN], F32, kind="ExternalInput")
    w_dt = p.dram("w_dt", [1, DIN], F32, kind="ExternalInput")
    b_dt = p.dram("b_dt", [1, DIN], F32, kind="ExternalInput")
    a_log = p.dram("a_log", [DIN, NSTATE], F32, kind="ExternalInput")
    d_par = p.dram("d_par", [1, DIN], F32, kind="ExternalInput")
    w_out = p.dram("w_out", [D, DIN], F32, kind="ExternalInput")
    cbidx_d = p.dram("cbidx", [128, NBS * NDT], I32, kind="ExternalInput")
    smask_d = p.dram("smask", [128, SPB * 2], F32, kind="ExternalInput")
    y_out = ext["y_out"] if ext else p.dram("y_ma", [TPC, D], F32, kind="ExternalOutput")

    lnD = p.dram("lnD", [NBS, 1 + SL, D], BF16)
    xzpreD = p.dram("xzpreD", [NBS, DIN, SL], BF16)
    zD = p.dram("zD", [NBS, DIN, SL], BF16)
    xcD = p.dram("xcD", [NBS, DIN, SL], BF16)
    csD = p.dram("csD", [NBS, DIN, SL], F32)
    ypD = p.dram("ypD", [NBS, DIN, SL], F32)
    yfD = p.dram("yfD", [NBS, DIN, SL], BF16)
    xpD = p.dram("xpD", [NBS, 33, SL], F32)
    cb_ag_in = p.dram("cb_ag_in", [NBS * DIN, 8], F32)
    cb_ag = p.dram("cb_ag", [NC * NBS * DIN, 8], F32)
    sm_ag_in = p.dram("sm_ag_in", [NBS * DIN, 24], F32)
    sm_ag = p.dram("sm_ag", [NC * NBS * DIN, 24], F32)

    xt = p.sb("xt", [128, 2, D], F32)
    sq = p.sb("sq", [128, 2, D], F32)
    stat = p.sb("stat", [128, 8], F32)
    ln16 = p.sb("ln16", [128, 2, D], BF16)
    woT = p.sb("woT", [128, NDT, D], BF16)
    wxT = p.sb("wxT", [128, NDT, 33], BF16)
    cw = p.sb("cw", [128, NDT, 4], F32)
    cb_c = p.sb("cb_c", [128, NDT], F32)
    wdt_c = p.sb("wdt_c", [128, NDT], F32)
    bdt_c = p.sb("bdt_c", [128, NDT], F32)
    dpar_c = p.sb("dpar_c", [128, NDT], F32)
    asb = p.sb("asb", [128, NDT, NSTATE], F32)
    acol = p.sb("acol", [128, NDT, NSTATE], F32)
    cbidx_sb = p.sb("cbidx_sb", [128, NBS * NDT], I32)
    smask_sb = p.sb("smask_sb", [128, SPB * 2], F32)
    cbnd = p.sb("cbnd", [128, NBS, NDT, 4], F32)
    lnTw = p.sb("lnTw", [128, 8, SL], BF16)
    ztile = p.sb("ztile", [128, 2, SL], BF16)
    onesc = p.sb("onesc", [1, 128], F32)
    onesw = p.sb("onesw", [128, SL], F32)
    selm = p.sb("selm", [33, 33 * 128], F32)
    xt2 = p.sb("xt2", [128, D], F32)
    idb = p.sb("idb", [128, 128], BF16)
    idf = p.sb("idf", [128, 128], F32)

    ps_t = p.ps("ps_t", [128, 512], BF16)
    ps_m = p.ps("ps_m", [128, 512], F32)
    ps_m2 = p.ps("ps_m2", [128, 512], F32)
    ps_o = p.ps("ps_o", [128, 4, 512], F32)
    ps_tf = p.ps("ps_tf", [128, 512], F32)

    s_ld = p.sem("s_ld")
    s_st = p.sem("s_st")
    s_g = p.sem("s_g")
    s_gd = p.sem("s_gd")
    s_v = p.sem("s_v")
    s_a = p.sem("s_a")
    s_t = p.sem("s_t")
    s_cc = p.sem("s_cc")
    s_out = p.sem("s_out")

    blk = p.block()

    tail = ExitStack()
    wload = p.sb_into(tail, "wload", [128, 2 * D], F32)
    wload16 = p.sb_into(tail, "wload16", [128, 2 * D], BF16)
    winT = p.sb_into(tail, "winT", [128, 8, 2 * DIN], BF16)

    emit_identities(blk, s_g, idb, idf)

    @blk.gpsimd
    def _(g):
        s_g.chain(g, g.memset(selm[:], 0.0))
        s_g.inc(g.affine_select(out=selm[:], in_=selm[:], compare_op=ALU.not_equal,
                                fill=1.0, base=0,
                                pattern=[[-1, 33], [0, 128]], channel_multiplier=1))

    @blk.sync
    def _(s):
        s_ld.dma(s.dma_start(out=cbidx_sb[:], in_=cbidx_d))
        s_ld.dma(s.dma_start(out=smask_sb[:], in_=smask_d))
        s_ld.dma(s.dma_start(out=cw[:], in_=conv_w.rearrange("(a p) c -> p a c", p=128)))
        s_ld.dma(s.dma_start(out=asb[:], in_=a_log.rearrange("(a p) c -> p a c", p=128)))
        for dt in range(NDT):
            load_col(s, s_ld, cb_c[:, dt:dt + 1], conv_b[0:1, 128 * dt:128 * (dt + 1)])
            load_col(s, s_ld, wdt_c[:, dt:dt + 1], w_dt[0:1, 128 * dt:128 * (dt + 1)])
            load_col(s, s_ld, bdt_c[:, dt:dt + 1], b_dt[0:1, 128 * dt:128 * (dt + 1)])
            load_col(s, s_ld, dpar_c[:, dt:dt + 1], d_par[0:1, 128 * dt:128 * (dt + 1)])


    @blk.scalar
    def _(sc):
        s_ld.wait(sc)
        s_a.inc(sc.activation(out=acol[:], in_=asb[:], func=AF.Exp))

    @blk.vector
    def _(v):
        s_a.wait(v)
        v.tensor_scalar_mul(acol[:], acol[:], -1.0)
        v.memset(onesc[:], 1.0)
        v.memset(onesw[:], 1.0)
        s_v.inc(v.memset(cbnd[:], 0.0))

    emit_weightT(p, blk, w_in, winT, 32, 8,
                 (wload, wload16, ps_t, idb), (s_ld, s_v, s_t, s_g))
    emit_weightT(p, blk, w_out, woT, 8, 16,
                 (wload, wload16, ps_t, idb), (s_ld, s_v, s_t, s_g))

    for half in range(2):
        @blk.sync
        def _(s, half=half):
            s_v.wait(s)
            s_t.wait(s)
            s_ld.dma(s.dma_start(out=wload[0:33, 0:D],
                                 in_=w_x[:, D * half:D * (half + 1)]))

        @blk.vector
        def _(v, half=half):
            s_ld.wait(v)
            s_v.inc(v.tensor_copy(wload16[0:33, 0:D], wload[0:33, 0:D]))

        for d8 in range(8):
            @blk.tensor
            def _(t, half=half, d8=d8):
                s_v.wait(t)
                s_g.wait(t)
                ins = t.transpose(ps_t[:, 0:33],
                                  wload16[0:33, 128 * d8:128 * (d8 + 1)],
                                  idb[0:33, 0:33])
                s_t.inc(ins)

            @blk.vector
            def _(v, half=half, d8=d8):
                s_t.wait(v)
                s_v.inc(v.tensor_copy(wxT[:, half * 8 + d8, :], ps_t[:, 0:33]))

    emit_ln_pass(p, blk, x_in, lnD, None, (xt, sq, stat, ln16),
                 (s_ld, s_v, s_a, s_st, s_gd), TPC // 128, stage_last=False)

    for bs in range(NBS):
        for ti in range(4):
            @blk.sync
            def _(s, bs=bs, ti=ti):
                if bs == 0 and ti == 0:
                    s_st.wait(s)
                s_v.wait(s)
                s_a.wait(s)
                s_ld.dma(s.dma_start(out=ln16[:, 0, :],
                                     in_=lnD[bs, 1 + ti * 128:1 + (ti + 1) * 128, :]))

            for jj in range(2):
                @blk.tensor
                def _(t, jj=jj):
                    s_ld.wait(t)
                    s_v.wait(t)
                    s_g.wait(t)
                    for j4 in range(4):
                        j = jj * 4 + j4
                        ins = t.transpose(ps_t[:, 128 * j4:128 * (j4 + 1)],
                                          ln16[:, 0, 128 * j:128 * (j + 1)], idb[:])
                    s_t.inc(ins)

                @blk.vector
                def _(v, ti=ti, jj=jj):
                    s_t.wait(v)
                    for j4 in range(4):
                        j = jj * 4 + j4
                        ins = v.tensor_copy(lnTw[:, j, 128 * ti:128 * (ti + 1)],
                                            ps_t[:, 128 * j4:128 * (j4 + 1)])
                    s_v.inc(ins)

        for i in range(32):
            pb = i % 2

            @blk.tensor
            def _(t, i=i, pb=pb):
                s_v.wait(t)
                s_a.wait(t)
                s_st.wait(t)
                ps = ps_m if pb == 0 else ps_m2
                for j in range(8):
                    ins = t.matmul(ps[:, :], winT[:, j, 128 * i:128 * (i + 1)],
                                   lnTw[:, j, :], start=(j == 0), stop=(j == 7))
                s_t.inc(ins)

            if i < 16:
                @blk.vector
                def _(v, i=i, pb=pb):
                    s_t.wait(v)
                    s_st.wait(v)
                    ps = ps_m if pb == 0 else ps_m2
                    s_v.inc(v.tensor_copy(ztile[:, pb, :], ps[:, :]))

                @blk.scalar
                def _(sc, bs=bs, i=i, pb=pb):
                    s_v.wait(sc)
                    s_st.dma(sc.dma_start(out=xzpreD[bs, 128 * i:128 * (i + 1), :],
                                          in_=ztile[:, pb, :]))
                    s_a.inc(sc.nop())
            else:
                @blk.scalar
                def _(sc, bs=bs, i=i, pb=pb):
                    s_t.wait(sc)
                    s_st.wait(sc)
                    ps = ps_m if pb == 0 else ps_m2
                    s_a.inc(sc.activation(out=ztile[:, pb, :], in_=ps[:, :], func=AF.Silu))

                @blk.scalar
                def _(sc, bs=bs, i=i, pb=pb):
                    s_a.wait(sc)
                    s_st.dma(sc.dma_start(out=zD[bs, 128 * (i - 16):128 * (i - 15), :],
                                          in_=ztile[:, pb, :]))
                    s_a.inc(sc.nop())

    tail.close()

    convbuf = p.sb("convbuf", [128, SL + 8], BF16)
    xcT = p.sb("xcT", [128, NDT, SL], BF16)
    xp = p.sb("xp", [33, SL], F32)
    Bb = p.sb("Bb", [128, NSTATE, SL], BF16)
    Cb = p.sb("Cb", [128, NSTATE, SL], BF16)
    deltaT = p.sb("deltaT", [128, SL], F32)
    dsc1 = p.sb("dsc1", [128, SL], F32)
    dsc2 = p.sb("dsc2", [128, SL], F32)
    dpre_b = p.sb("dpre_b", [128, SL], F32)
    dAt = p.sb("dAt", [128, SL], F32)
    m0t = p.sb("m0t", [128, SL], BF16)
    hseq = p.sb("hseq", [128, SL], BF16)
    z1t = p.sb("z1t", [128, SL], BF16)
    ytile = p.sb("ytile", [128, SL], F32)
    cstile = p.sb("cstile", [128, SL], F32)
    dxc = p.sb("dxc", [128, SL], BF16)
    hstage = p.sb("hstage", [128, NDT, 17], F32)
    smbuf = p.sb("smbuf", [128, NDT, 17], F32)
    hinit_all = p.sb("hinit_all", [128, NDT * NSTATE], F32)
    hexp = p.sb("hexp", [128, NDT * NSTATE], F32)
    hmine = p.sb("hmine", [128, NBS, NDT, NSTATE], F32)
    ept = p.sb("ept", [128, SL], F32)
    corr = p.sb("corr", [128, SL], F32)
    sz_t = p.sb("sz_t", [128, SL], BF16)
    xcr = p.sb("xcr", [128, SL], BF16)
    yf_t = p.sb("yf_t", [128, 2, SL], BF16)
    outT = p.sb("outT_l1", [128, 8, SL], BF16)

    @blk.gpsimd
    def _(g):
        s_st.wait(g)
        for bs in range(NBS):
            for dt in range(NDT):
                s_gd.dma(g.dma_start(
                    out=cb_ag_in[bs * DIN + dt * 128:bs * DIN + (dt + 1) * 128, 0:3],
                    in_=xzpreD[bs, dt * 128:(dt + 1) * 128, SL - 3:SL]))
        s_gd.wait(g)
        s_cc.inc(g.collective_compute("AllGather", ALU.bypass,
                                      replica_groups=[list(range(NC))],
                                      ins=[cb_ag_in], outs=[cb_ag]))
        s_cc.wait(g)
        s_v.wait(g)
        breg = g.to_reg(NC * NBS * DIN - 1)
        for bs in range(NBS):
            for dt in range(NDT):
                s_gd.dma(g.indirect_dma_start(
                    out=cbnd[:, bs, dt, 0:3], out_offset=None, in_=cb_ag,
                    in_offset=IndirectOffsetOnAxis(
                        cbidx_sb[:, bs * NDT + dt:bs * NDT + dt + 1], axis=0),
                    bounds_check=breg, oob_is_err=False))
        s_gd.wait(g)

    for bs in range(NBS):
        for dt in range(NDT):
            @blk.sync
            def _(s, bs=bs, dt=dt):
                s_gd.wait(s)
                s_a.wait(s)
                s_ld.dma(s.dma_start(out=convbuf[:, 3:3 + SL],
                                     in_=xzpreD[bs, dt * 128:(dt + 1) * 128, :]))

            @blk.vector
            def _(v, bs=bs, dt=dt):
                s_ld.wait(v)
                s_v.chain(v, v.tensor_copy(convbuf[:, 0:3], cbnd[:, bs, dt, 0:3]))
                s_v.chain(v, v.tensor_scalar(out=dsc1[:], in0=convbuf[:, 0:SL],
                                             scalar1=cw[:, dt, 0:1], scalar2=None,
                                             op0=ALU.mult))
                for j in range(1, 4):
                    s_v.chain(v, v.scalar_tensor_tensor(
                        out=dsc1[:], in0=convbuf[:, j:j + SL],
                        scalar=cw[:, dt, j:j + 1], in1=dsc1[:],
                        op0=ALU.mult, op1=ALU.add))
                s_v.inc(v.engine_nop() if False else v.tensor_copy(dsc2[:], dsc1[:]))

            @blk.scalar
            def _(sc, bs=bs, dt=dt):
                s_v.wait(sc)
                s_a.inc(sc.activation(out=xcT[:, dt, :], in_=dsc2[:], func=AF.Silu,
                                      bias=cb_c[:, dt:dt + 1]))

            @blk.scalar
            def _(sc, bs=bs, dt=dt):
                s_a.wait(sc)
                s_st.dma(sc.dma_start(out=xcD[bs, dt * 128:(dt + 1) * 128, :],
                                      in_=xcT[:, dt, :]))

        @blk.tensor
        def _(t, bs=bs):
            s_a.wait(t)
            s_v.wait(t)
            for dt in range(NDT):
                ins = t.matmul(ps_m[:, 0:33] if False else ps_m[0:33, :],
                               wxT[:, dt, :], xcT[:, dt, :],
                               start=(dt == 0), stop=(dt == NDT - 1))
            s_t.inc(ins)

        @blk.vector
        def _(v, bs=bs):
            s_t.wait(v)
            s_st.wait(v)
            s_v.inc(v.tensor_copy(xp[:, :], ps_m[0:33, :]))

        @blk.scalar
        def _(sc, bs=bs):
            s_v.wait(sc)
            s_st.dma(sc.dma_start(out=xpD[bs], in_=xp[:, :]))

        for n in range(NSTATE):
            @blk.tensor
            def _(t, bs=bs, n=n):
                s_v.wait(t)
                ins = t.matmul(ps_m[:, :], selm[:, 128 * n:128 * (n + 1)],
                               xp[:, :], start=True, stop=True)
                s_t.inc(ins)
                ins = t.matmul(ps_m2[:, :],
                               selm[:, 128 * (NSTATE + n):128 * (NSTATE + n + 1)],
                               xp[:, :], start=True, stop=True)
                s_t.inc(ins)

            @blk.vector
            def _(v, bs=bs, n=n):
                s_t.wait(v)
                v.tensor_copy(Bb[:, n, :], ps_m[:, :])
                s_v.inc(v.tensor_copy(Cb[:, n, :], ps_m2[:, :]))

        @blk.tensor
        def _(t, bs=bs):
            s_v.wait(t)
            ins = t.matmul(ps_m[:, :], selm[:, 128 * 32:128 * 33],
                           xp[:, :], start=True, stop=True)
            s_t.inc(ins)

        @blk.vector
        def _(v, bs=bs):
            s_t.wait(v)
            s_v.inc(v.tensor_copy(dpre_b[:], ps_m[:, :]))

        for dt in range(NDT):
            @blk.vector
            def _(v, bs=bs, dt=dt):
                if dt == 0:
                    s_v.wait(v)
                s_a.wait(v)
                s_st.wait(v)
                s_v.chain(v, v.tensor_scalar(out=deltaT[:], in0=dpre_b[:],
                                             scalar1=wdt_c[:, dt:dt + 1],
                                             scalar2=bdt_c[:, dt:dt + 1],
                                             op0=ALU.mult, op1=ALU.add))
                s_v.inc(v.tensor_scalar_max(dsc1[:], deltaT[:], 0.0))

            @blk.scalar
            def _(sc, bs=bs, dt=dt):
                s_v.wait(sc)
                s_a.chain(sc, sc.activation(out=dsc2[:], in_=deltaT[:], func=AF.Abs))
                s_a.chain(sc, sc.activation(out=dsc2[:], in_=dsc2[:], func=AF.Exp,
                                            scale=-1.0))
                s_a.inc(sc.activation(out=dsc2[:], in_=dsc2[:], func=AF.Ln, bias=1.0))

            @blk.vector
            def _(v, bs=bs, dt=dt):
                s_a.wait(v)
                s_v.chain(v, v.tensor_add(deltaT[:], dsc1[:], dsc2[:]))
                s_v.chain(v, v.tensor_tensor_scan(out=cstile[:], data0=onesw[:],
                                                  data1=deltaT[:], initial=0.0,
                                                  op0=ALU.mult, op1=ALU.add))
                s_v.chain(v, v.tensor_mul(dxc[:], deltaT[:], xcT[:, dt, :]))
                s_v.inc(v.tensor_copy(hstage[:, dt, 16:17], cstile[:, SL - 1:SL]))

            @blk.scalar
            def _(sc, bs=bs, dt=dt):
                s_v.wait(sc)
                s_st.dma(sc.dma_start(out=csD[bs, dt * 128:(dt + 1) * 128, :],
                                      in_=cstile[:]))

            for n in range(NSTATE):
                @blk.scalar
                def _(sc, bs=bs, dt=dt, n=n):
                    s_v.wait(sc)
                    s_a.inc(sc.activation(out=dAt[:], in_=deltaT[:], func=AF.Exp,
                                          scale=acol[:, dt, n:n + 1]))

                @blk.vector
                def _(v, bs=bs, dt=dt, n=n):
                    s_a.wait(v)
                    s_v.chain(v, v.tensor_mul(m0t[:], dxc[:], Bb[:, n, :]))
                    s_v.chain(v, v.tensor_tensor_scan(out=hseq[:], data0=dAt[:],
                                                      data1=m0t[:], initial=0.0,
                                                      op0=ALU.mult, op1=ALU.add))
                    s_v.chain(v, v.tensor_mul(z1t[:], Cb[:, n, :], hseq[:]))
                    if n == 0:
                        s_v.chain(v, v.tensor_copy(ytile[:], z1t[:]))
                    else:
                        s_v.chain(v, v.tensor_add(ytile[:], ytile[:], z1t[:]))
                    s_v.inc(v.tensor_copy(hstage[:, dt, n:n + 1], hseq[:, SL - 1:SL]))

            @blk.scalar
            def _(sc, bs=bs, dt=dt):
                s_v.wait(sc)
                s_st.dma(sc.dma_start(out=ypD[bs, dt * 128:(dt + 1) * 128, :],
                                      in_=ytile[:]))
                s_a.inc(sc.nop())

        @blk.gpsimd
        def _(g, bs=bs):
            s_v.wait(g)
            for dt in range(NDT):
                s_gd.dma(g.dma_start(
                    out=sm_ag_in[bs * DIN + dt * 128:bs * DIN + (dt + 1) * 128, 0:17],
                    in_=hstage[:, dt, :]))

    @blk.gpsimd
    def _(g):
        s_gd.wait(g)
        s_st.wait(g)
        s_cc.inc(g.collective_compute("AllGather", ALU.bypass,
                                      replica_groups=[list(range(NC))],
                                      ins=[sm_ag_in], outs=[sm_ag]))
        s_cc.wait(g)

    for b in range(B):
        @blk.vector
        def _(v, b=b):
            if b == 0:
                s_cc.wait(v)
                v.memset(hmine[:], 0.0)
            s_v.inc(v.memset(hinit_all[:], 0.0))

        for s_idx in range(SPB):
            r_s, slot_s = owner_of(s_idx)
            base = r_s * (NBS * DIN) + (b * 2 + slot_s) * DIN

            @blk.sync
            def _(s, base=base):
                s_v.wait(s)
                s_cc.wait(s)
                s_ld.dma(s.dma_start(
                    out=smbuf[:],
                    in_=sm_ag[base:base + DIN, 0:17].rearrange("(a p) c -> p a c", p=128)))

            @blk.vector
            def _(v, b=b, s_idx=s_idx):
                s_ld.wait(v)
                for slot in range(2):
                    s_v.chain(v, v.scalar_tensor_tensor(
                        out=hmine[:, b * 2 + slot, :, :].rearrange("p a c -> p (a c)"),
                        in0=hinit_all[:],
                        scalar=smask_sb[:, s_idx * 2 + slot:s_idx * 2 + slot + 1],
                        in1=hmine[:, b * 2 + slot, :, :].rearrange("p a c -> p (a c)"),
                        op0=ALU.mult, op1=ALU.add))
                dtot_rep = smbuf[:, :, 16:17].broadcast_to([128, NDT, NSTATE])
                s_v.inc(v.tensor_mul(hexp[:].rearrange("p (a c) -> p a c", a=NDT),
                                     acol[:, :, :], dtot_rep))

            @blk.scalar
            def _(sc):
                s_v.wait(sc)
                s_a.inc(sc.activation(out=hexp[:], in_=hexp[:], func=AF.Exp))

            @blk.vector
            def _(v):
                s_a.wait(v)
                s_v.chain(v, v.tensor_mul(hinit_all[:], hexp[:], hinit_all[:]))
                s_v.inc(v.tensor_add(
                    hinit_all[:].rearrange("p (a c) -> p a c", a=NDT),
                    hinit_all[:].rearrange("p (a c) -> p a c", a=NDT),
                    smbuf[:, :, 0:16]))


    for bs in range(NBS):
        b = bs // 2
        @blk.sync
        def _(s, bs=bs):
            s_t.wait(s)
            s_ld.dma(s.dma_start(out=xp[:, :], in_=xpD[bs]))

        for n in range(NSTATE):
            @blk.tensor
            def _(t, bs=bs, n=n):
                s_v.wait(t)
                if n == 0:
                    s_ld.wait(t)
                ins = t.matmul(ps_m[:, :],
                               selm[:, 128 * (NSTATE + n):128 * (NSTATE + n + 1)],
                               xp[:, :], start=True, stop=True)
                s_t.inc(ins)

            @blk.vector
            def _(v, n=n):
                s_t.wait(v)
                s_v.inc(v.tensor_copy(Cb[:, n, :], ps_m[:, :]))

        for dt in range(NDT):
            @blk.sync
            def _(s, bs=bs, dt=dt):
                s_v.wait(s)
                s_st.wait(s)
                s_ld.dma(s.dma_start(out=cstile[:], in_=csD[bs, dt * 128:(dt + 1) * 128, :]))
                s_ld.dma(s.dma_start(out=ytile[:], in_=ypD[bs, dt * 128:(dt + 1) * 128, :]))
                s_ld.dma(s.dma_start(out=xcr[:], in_=xcD[bs, dt * 128:(dt + 1) * 128, :]))
                s_ld.dma(s.dma_start(out=sz_t[:], in_=zD[bs, dt * 128:(dt + 1) * 128, :]))

            for n in range(NSTATE):
                @blk.scalar
                def _(sc, bs=bs, dt=dt, n=n):
                    if n == 0:
                        s_ld.wait(sc)
                    s_v.wait(sc)
                    s_a.inc(sc.activation(out=ept[:], in_=cstile[:], func=AF.Exp,
                                          scale=acol[:, dt, n:n + 1]))

                @blk.vector
                def _(v, bs=bs, dt=dt, n=n):
                    s_a.wait(v)
                    s_v.chain(v, v.scalar_tensor_tensor(
                        out=corr[:], in0=ept[:],
                        scalar=hmine[:, bs, dt, n:n + 1],
                        in1=Cb[:, n, :], op0=ALU.mult, op1=ALU.mult))
                    s_v.inc(v.tensor_add(ytile[:], ytile[:], corr[:]))

            @blk.vector
            def _(v, bs=bs, dt=dt):
                s_v.chain(v, v.scalar_tensor_tensor(
                    out=ytile[:], in0=xcr[:], scalar=dpar_c[:, dt:dt + 1],
                    in1=ytile[:], op0=ALU.mult, op1=ALU.add))
                s_v.inc(v.tensor_mul(yf_t[:, dt % 2, :], ytile[:], sz_t[:]))

            @blk.scalar
            def _(sc, bs=bs, dt=dt):
                s_v.wait(sc)
                s_st.dma(sc.dma_start(out=yfD[bs, dt * 128:(dt + 1) * 128, :],
                                      in_=yf_t[:, dt % 2, :]))
                s_a.inc(sc.nop())

    for bs in range(NBS):
        for rnd in range(2):
            for dt in range(NDT):
                @blk.sync
                def _(s, bs=bs, rnd=rnd, dt=dt):
                    s_t.wait(s)
                    if bs == 0 and rnd == 0 and dt == 0:
                        s_st.wait(s)
                    s_ld.dma(s.dma_start(out=yf_t[:, dt % 2, :],
                                         in_=yfD[bs, dt * 128:(dt + 1) * 128, :]))

                @blk.tensor
                def _(t, bs=bs, rnd=rnd, dt=dt):
                    s_ld.wait(t)
                    if dt == 0:
                        s_v.wait(t)
                    for dm4 in range(4):
                        dm = rnd * 4 + dm4
                        ins = t.matmul(ps_o[:, dm4, :], woT[:, dt, 128 * dm:128 * (dm + 1)],
                                       yf_t[:, dt % 2, :], start=(dt == 0),
                                       stop=(dt == NDT - 1))
                    s_t.inc(ins)

            @blk.vector
            def _(v, bs=bs, rnd=rnd):
                s_t.wait(v)
                for dm4 in range(4):
                    ins = v.tensor_copy(outT[:, rnd * 4 + dm4, :], ps_o[:, dm4, :])
                s_v.inc(ins)

        for ti in range(4):
            @blk.sync
            def _(s, bs=bs, ti=ti):
                s_out.wait(s)
                row0 = bs * SL + ti * 128
                s_ld.dma(s.dma_start(out=xt2[:], in_=x_in[row0:row0 + 128, :]))

            for half in range(2):
                @blk.tensor
                def _(t, ti=ti, half=half):
                    s_v.wait(t)
                    for i4 in range(4):
                        i = half * 4 + i4
                        ins = t.transpose(ps_tf[:, 128 * i4:128 * (i4 + 1)] if False
                                          else ps_t[:, 128 * i4:128 * (i4 + 1)],
                                          outT[:, i, 128 * ti:128 * (ti + 1)], idb[:])
                    s_t.inc(ins)

                @blk.vector
                def _(v, ti=ti, half=half):
                    s_t.wait(v)
                    s_ld.wait(v)
                    for i4 in range(4):
                        i = half * 4 + i4
                        ins = v.tensor_add(xt2[:, 128 * i:128 * (i + 1)],
                                           xt2[:, 128 * i:128 * (i + 1)],
                                           ps_t[:, 128 * i4:128 * (i4 + 1)])
                    s_v.inc(ins)

            @blk.scalar
            def _(sc, bs=bs, ti=ti):
                s_v.wait(sc)
                row0 = bs * SL + ti * 128
                s_out.dma(sc.dma_start(out=y_out[row0:row0 + 128, :], in_=xt2[:]))

    if ext:
        return

    @blk.gpsimd
    def _(g):
        s_out.wait(g)

    return p.finish()


_L1ATTN_NC = None


def run_l1attn(xshards, W_in, conv_w, conv_b, W_x, w_dt, b_dt, A_log, D_param, W_out):
    global _L1ATTN_NC
    if _L1ATTN_NC is None:
        _L1ATTN_NC = build_l1attn()
    in_maps = []
    for c in range(NC):
        cbidx, smask = l1attn_tables(c)
        in_maps.append(dict(
            x=xshards[c], w_in=W_in, conv_w=conv_w.reshape(DIN, 4),
            conv_b=conv_b.reshape(1, DIN), w_x=W_x, w_dt=w_dt.reshape(1, DIN),
            b_dt=b_dt.reshape(1, DIN), a_log=A_log, d_par=D_param.reshape(1, DIN),
            w_out=W_out, cbidx=cbidx, smask=smask))
    res = run_bass_kernel_spmd(_L1ATTN_NC, in_maps, core_ids=list(range(NC)))
    return [np.asarray(res.results[c]["y_ma"]) for c in range(NC)]



NE = 8
KSEL = 8192
FWIN = 256
NTOK = NC * TPC
FF = 4096


def l1ffn_tables(core):
    p = np.arange(128)
    myslot_g = (core * 128 + p).astype(np.int32).reshape(128, 1)
    mycomb = np.zeros((128, NE), dtype=np.int32)
    for e in range(NE):
        mycomb[:, e] = e * 1024 + core * 128 + p
    return myslot_g, mycomb


def build_l1ffn(debug=False, ext=None):
    p = ext["p"] if ext else Prog()
    nc = p.nc

    x_in = ext["x_in"] if ext else p.dram("x", [TPC, D], F32, kind="ExternalInput")
    wg = p.dram("wg", [D, NE], F32, kind="ExternalInput")
    wg_my = p.dram("wg_my", [D, 1], F32, kind="ExternalInput")
    w1 = p.dram("w1", [D, FF], F32, kind="ExternalInput")
    w2 = p.dram("w2", [FF, D], F32, kind="ExternalInput")
    b1 = p.dram("b1", [1, FF], F32, kind="ExternalInput")
    b2 = p.dram("b2", [1, D], F32, kind="ExternalInput")
    msg_idx = p.dram("msg_idx", [128, 1], I32, kind="ExternalInput")
    mc_idx = p.dram("mc_idx", [128, NE], I32, kind="ExternalInput")
    y_out = ext["y_out"] if ext else p.dram("y_mo", [TPC, D], F32, kind="ExternalOutput")

    lnD = p.dram("lnD", [NBS, 1 + SL, D], BF16)
    lnfD = p.dram("lnfD", [TPC, D], F32)
    ln_ag_in = p.dram("ln_ag_in", [TPC, D], BF16)
    agLN = p.dram("agLN", [NTOK, D], BF16)
    s_ag_in = p.dram("s_ag_in", [TPC, NE], F32)
    agS = p.dram("agS", [NTOK, NE], F32)
    slotD = p.dram("slotD", [NE, NTOK], I32)
    myslotD = p.dram("myslotD", [NTOK], I32)
    myslotC = p.dram("myslotC", [NE, TPC], I32)
    XeD = p.dram("XeD", [KSEL, D], BF16)
    y_ag_in = p.dram("y_ag_in", [KSEL, D], F32)
    agY = p.dram("agY", [NE * KSEL, D], F32)

    if debug:
        dbg_slot = p.dram("dbg_slot", [NE, NTOK], I32, kind="ExternalOutput")
        dbg_s = p.dram("dbg_s", [NTOK, NE], F32, kind="ExternalOutput")
        dbg_xe = p.dram("dbg_xe", [KSEL, D], BF16, kind="ExternalOutput")
        dbg_yag = p.dram("dbg_yag", [KSEL, D], F32, kind="ExternalOutput")
        dbg_msl = p.dram("dbg_msl", [NTOK], I32, kind="ExternalOutput")
        dbg_bis = p.dram("dbg_bis", [1, 8 * NE], F32, kind="ExternalOutput")
        dbg_hib = p.dram("dbg_hib", [128, NE], F32, kind="ExternalOutput")
        dbg_cntp = p.dram("dbg_cntp", [128, NE], F32, kind="ExternalOutput")
        dbg_selg = p.dram("dbg_selg", [128, 256], F32, kind="ExternalOutput")
        dbg_slotf = p.dram("dbg_slotf", [128, 256], F32, kind="ExternalOutput")
        dbg_pcs = p.dram("dbg_pcs", [128, 256], F32, kind="ExternalOutput")
        dbg_sloti = p.dram("dbg_sloti", [128, 256], I32, kind="ExternalOutput")
    xt = p.sb("xt", [128, 2, D], F32)
    sq = p.sb("sq", [128, 2, D], F32)
    stat = p.sb("stat", [128, 8], F32)
    ln16 = p.sb("ln16", [128, 2, D], BF16)
    wgsb = p.sb("wgsb", [128, 8, NE], F32)
    wgmy = p.sb("wgmy", [128, 8, 1], BF16)
    b1c = p.sb("b1c", [128, 32], F32)
    b2c = p.sb("b2c", [128, 8], F32)
    msg_sb = p.sb("msg_sb", [128, 1], I32)
    mc_sb = p.sb("mc_sb", [128, NE], I32)
    bis = p.sb("bis", [1, 8 * NE], F32)
    bisd = p.sb("bisd", [1, NE], I32)
    cntp = p.sb("cntp", [128, NE], F32)
    midb = p.sb("midb", [128, NE], F32)
    lob = p.sb("lob", [128, NE], F32)
    hib = p.sb("hib", [128, NE], F32)
    ones1 = p.sb("ones1", [128, 1], F32)
    onesc = p.sb("onesc", [1, 128], F32)
    lstrict = p.sb("lstrict", [128, 128], F32)
    tots = p.sb("tots", [128, 1], F32)
    excl = p.sb("excl", [128, 1], F32)
    rneed_b = p.sb("rneed_b", [128, 1], F32)
    idxcol = p.sb("idxcol", [128, 2, 1], I32)
    xbl = p.sb("xbl", [128, 2, D], BF16)
    acc = p.sb("acc", [128, 2, D], F32)
    xt2 = p.sb("xt2", [128, 2, D], F32)
    idb = p.sb("idb", [128, 128], BF16)
    idf = p.sb("idf", [128, 128], F32)

    ga = ExitStack()

    def gA(name, shape, dtype):
        return p.sb_into(ga, name, shape, dtype)

    lnf = gA("lnf", [128, 2, D], F32)
    lnTwf = gA("lnTwf", [128, 8, SL], F32)
    stile = gA("stile", [8, 2, SL], F32)
    stok = gA("stok", [128, 2, 8], F32)
    ssb = gA("ssb", [128, 256, NE], F32)
    junk = gA("junk", [128, 256], F32)
    onesr = gA("onesr", [128, 256], F32)
    huget = gA("huget", [128, 256], F32)
    selg = gA("selg", [128, 256], F32)
    tiem = gA("tiem", [128, 256], F32)
    pcs = gA("pcs", [128, 256], F32)
    sranks = gA("sranks", [128, 256], F32)
    slotf = gA("slotf", [128, 256], F32)
    sloti = gA("sloti", [128, 256], I32)
    seli = gA("seli", [128, 256], I32)
    msg_cols = gA("msg_cols", [128, 256], I32)

    ps_t = p.ps("ps_t", [128, 512], BF16)
    ps_tf = p.ps("ps_tf", [128, 512], F32)
    ps_m = p.ps("ps_m", [128, 512], F32)
    ps_m2 = p.ps("ps_m2", [128, 512], F32)

    s_ld = p.sem("s_ld")
    s_st = p.sem("s_st")
    s_g = p.sem("s_g")
    s_gd = p.sem("s_gd")
    s_v = p.sem("s_v")
    s_a = p.sem("s_a")
    s_t = p.sem("s_t")
    s_cc = p.sem("s_cc")
    s_out = p.sem("s_out")

    blk = p.block()
    emit_identities(blk, s_g, idb, idf)

    @blk.gpsimd
    def _(g):
        s_g.chain(g, g.memset(lstrict[:], 0.0))
        s_g.inc(g.affine_select(out=lstrict[:], in_=lstrict[:],
                                compare_op=ALU.is_ge, fill=1.0, base=0,
                                pattern=[[-1, 128]], channel_multiplier=1))
        for j in range(8):
            s_gd.dma(g.dma_start(out=wgmy[:, j, :], in_=wg_my[128 * j:128 * (j + 1), :]))

    @blk.sync
    def _(s):
        s_ld.dma(s.dma_start(out=msg_sb[:], in_=msg_idx))
        s_ld.dma(s.dma_start(out=mc_sb[:], in_=mc_idx))
        for j in range(8):
            s_ld.dma(s.dma_start(out=wgsb[:, j, :], in_=wg[128 * j:128 * (j + 1), :]))
        for j in range(32):
            load_col(s, s_ld, b1c[:, j:j + 1], b1[0:1, 128 * j:128 * (j + 1)])
        for j in range(8):
            load_col(s, s_ld, b2c[:, j:j + 1], b2[0:1, 128 * j:128 * (j + 1)])

    @blk.vector
    def _(v):
        v.memset(ones1[:], 1.0)
        v.memset(onesc[:], 1.0)
        v.memset(onesr[:], 1.0)
        s_v.inc(v.memset(huget[:], float(HUGE)))

    for k in range(TPC // 128):
        bs, ti = divmod(k, 4)
        pb = k % 2

        @blk.sync
        def _(s, k=k, pb=pb):
            s_v.wait(s)
            s_ld.dma(s.dma_start(out=xt[:, pb, :], in_=x_in[k * 128:(k + 1) * 128, :]))

        @blk.vector
        def _(v, pb=pb):
            s_ld.wait(v)
            s_a.wait(v)
            s_v.chain(v, v.tensor_reduce(out=stat[:, 0:1], in_=xt[:, pb, :],
                                         axis=AX.X, op=ALU.add))
            s_v.chain(v, v.tensor_scalar_mul(stat[:, 2:3], stat[:, 0:1], 1.0 / D))
            s_v.inc(v.tensor_scalar(out=sq[:, pb, :], in0=xt[:, pb, :],
                                    scalar1=stat[:, 2:3], scalar2=None,
                                    op0=ALU.subtract))

        @blk.scalar
        def _(sc, pb=pb):
            s_v.wait(sc)
            s_a.inc(sc.activation(out=sq[:, pb, :], in_=sq[:, pb, :], func=AF.Square,
                                  accum_out=stat[:, 1:2]))

        @blk.vector
        def _(v):
            s_a.wait(v)
            s_v.inc(v.tensor_scalar(out=stat[:, 3:4], in0=stat[:, 1:2],
                                    scalar1=1.0 / D, scalar2=EPS,
                                    op0=ALU.mult, op1=ALU.add))

        @blk.scalar
        def _(sc):
            s_v.wait(sc)
            s_a.inc(sc.activation(out=stat[:, 4:5], in_=stat[:, 3:4], func=AF.Sqrt))

        @blk.vector
        def _(v, pb=pb):
            s_a.wait(v)
            s_st.wait(v)
            s_v.chain(v, v.reciprocal(stat[:, 5:6], stat[:, 4:5]))
            s_v.chain(v, v.tensor_scalar(out=sq[:, pb, :], in0=xt[:, pb, :],
                                         scalar1=stat[:, 2:3], scalar2=None,
                                         op0=ALU.subtract))
            s_v.chain(v, v.tensor_scalar(out=lnf[:, pb, :], in0=sq[:, pb, :],
                                         scalar1=stat[:, 5:6], scalar2=None,
                                         op0=ALU.mult))
            s_v.inc(v.tensor_copy(ln16[:, pb, :], lnf[:, pb, :]))

        @blk.scalar
        def _(sc, k=k, pb=pb):
            s_v.wait(sc)
            s_st.dma(sc.dma_start(out=ln_ag_in[k * 128:(k + 1) * 128, :],
                                  in_=ln16[:, pb, :]))
            s_st.dma(sc.dma_start(out=lnfD[k * 128:(k + 1) * 128, :],
                                  in_=lnf[:, pb, :]))

    for bs in range(NBS):
        for ti in range(4):
            @blk.sync
            def _(s, bs=bs, ti=ti):
                s_v.wait(s)
                s_a.wait(s)
                if bs == 0 and ti == 0:
                    s_st.wait(s)
                row0 = bs * SL + ti * 128
                s_ld.dma(s.dma_start(out=lnf[:, 0, :], in_=lnfD[row0:row0 + 128, :]))

            for jj in range(2):
                @blk.tensor
                def _(t, jj=jj):
                    s_ld.wait(t)
                    s_v.wait(t)
                    s_g.wait(t)
                    for j4 in range(4):
                        j = jj * 4 + j4
                        ins = t.transpose(ps_tf[:, 128 * j4:128 * (j4 + 1)],
                                          lnf[:, 0, 128 * j:128 * (j + 1)], idf[:])
                    s_t.inc(ins)

                @blk.vector
                def _(v, ti=ti, jj=jj):
                    s_t.wait(v)
                    for j4 in range(4):
                        j = jj * 4 + j4
                        ins = v.tensor_copy(lnTwf[:, j, 128 * ti:128 * (ti + 1)],
                                            ps_tf[:, 128 * j4:128 * (j4 + 1)])
                    s_v.inc(ins)

        @blk.tensor
        def _(t, bs=bs):
            s_v.wait(t)
            for j in range(8):
                ins = t.matmul(ps_m[0:8, :], wgsb[:, j, :], lnTwf[:, j, :],
                               start=(j == 0), stop=(j == 7))
            s_t.inc(ins)

        @blk.vector
        def _(v, bs=bs):
            s_t.wait(v)
            s_a.wait(v)
            s_v.inc(v.tensor_copy(stile[:, bs % 2, :], ps_m[0:8, :]))

        for ti in range(4):
            @blk.tensor
            def _(t, bs=bs, ti=ti):
                s_v.wait(t)
                s_a.wait(t)
                ins = t.transpose(ps_tf[:, 0:8],
                                  stile[0:8, bs % 2, 128 * ti:128 * (ti + 1)],
                                  idf[0:8, 0:8])
                s_t.inc(ins)

            @blk.vector
            def _(v, bs=bs, ti=ti):
                s_t.wait(v)
                s_st.wait(v)
                s_v.inc(v.tensor_copy(stok[:, ti % 2, :], ps_tf[:, 0:8]))

            @blk.scalar
            def _(sc, bs=bs, ti=ti):
                s_v.wait(sc)
                row0 = bs * SL + ti * 128
                s_st.dma(sc.dma_start(out=s_ag_in[row0:row0 + 128, :],
                                      in_=stok[:, ti % 2, :]))

    @blk.gpsimd
    def _(g):
        s_st.wait(g)
        s_cc.inc(g.collective_compute("AllGather", ALU.bypass,
                                      replica_groups=[list(range(NC))],
                                      ins=[ln_ag_in], outs=[agLN]))
        s_cc.inc(g.collective_compute("AllGather", ALU.bypass,
                                      replica_groups=[list(range(NC))],
                                      ins=[s_ag_in], outs=[agS]))
        s_cc.wait(g)

    @blk.sync
    def _(s):
        s_cc.wait(s)
        s_ld.dma(s.dma_start(out=ssb[:],
                             in_=agS.rearrange("(p r) e -> p r e", p=128)))

    BIS_ITERS = 34
    def bcol(name, e):
        i = {"lo": 0, "hi": 1, "mid": 2, "cnt": 3, "dir": 4, "rneed": 5,
             "c1": 6}[name]
        return bis[0:1, i * NE + e:i * NE + e + 1]

    @blk.vector
    def _(v):
        s_ld.wait(v)
        v.memset(bis[0:1, 0 * NE:1 * NE], -100.0)
        s_v.inc(v.memset(bis[0:1, 1 * NE:2 * NE], 100.0))

    for it in range(BIS_ITERS):
        @blk.vector
        def _(v, it=it):
            if it > 0:
                s_t.wait(v)
                s_v.chain(v, v.tensor_copy(bis[0:1, 3 * NE:4 * NE], ps_m[0:1, 0:NE]))
                s_v.chain(v, v.tensor_scalar(out=bisd[0:1, :],
                                             in0=bis[0:1, 3 * NE:4 * NE],
                                             scalar1=float(KSEL), scalar2=None,
                                             op0=ALU.is_ge))
                s_v.chain(v, v.select(bis[0:1, 0 * NE:1 * NE], bisd[0:1, :],
                                      bis[0:1, 2 * NE:3 * NE], bis[0:1, 0 * NE:1 * NE]))
                s_v.chain(v, v.select(bis[0:1, 1 * NE:2 * NE], bisd[0:1, :],
                                      bis[0:1, 1 * NE:2 * NE], bis[0:1, 2 * NE:3 * NE]))
            s_v.chain(v, v.tensor_add(bis[0:1, 2 * NE:3 * NE], bis[0:1, 0 * NE:1 * NE],
                                      bis[0:1, 1 * NE:2 * NE]))
            s_v.inc(v.tensor_scalar_mul(bis[0:1, 2 * NE:3 * NE],
                                        bis[0:1, 2 * NE:3 * NE], 0.5))

        @blk.tensor
        def _(t, it=it):
            s_v.wait(t)
            ins = t.matmul(ps_m2[:, 0:NE], onesc[:, :], bis[0:1, 2 * NE:3 * NE],
                           start=True, stop=True)
            s_t.inc(ins)

        @blk.vector
        def _(v, it=it):
            s_t.wait(v)
            s_v.chain(v, v.tensor_copy(midb[:], ps_m2[:, 0:NE]))
            for e in range(NE):
                s_v.chain(v, v.tensor_scalar(
                    out=junk[:], in0=ssb[:, :, e], scalar1=midb[:, e:e + 1],
                    scalar2=0.0, op0=ALU.is_gt, op1=ALU.add,
                    accum_out=cntp[:, e:e + 1]))
            s_v.inc(v.engine_nop())

        @blk.tensor
        def _(t, it=it):
            s_v.wait(t)
            ins = t.matmul(ps_m[0:1, 0:NE], ones1[:, 0:1],
                           cntp[:, :], start=True, stop=True)
            s_t.inc(ins)

    @blk.vector
    def _(v):
        s_t.wait(v)
        s_v.chain(v, v.tensor_copy(bis[0:1, 3 * NE:4 * NE], ps_m[0:1, 0:NE]))
        s_v.chain(v, v.tensor_scalar(out=bisd[0:1, :],
                                     in0=bis[0:1, 3 * NE:4 * NE],
                                     scalar1=float(KSEL), scalar2=None,
                                     op0=ALU.is_ge))
        s_v.chain(v, v.select(bis[0:1, 0 * NE:1 * NE], bisd[0:1, :],
                              bis[0:1, 2 * NE:3 * NE], bis[0:1, 0 * NE:1 * NE]))
        s_v.inc(v.select(bis[0:1, 1 * NE:2 * NE], bisd[0:1, :],
                         bis[0:1, 1 * NE:2 * NE], bis[0:1, 2 * NE:3 * NE]))

    @blk.tensor
    def _(t):
        s_v.wait(t)
        ins = t.matmul(ps_m2[:, 0:NE], onesc[:, :], bis[0:1, 0 * NE:1 * NE],
                       start=True, stop=True)
        s_t.inc(ins)
        ins = t.matmul(ps_tf[:, 0:NE], onesc[:, :], bis[0:1, 1 * NE:2 * NE],
                       start=True, stop=True)
        s_t.inc(ins)

    @blk.vector
    def _(v):
        s_t.wait(v)
        v.tensor_copy(lob[:], ps_m2[:, 0:NE])
        s_v.chain(v, v.tensor_copy(hib[:], ps_tf[:, 0:NE]))
        for e in range(NE):
            s_v.chain(v, v.tensor_scalar(
                out=junk[:], in0=ssb[:, :, e], scalar1=hib[:, e:e + 1],
                scalar2=0.0, op0=ALU.is_gt, op1=ALU.add,
                accum_out=cntp[:, e:e + 1]))
        s_v.inc(v.engine_nop())

    @blk.tensor
    def _(t):
        s_v.wait(t)
        ins = t.matmul(ps_m[0:1, 0:NE], ones1[:, 0:1], cntp[:, :],
                       start=True, stop=True)
        s_t.inc(ins)

    @blk.vector
    def _(v):
        s_t.wait(v)
        s_v.chain(v, v.tensor_copy(bis[0:1, 6 * NE:7 * NE], ps_m[0:1, 0:NE]))
        s_v.chain(v, v.tensor_scalar(out=bis[0:1, 5 * NE:6 * NE],
                                     in0=bis[0:1, 6 * NE:7 * NE],
                                     scalar1=-1.0, scalar2=float(KSEL),
                                     op0=ALU.mult, op1=ALU.add))
        s_v.inc(v.engine_nop())

    for e in range(NE):
        @blk.tensor
        def _(t, e=e):
            s_v.wait(t)
            ins = t.matmul(ps_m2[:, 0:1], onesc[:, :], bcol("rneed", e),
                           start=True, stop=True)
            s_t.inc(ins)

        @blk.vector
        def _(v, e=e):
            s_t.wait(v)
            s_v.chain(v, v.tensor_copy(rneed_b[:], ps_m2[:, 0:1]))
            s_v.chain(v, v.tensor_scalar(out=selg[:], in0=ssb[:, :, e],
                                         scalar1=hib[:, e:e + 1], scalar2=None,
                                         op0=ALU.is_gt))
            s_v.chain(v, v.tensor_scalar(out=tiem[:], in0=ssb[:, :, e],
                                         scalar1=lob[:, e:e + 1], scalar2=None,
                                         op0=ALU.is_gt))
            s_v.chain(v, v.tensor_sub(tiem[:], tiem[:], selg[:]))
            s_v.chain(v, v.tensor_tensor_scan(out=pcs[:], data0=onesr[:],
                                              data1=tiem[:], initial=0.0,
                                              op0=ALU.mult, op1=ALU.add))
            s_v.inc(v.tensor_copy(tots[:], pcs[:, 255:256]))

        @blk.tensor
        def _(t, e=e):
            s_v.wait(t)
            ins = t.matmul(ps_m2[:, 0:1], lstrict[:, :], tots[:, 0:1],
                           start=True, stop=True)
            s_t.inc(ins)

        @blk.vector
        def _(v, e=e):
            s_t.wait(v)
            s_v.chain(v, v.tensor_copy(excl[:], ps_m2[:, 0:1]))
            s_v.chain(v, v.tensor_scalar(out=sranks[:], in0=pcs[:],
                                         scalar1=excl[:, 0:1], scalar2=-1.0,
                                         op0=ALU.add, op1=ALU.add))
            s_v.chain(v, v.tensor_scalar(out=sranks[:], in0=sranks[:],
                                         scalar1=rneed_b[:, 0:1], scalar2=None,
                                         op0=ALU.is_lt))
            s_v.chain(v, v.tensor_mul(sranks[:], sranks[:], tiem[:]))
            s_v.chain(v, v.tensor_add(selg[:], selg[:], sranks[:]))
            s_v.chain(v, v.tensor_tensor_scan(out=pcs[:], data0=onesr[:],
                                              data1=selg[:], initial=0.0,
                                              op0=ALU.mult, op1=ALU.add))
            s_v.inc(v.tensor_copy(tots[:], pcs[:, 255:256]))

        @blk.tensor
        def _(t, e=e):
            s_v.wait(t)
            ins = t.matmul(ps_m2[:, 0:1], lstrict[:, :], tots[:, 0:1],
                           start=True, stop=True)
            s_t.inc(ins)

        @blk.vector
        def _(v, e=e):
            s_t.wait(v)
            s_v.chain(v, v.tensor_copy(excl[:], ps_m2[:, 0:1]))
            s_v.chain(v, v.tensor_scalar(out=slotf[:], in0=pcs[:],
                                         scalar1=excl[:, 0:1], scalar2=-1.0,
                                         op0=ALU.add, op1=ALU.add))
            s_v.chain(v, v.tensor_scalar(out=sranks[:], in0=slotf[:],
                                         scalar1=-float(HUGE), scalar2=None,
                                         op0=ALU.add))
            s_v.chain(v, v.tensor_mul(sranks[:], sranks[:], selg[:]))
            s_v.chain(v, v.tensor_scalar(out=slotf[:], in0=sranks[:],
                                         scalar1=float(HUGE), scalar2=None,
                                         op0=ALU.add))
            s_v.inc(v.tensor_copy(sloti[:], slotf[:]))

        @blk.scalar
        def _(sc, e=e):
            s_v.wait(sc)
            s_st.dma(sc.dma_start(out=slotD[e].rearrange("(p r) -> p r", p=128),
                                  in_=sloti[:]))

    @blk.gpsimd
    def _(g):
        s_st.wait(g)
        s_ld.wait(g)
        breg = g.to_reg(NE * 128 - 1)
        bregs2 = g.to_reg(NE * 1024 - 1)
        s_gd.dma(g.indirect_dma_start(
            out=msg_cols[:], out_offset=None,
            in_=slotD.rearrange("a b -> (a b)").unsqueeze(1) if False
            else slotD.rearrange("e (q c) -> (e q) c", c=256),
            in_offset=IndirectOffsetOnAxis(msg_sb[:, 0:1], axis=0),
            bounds_check=breg, oob_is_err=False))
        s_gd.wait(g)
        s_gd.dma(g.dma_start(out=myslotD.rearrange("(p r) -> p r", p=128),
                             in_=msg_cols[:]))
        for e in range(NE):
            s_gd.dma(g.indirect_dma_start(
                out=msg_cols[:, 0:32], out_offset=None,
                in_=slotD.rearrange("e (q c) -> (e q) c", c=32),
                in_offset=IndirectOffsetOnAxis(mc_sb[:, e:e + 1], axis=0),
                bounds_check=bregs2, oob_is_err=False))
            s_gd.wait(g)
            s_gd.dma(g.dma_start(out=myslotC[e].rearrange("(p r) -> p r", p=128),
                                 in_=msg_cols[:, 0:32]))
            s_gd.wait(g)

    @blk.gpsimd
    def _(g):
        breg3 = g.to_reg(KSEL - 1)
        loads = []
        for pb2 in range(NTOK // 128):
            if pb2 < 2:
                s_gd.dma(g.dma_start(
                    out=idxcol[:, pb2 % 2, :],
                    in_=myslotD[pb2 * 128:(pb2 + 1) * 128].unsqueeze(1)))
                loads.append(s_gd.n)
        for pb2 in range(NTOK // 128):
            s_gd.dma(g.dma_start(out=xbl[:, pb2 % 2, :],
                                 in_=agLN[pb2 * 128:(pb2 + 1) * 128, :]))
            g.wait_ge(s_gd.h, s_gd.n)
            s_gd.dma(g.indirect_dma_start(
                out=XeD, out_offset=IndirectOffsetOnAxis(idxcol[:, pb2 % 2, :], axis=0),
                in_=xbl[:, pb2 % 2, :], in_offset=None,
                bounds_check=breg3, oob_is_err=False))
            nxt = pb2 + 2
            if nxt < NTOK // 128:
                s_gd.dma(g.dma_start(
                    out=idxcol[:, nxt % 2, :],
                    in_=myslotD[nxt * 128:(nxt + 1) * 128].unsqueeze(1)))
                loads.append(s_gd.n)
        s_gd.wait(g)

    if debug:
        @blk.gpsimd
        def _(g):
            s_out.wait(g) if False else None
            s_gd.dma(g.dma_start(out=dbg_slot, in_=slotD))
            s_gd.dma(g.dma_start(out=dbg_s, in_=agS))
            s_gd.dma(g.dma_start(out=dbg_xe, in_=XeD))
            s_gd.dma(g.dma_start(out=dbg_msl, in_=myslotD))
            s_gd.dma(g.dma_start(out=dbg_bis, in_=bis[:]))
            s_gd.dma(g.dma_start(out=dbg_hib, in_=hib[:]))
            s_gd.dma(g.dma_start(out=dbg_cntp, in_=cntp[:]))
            s_gd.dma(g.dma_start(out=dbg_selg, in_=selg[:]))
            s_gd.dma(g.dma_start(out=dbg_slotf, in_=slotf[:]))
            s_gd.dma(g.dma_start(out=dbg_pcs, in_=pcs[:]))
            s_gd.dma(g.dma_start(out=dbg_sloti, in_=sloti[:]))
            s_gd.wait(g)
        ga.close()
        return p.finish()

    ga.close()

    w1sb = p.sb("w1sb", [128, 8, FF], BF16)
    w2sb = p.sb("w2sb", [128, 32, D], BF16)
    xeT = p.sb("xeT", [128, 8, FWIN], BF16)
    xet_tok = p.sb("xet_tok", [128, 2, D], BF16)
    hidT = p.sb("hidT", [128, 32, FWIN], BF16)
    grow = p.sb("grow", [1, FWIN], F32)
    gbs = p.sb("gbs", [128, FWIN], F32)
    otmp = p.sb("otmp", [128, FWIN], F32)
    oT = p.sb("oT", [128, 8, FWIN], BF16)
    otok = p.sb("otok", [128, 2, D], BF16)

    @blk.gpsimd
    def _(g):
        for j in range(8):
            s_gd.dma(g.dma_start(out=w1sb[:, j, :], in_=w1[128 * j:128 * (j + 1), :]))
        for j in range(32):
            s_gd.dma(g.dma_start(out=w2sb[:, j, :], in_=w2[128 * j:128 * (j + 1), :]))

    for w in range(KSEL // FWIN):
        for ti in range(2):
            @blk.sync
            def _(s, w=w, ti=ti):
                s_v.wait(s)
                s_a.wait(s)
                if w == 0 and ti == 0:
                    s_gd.wait(s)
                row0 = w * FWIN + ti * 128
                s_ld.dma(s.dma_start(out=xet_tok[:, ti % 2, :],
                                     in_=XeD[row0:row0 + 128, :]))

            for jj in range(2):
                @blk.tensor
                def _(t, ti=ti, jj=jj):
                    s_ld.wait(t)
                    s_v.wait(t)
                    s_g.wait(t)
                    for j4 in range(4):
                        j = jj * 4 + j4
                        ins = t.transpose(ps_t[:, 128 * j4:128 * (j4 + 1)],
                                          xet_tok[:, ti % 2, 128 * j:128 * (j + 1)],
                                          idb[:])
                    s_t.inc(ins)

                @blk.vector
                def _(v, ti=ti, jj=jj):
                    s_t.wait(v)
                    for j4 in range(4):
                        j = jj * 4 + j4
                        ins = v.tensor_copy(xeT[:, j, 128 * ti:128 * (ti + 1)],
                                            ps_t[:, 128 * j4:128 * (j4 + 1)])
                    s_v.inc(ins)

        @blk.tensor
        def _(t, w=w):
            s_v.wait(t)
            s_gd.wait(t)
            for j in range(8):
                ins = t.matmul(ps_m2[0:1, 0:FWIN], wgmy[:, j, :], xeT[:, j, :],
                               start=(j == 0), stop=(j == 7))
            s_t.inc(ins)

        @blk.scalar
        def _(sc, w=w):
            s_t.wait(sc)
            s_a.inc(sc.activation(out=grow[:], in_=ps_m2[0:1, 0:FWIN], func=AF.Sigmoid))

        for fb in range(32):
            @blk.tensor
            def _(t, w=w, fb=fb):
                s_a.wait(t)
                for j in range(8):
                    ins = t.matmul(ps_m[:, 0:FWIN], w1sb[:, j, 128 * fb:128 * (fb + 1)],
                                   xeT[:, j, :], start=(j == 0), stop=(j == 7))
                s_t.inc(ins)

            @blk.scalar
            def _(sc, w=w, fb=fb):
                s_t.wait(sc)
                s_a.inc(sc.activation(out=hidT[:, fb, :], in_=ps_m[:, 0:FWIN],
                                      func=AF.Gelu, bias=b1c[:, fb:fb + 1]))

        @blk.tensor
        def _(t, w=w):
            s_a.wait(t)
            ins = t.matmul(ps_m2[:, 0:FWIN], onesc[:, :], grow[:, :],
                           start=True, stop=True)
            s_t.inc(ins)

        @blk.vector
        def _(v, w=w):
            s_t.wait(v)
            s_v.inc(v.tensor_copy(gbs[:], ps_m2[:, 0:FWIN]))

        for hb in range(8):
            @blk.tensor
            def _(t, w=w, hb=hb):
                s_v.wait(t)
                for j in range(32):
                    ins = t.matmul(ps_m[:, 0:FWIN], w2sb[:, j, 128 * hb:128 * (hb + 1)],
                                   hidT[:, j, :], start=(j == 0), stop=(j == 31))
                s_t.inc(ins)

            @blk.vector
            def _(v, w=w, hb=hb):
                s_t.wait(v)
                s_v.chain(v, v.tensor_scalar(out=otmp[:], in0=ps_m[:, 0:FWIN],
                                             scalar1=b2c[:, hb:hb + 1], scalar2=None,
                                             op0=ALU.add))
                s_v.inc(v.tensor_mul(oT[:, hb, :], otmp[:], gbs[:]))

        for ti in range(2):
            for half in range(2):
                @blk.tensor
                def _(t, w=w, ti=ti, half=half):
                    s_v.wait(t)
                    s_gd.wait(t)
                    for i4 in range(4):
                        i = half * 4 + i4
                        ins = t.transpose(ps_t[:, 128 * i4:128 * (i4 + 1)],
                                          oT[:, i, 128 * ti:128 * (ti + 1)], idb[:])
                    s_t.inc(ins)

                @blk.vector
                def _(v, ti=ti, half=half):
                    s_t.wait(v)
                    for i4 in range(4):
                        i = half * 4 + i4
                        ins = v.tensor_copy(otok[:, ti % 2, 128 * i:128 * (i + 1)],
                                            ps_t[:, 128 * i4:128 * (i4 + 1)])
                    s_v.inc(ins)

            @blk.gpsimd
            def _(g, w=w, ti=ti):
                s_v.wait(g)
                row0 = w * FWIN + ti * 128
                s_gd.dma(g.dma_start(out=y_ag_in[row0:row0 + 128, :],
                                     in_=otok[:, ti % 2, :]))

    @blk.gpsimd
    def _(g):
        s_st.wait(g)
        s_cc.inc(g.collective_compute("AllGather", ALU.bypass,
                                      replica_groups=[list(range(NC))],
                                      ins=[y_ag_in], outs=[agY]))
        s_cc.wait(g)

    bregs_c = {}

    @blk.gpsimd
    def _(g):
        s_cc.wait(g)
        bregs_c["y"] = g.to_reg(NE * KSEL - 1)

    for k in range(TPC // 128):
        pb = k % 2

        @blk.gpsimd
        def _(g, k=k, pb=pb):
            s_v.wait(g)
            s_g.chain(g, g.memset(acc[:, pb, :], 0.0))
            for e in range(NE):
                s_gd.dma(g.dma_start(out=idxcol[:, pb, :],
                                     in_=myslotC[e, k * 128:(k + 1) * 128].unsqueeze(1)))
                s_gd.wait(g)
                s_g.chain(g, g.tensor_single_scalar(out=idxcol[:, pb, :],
                                                    in_=idxcol[:, pb, :],
                                                    scalar=e * KSEL, op=ALU.add))
                s_gd.dma(g.indirect_dma_start(
                    out=acc[:, pb, :], out_offset=None, in_=agY,
                    in_offset=IndirectOffsetOnAxis(idxcol[:, pb, :], axis=0),
                    bounds_check=bregs_c["y"], oob_is_err=False,
                    compute_op=ALU.add))
                s_gd.wait(g)
            s_g.inc(g.nop())

        @blk.sync
        def _(s, k=k, pb=pb):
            s_out.wait(s)
            s_ld.dma(s.dma_start(out=xt2[:, pb, :], in_=x_in[k * 128:(k + 1) * 128, :]))

        @blk.vector
        def _(v, k=k, pb=pb):
            s_ld.wait(v)
            s_g.wait(v)
            s_gd.wait(v)
            ins = v.scalar_tensor_tensor(out=xt2[:, pb, :], in0=acc[:, pb, :],
                                         scalar=0.5, in1=xt2[:, pb, :],
                                         op0=ALU.mult, op1=ALU.add)
            s_v.inc(ins)

        @blk.scalar
        def _(sc, k=k, pb=pb):
            s_v.wait(sc)
            s_out.dma(sc.dma_start(out=y_out[k * 128:(k + 1) * 128, :],
                                   in_=xt2[:, pb, :]))

    if ext:
        return

    @blk.gpsimd
    def _(g):
        s_out.wait(g)
        s_gd.wait(g)

    return p.finish()


_L1FFN_NC = None


def run_l1ffn(xshards, Wg, W1e_list, b1e_list, W2e_list, b2e_list):
    global _L1FFN_NC
    if _L1FFN_NC is None:
        _L1FFN_NC = build_l1ffn()
    in_maps = []
    for c in range(NC):
        msg, mcomb = l1ffn_tables(c)
        in_maps.append(dict(
            x=xshards[c], wg=Wg, wg_my=np.ascontiguousarray(Wg[:, c:c + 1]),
            w1=W1e_list[c], b1=b1e_list[c].reshape(1, FF),
            w2=W2e_list[c], b2=b2e_list[c].reshape(1, D),
            msg_idx=msg, mc_idx=mcomb))
    res = run_bass_kernel_spmd(_L1FFN_NC, in_maps, core_ids=list(range(NC)))
    return [np.asarray(res.results[c]["y_mo"]) for c in range(NC)]




def kernel_unfused(L0_attn_x, L0_attn_w_q, L0_attn_w_k, L0_attn_w_v,
           L0_ffn_W_r, L0_ffn_W_k, L0_ffn_W_v, L0_ffn_mu_r, L0_ffn_mu_k,
           L1_attn_W_in, L1_attn_conv_w, L1_attn_conv_b, L1_attn_W_x,
           L1_attn_w_dt, L1_attn_b_dt, L1_attn_A_log, L1_attn_D_param,
           L1_attn_W_out, L1_ffn_Wg, L1_ffn_W1, L1_ffn_b1, L1_ffn_W2,
           L1_ffn_b2):
    f = np.asarray
    x = np.ascontiguousarray(f(L0_attn_x), dtype=np.float32)
    sh = shard_x(x)
    w1l = [np.ascontiguousarray(f(L1_ffn_W1)[c]) for c in range(NE)]
    b1l = [np.ascontiguousarray(f(L1_ffn_b1)[c]) for c in range(NE)]
    w2l = [np.ascontiguousarray(f(L1_ffn_W2)[c]) for c in range(NE)]
    b2l = [np.ascontiguousarray(f(L1_ffn_b2)[c]) for c in range(NE)]

    sh = run_l0ffn(sh, f(L0_ffn_W_r), f(L0_ffn_W_k), f(L0_ffn_W_v),
                   f(L0_ffn_mu_r), f(L0_ffn_mu_k))
    sh = run_l0attn(sh, f(L0_attn_w_q), f(L0_attn_w_k), f(L0_attn_w_v))
    sh = run_l0ffn(sh, f(L0_ffn_W_r), f(L0_ffn_W_k), f(L0_ffn_W_v),
                   f(L0_ffn_mu_r), f(L0_ffn_mu_k))
    sh = run_l1ffn(sh, f(L1_ffn_Wg), w1l, b1l, w2l, b2l)
    sh = run_l1attn(sh, f(L1_attn_W_in), f(L1_attn_conv_w), f(L1_attn_conv_b),
                    f(L1_attn_W_x), f(L1_attn_w_dt), f(L1_attn_b_dt),
                    f(L1_attn_A_log), f(L1_attn_D_param), f(L1_attn_W_out))
    sh = run_l1ffn(sh, f(L1_ffn_Wg), w1l, b1l, w2l, b2l)
    return unshard_x(sh)



def emit_barrier(p, blk):
    sems = [p.sem(n) for n in ("s_ld", "s_st", "s_g", "s_gd", "s_v", "s_a",
                               "s_t", "s_cc", "s_out")]
    sbar = p.sem("s_bar")

    @blk.gpsimd
    def _(g):
        for sm in sems:
            sm.wait(g)
        sbar.inc(g.nop())

    for sec in (blk.sync, blk.vector, blk.scalar, blk.tensor):
        @sec
        def _(eng):
            sbar.wait(eng)


def build_fused():
    p = Prog()
    x_in = p.dram("x", [TPC, D], F32, kind="ExternalInput")
    y_out = p.dram("y", [TPC, D], F32, kind="ExternalOutput")
    xI0 = p.dram("xI0", [TPC, D], F32)
    xI1 = p.dram("xI1", [TPC, D], F32)
    blk = p.block()

    chain_io = [(x_in, xI0), (xI0, xI1), (xI1, xI0), (xI0, xI1), (xI1, xI0),
                (xI0, y_out)]
    builders = [build_l0ffn, build_l0attn, build_l0ffn, build_l1ffn,
                build_l1attn, build_l1ffn]
    for (xi, yo), bld in zip(chain_io, builders):
        p.push_scope()
        bld(ext=dict(p=p, x_in=xi, y_out=yo))
        p.pop_scope()
        emit_barrier(p, blk)

    s_out = p.sem("s_out")

    @blk.gpsimd
    def _(g):
        s_out.wait(g)

    return p.finish()


_FUSED_NC = None


def kernel_fused(L0_attn_x, L0_attn_w_q, L0_attn_w_k, L0_attn_w_v,
                 L0_ffn_W_r, L0_ffn_W_k, L0_ffn_W_v, L0_ffn_mu_r, L0_ffn_mu_k,
                 L1_attn_W_in, L1_attn_conv_w, L1_attn_conv_b, L1_attn_W_x,
                 L1_attn_w_dt, L1_attn_b_dt, L1_attn_A_log, L1_attn_D_param,
                 L1_attn_W_out, L1_ffn_Wg, L1_ffn_W1, L1_ffn_b1, L1_ffn_W2,
                 L1_ffn_b2):
    global _FUSED_NC
    if _FUSED_NC is None:
        _FUSED_NC = build_fused()
    f = np.asarray
    x = np.ascontiguousarray(f(L0_attn_x), dtype=np.float32)
    sh = shard_x(x)
    mu = np.stack([f(L0_ffn_mu_r), f(L0_ffn_mu_k)]).astype(np.float32)
    Wg = f(L1_ffn_Wg)
    in_maps = []
    for c in range(NC):
        tb = attn_tables(c)
        cbidx, smask = l1attn_tables(c)
        msg, mcomb = l1ffn_tables(c)
        im = dict(x=sh[c])
        in_maps.append(im)
        im.update(dict(
            w_r=f(L0_ffn_W_r), w_k=f(L0_ffn_W_k), w_v=f(L0_ffn_W_v),
            mu_rk=mu, bidx=l0ffn_bidx(c),
            wq_at=f(L0_attn_w_q), wk_at=f(L0_attn_w_k), wv_at=f(L0_attn_w_v),
        ))
        im.update(dict(
            qidx=tb["qidx"], kidx=tb["kidx"], vidx=tb["vidx"], thr=tb["thr"],
            rf=tb["rf"], rb=tb["rb"], ysc=tb["ysc"],
            w_in=f(L1_attn_W_in), conv_w=f(L1_attn_conv_w).reshape(DIN, 4),
            conv_b=f(L1_attn_conv_b).reshape(1, DIN), w_x=f(L1_attn_W_x),
            w_dt=f(L1_attn_w_dt).reshape(1, DIN),
            b_dt=f(L1_attn_b_dt).reshape(1, DIN), a_log=f(L1_attn_A_log),
            d_par=f(L1_attn_D_param).reshape(1, DIN), w_out=f(L1_attn_W_out),
            cbidx=cbidx, smask=smask,
            wg=Wg, wg_my=np.ascontiguousarray(Wg[:, c:c + 1]),
            w1=np.ascontiguousarray(f(L1_ffn_W1)[c]),
            b1=np.ascontiguousarray(f(L1_ffn_b1)[c]).reshape(1, FF),
            w2=np.ascontiguousarray(f(L1_ffn_W2)[c]),
            b2=np.ascontiguousarray(f(L1_ffn_b2)[c]).reshape(1, D),
            msg_idx=msg, mc_idx=mcomb))
    res = run_bass_kernel_spmd(_FUSED_NC, in_maps, core_ids=list(range(NC)))
    return unshard_x([np.asarray(res.results[c]["y"]) for c in range(NC)])


kernel = kernel_fused


# version 04
# speedup vs baseline: 1.3541x; 1.3541x over previous
import numpy as np
from contextlib import ExitStack

import concourse.bass as bass
import concourse.mybir as mybir
from concourse.bass import IndirectOffsetOnAxis
from concourse.bass_utils import run_bass_kernel_spmd

F32 = mybir.dt.float32
BF16 = mybir.dt.bfloat16
I32 = mybir.dt.int32
AF = mybir.ActivationFunctionType
ALU = mybir.AluOpType
AX = mybir.AxisListType

NC = 8
B = 4
T = 8192
D = 1024
SL = 512
SPB = T // SL
NBS = B * 2
TPC = B * 2 * SL
EPS = 1e-5
HUGE = 1 << 17


def stripe_of(core, slot):
    return core if slot == 0 else (SPB - 1 - core)


def owner_of(stripe):
    return (stripe, 0) if stripe < NC else (SPB - 1 - stripe, 1)


class Sem:
    def __init__(self, ctx, nc, name):
        self.h = ctx.enter_context(nc.semaphore(name))
        self.n = 0

    def inc(self, instr, k=1):
        instr.then_inc(self.h, k)
        self.n += k
        return instr

    def dma(self, instr):
        return self.inc(instr, 16)

    def wait(self, eng, n=None):
        eng.wait_ge(self.h, self.n if n is None else n)

    def chain(self, eng, instr):
        self.inc(instr, 1)
        self.wait(eng)
        return instr


class Prog:
    def __init__(self):
        self.nc = bass.Bass("TRN2", target_bir_lowering=False, debug=False)
        self.ctx = ExitStack()
        self.ctx.__enter__()
        self._sems = {}
        self._drams = {}
        self._names = set()
        self._scopes = []
        self._blk = None

    def _uniq(self, name):
        n, i = name, 0
        while n in self._names:
            i += 1
            n = f"{name}_{i}"
        self._names.add(n)
        return n

    def push_scope(self):
        st = ExitStack()
        self._scopes.append(st)
        return st

    def pop_scope(self):
        self._scopes.pop().close()

    def _stack(self):
        return self._scopes[-1] if self._scopes else self.ctx

    def sem(self, name):
        if name not in self._sems:
            self._sems[name] = Sem(self.ctx, self.nc, name)
        return self._sems[name]

    def sb(self, name, shape, dtype):
        return self._stack().enter_context(
            self.nc.sbuf_tensor(self._uniq(name), shape, dtype))

    def sb_into(self, stack, name, shape, dtype):
        return stack.enter_context(
            self.nc.sbuf_tensor(self._uniq(name), shape, dtype))

    def ps(self, name, shape, dtype):
        return self._stack().enter_context(
            self.nc.psum_tensor(self._uniq(name), shape, dtype))

    def dram(self, name, shape, dtype, kind="Internal"):
        if name in self._drams:
            ap, sh, dt = self._drams[name]
            assert list(sh) == list(shape) and dt == dtype, name
            return ap
        ap = self.nc.dram_tensor(name, shape, dtype, kind=kind).ap()
        self._drams[name] = (ap, shape, dtype)
        return ap

    def block(self):
        if self._blk is None:
            self._blk = self.ctx.enter_context(self.nc.Block())
        return self._blk

    def finish(self):
        self.ctx.close()
        return self.nc


def load_col(eng, sem, dst_col, src_row):
    sem.dma(eng.dma_start(out=dst_col, in_=src_row.rearrange("a b -> b a")))


def emit_identities(blk, s_g, idb, idf):
    @blk.gpsimd
    def _(g):
        g.memset(idb[:], 0.0)
        s_g.inc(g.memset(idf[:], 0.0))
        s_g.wait(g)
        g.affine_select(out=idb[:], in_=idb[:], compare_op=ALU.not_equal,
                        fill=1.0, base=0, pattern=[[-1, 128]], channel_multiplier=1)
        s_g.inc(g.affine_select(out=idf[:], in_=idf[:], compare_op=ALU.not_equal,
                                fill=1.0, base=0, pattern=[[-1, 128]],
                                channel_multiplier=1))


def emit_weightT(p, blk, wap, dst, dout_blocks, din_blocks, bufs, sems):
    wload, wload16, ps_t, idb = bufs
    s_ld, s_v, s_t, s_g = sems
    for i in range(dout_blocks):
        @blk.sync
        def _(s, i=i):
            s_v.wait(s)
            s_ld.dma(s.dma_start(out=wload[:, 0:128 * din_blocks],
                                 in_=wap[128 * i:128 * (i + 1), 0:128 * din_blocks]))

        @blk.vector
        def _(v, i=i):
            s_ld.wait(v)
            s_t.wait(v)
            s_v.inc(v.tensor_copy(wload16[:, 0:128 * din_blocks],
                                  wload[:, 0:128 * din_blocks]))

        for jj in range((din_blocks + 3) // 4):
            nj = min(4, din_blocks - jj * 4)

            @blk.tensor
            def _(t, i=i, jj=jj, nj=nj):
                s_v.wait(t)
                s_g.wait(t)
                for j4 in range(nj):
                    j = jj * 4 + j4
                    ins = t.transpose(ps_t[:, 128 * j4:128 * (j4 + 1)],
                                      wload16[:, 128 * j:128 * (j + 1)], idb[:])
                s_t.inc(ins)

            @blk.vector
            def _(v, i=i, jj=jj, nj=nj):
                s_t.wait(v)
                for j4 in range(nj):
                    j = jj * 4 + j4
                    ins = v.tensor_copy(dst[:, j, 128 * i:128 * (i + 1)],
                                        ps_t[:, 128 * j4:128 * (j4 + 1)])
                s_v.inc(ins)


def emit_ln_pass(p, blk, x_src, lnD, ag_in, bufs, sems, n_tiles,
                 lnD_rowmaj=True, stage_last=True):
    xt, sq, stat, ln16 = bufs
    s_ld, s_v, s_a, s_st, s_gd = sems
    for k in range(n_tiles):
        bs, ti = divmod(k, 4)
        pb = k % 2

        @blk.sync
        def _(s, k=k, pb=pb):
            s_v.wait(s)
            s_ld.dma(s.dma_start(out=xt[:, pb, :], in_=x_src[k * 128:(k + 1) * 128, :]))

        @blk.vector
        def _(v, pb=pb):
            s_ld.wait(v)
            s_a.wait(v)
            s_v.chain(v, v.tensor_reduce(out=stat[:, 0:1], in_=xt[:, pb, :],
                                         axis=AX.X, op=ALU.add))
            s_v.chain(v, v.tensor_scalar_mul(stat[:, 2:3], stat[:, 0:1], 1.0 / D))
            s_v.inc(v.tensor_scalar(out=sq[:, pb, :], in0=xt[:, pb, :], scalar1=stat[:, 2:3],
                                    scalar2=None, op0=ALU.subtract))

        @blk.scalar
        def _(sc, pb=pb):
            s_v.wait(sc)
            s_a.inc(sc.activation(out=sq[:, pb, :], in_=sq[:, pb, :], func=AF.Square,
                                  accum_out=stat[:, 1:2]))

        @blk.vector
        def _(v):
            s_a.wait(v)
            s_v.inc(v.tensor_scalar(out=stat[:, 3:4], in0=stat[:, 1:2],
                                    scalar1=1.0 / D, scalar2=EPS,
                                    op0=ALU.mult, op1=ALU.add))

        @blk.scalar
        def _(sc):
            s_v.wait(sc)
            s_a.inc(sc.activation(out=stat[:, 4:5], in_=stat[:, 3:4], func=AF.Sqrt))

        @blk.vector
        def _(v, pb=pb):
            s_a.wait(v)
            s_st.wait(v)
            s_gd.wait(v)
            s_v.chain(v, v.reciprocal(stat[:, 5:6], stat[:, 4:5]))
            s_v.chain(v, v.tensor_scalar(out=sq[:, pb, :], in0=xt[:, pb, :],
                                         scalar1=stat[:, 2:3], scalar2=None,
                                         op0=ALU.subtract))
            s_v.inc(v.tensor_scalar(out=ln16[:, pb, :], in0=sq[:, pb, :], scalar1=stat[:, 5:6],
                                    scalar2=None, op0=ALU.mult))

        @blk.scalar
        def _(sc, bs=bs, ti=ti, pb=pb):
            s_v.wait(sc)
            s_st.dma(sc.dma_start(out=lnD[bs, 1 + ti * 128:1 + (ti + 1) * 128, :],
                                  in_=ln16[:, pb, :]))

        if stage_last and ti == 3:
            @blk.gpsimd
            def _(g, bs=bs, pb=pb):
                s_v.wait(g)
                s_gd.dma(g.dma_start(out=ag_in[bs:bs + 1, :],
                                     in_=ln16[127:128, pb, :]))



def build_l0ffn(ext=None):
    p = ext["p"] if ext else Prog()
    nc = p.nc

    x_in = ext["x_in"] if ext else p.dram("x", [TPC, D], F32, kind="ExternalInput")
    w_r = p.dram("w_r", [D, D], F32, kind="ExternalInput")
    w_k = p.dram("w_k", [D, D], F32, kind="ExternalInput")
    w_v = p.dram("w_v", [D, D], F32, kind="ExternalInput")
    mu_rk = p.dram("mu_rk", [2, D], F32, kind="ExternalInput")
    bidx = p.dram("bidx", [NBS, 1], I32, kind="ExternalInput")
    y_out = ext["y_out"] if ext else p.dram("y", [TPC, D], F32, kind="ExternalOutput")

    lnD = p.dram("lnD", [NBS, 1 + SL, D], BF16)
    ag_in = p.dram("ag_in", [NBS, D], BF16)
    ag_out = p.dram("ag_out", [NC * NBS, D], BF16)

    xt = p.sb("xt", [128, 2, D], F32)
    xt2 = p.sb("xt2", [128, D], F32)
    sq = p.sb("sq", [128, 2, D], F32)
    stat = p.sb("stat", [128, 8], F32)
    ln16 = p.sb("ln16", [128, 2, D], BF16)
    wload = p.sb("wload", [128, D], F32)
    wload16 = p.sb("wload16", [128, D], BF16)
    wts = {nm: p.sb(nm + "T", [128, 8, D], BF16) for nm in ("wr", "wk", "wv")}
    mu_t = p.sb("mu_t", [128, 16], F32)
    bidx_sb = p.sb("bidx_sb", [NBS, 1], I32)
    brow = p.sb("brow", [NBS, D], BF16)
    uT = p.sb("uT", [128, 8, SL], BF16)
    upT = p.sb("upT", [128, 8, SL], BF16)
    xrT = p.sb("xrT", [128, 8, SL], BF16)
    xkT = p.sb("xkT", [128, 8, SL], BF16)
    rT = p.sb("rT", [128, 8, SL], BF16)
    kaT = p.sb("kaT", [128, 8, SL], BF16)
    vvT = p.sb("vvT", [128, 8, SL], F32)
    relu_t = p.sb("relu_t", [128, SL], F32)
    idb = p.sb("idb", [128, 128], BF16)
    idf = p.sb("idf", [128, 128], F32)

    ps_t = p.ps("ps_t", [128, 512], BF16)
    ps_tf = p.ps("ps_tf", [128, 512], F32)
    ps_m = p.ps("ps_m", [128, 512], F32)
    ps_m2 = p.ps("ps_m2", [128, 512], F32)

    s_ld = p.sem("s_ld")
    s_st = p.sem("s_st")
    s_g = p.sem("s_g")
    s_gd = p.sem("s_gd")
    s_v = p.sem("s_v")
    s_a = p.sem("s_a")
    s_t = p.sem("s_t")
    s_cc = p.sem("s_cc")
    s_out = p.sem("s_out")

    blk = p.block()

    emit_identities(blk, s_g, idb, idf)

    @blk.sync
    def _(s):
        s_ld.dma(s.dma_start(out=bidx_sb[:], in_=bidx))
        for ch in range(8):
            load_col(s, s_ld, mu_t[:, 2 * ch:2 * ch + 1],
                     mu_rk[0:1, 128 * ch:128 * (ch + 1)])
            load_col(s, s_ld, mu_t[:, 2 * ch + 1:2 * ch + 2],
                     mu_rk[1:2, 128 * ch:128 * (ch + 1)])

    for nm, wap in (("wr", w_r), ("wk", w_k), ("wv", w_v)):
        emit_weightT(p, blk, wap, wts[nm], 8, 8,
                     (wload, wload16, ps_t, idb), (s_ld, s_v, s_t, s_g))

    emit_ln_pass(p, blk, x_in, lnD, ag_in, (xt, sq, stat, ln16),
                 (s_ld, s_v, s_a, s_st, s_gd), TPC // 128)

    @blk.gpsimd
    def _(g):
        s_gd.wait(g)
        s_st.wait(g)
        s_cc.inc(g.collective_compute(
            "AllGather", ALU.bypass, replica_groups=[list(range(NC))],
            ins=[ag_in], outs=[ag_out]))
        s_cc.wait(g)
        s_g.inc(g.memset(brow[:], 0.0))
        s_g.wait(g)
        s_ld.wait(g)
        s_gd.dma(g.indirect_dma_start(
            out=brow[:], out_offset=None, in_=ag_out,
            in_offset=IndirectOffsetOnAxis(bidx_sb[:, 0:1], axis=0),
            bounds_check=NC * NBS - 1, oob_is_err=False))
        s_gd.wait(g)
        for bs in range(NBS):
            s_gd.dma(g.dma_start(out=lnD[bs, 0:1, :], in_=brow[bs:bs + 1, :]))

    for bs in range(NBS):
        for ti in range(4):
            @blk.sync
            def _(s, bs=bs, ti=ti):
                if ti == 0:
                    s_g.wait(s)
                    s_gd.wait(s)
                    s_st.wait(s)
                    s_v.wait(s)
                else:
                    s_v.wait(s)
                s_ld.dma(s.dma_start(out=ln16[:, 0, :],
                                     in_=lnD[bs, 1 + ti * 128:1 + (ti + 1) * 128, :]))
                s_ld.dma(s.dma_start(out=ln16[:, 1, :],
                                     in_=lnD[bs, ti * 128:(ti + 1) * 128, :]))

            for src, dstbuf in ((0, uT), (1, upT)):
                for jj in range(2):
                    @blk.tensor
                    def _(t, src=src, jj=jj):
                        s_ld.wait(t)
                        s_v.wait(t)
                        s_g.wait(t)
                        for j4 in range(4):
                            j = jj * 4 + j4
                            ins = t.transpose(ps_t[:, 128 * j4:128 * (j4 + 1)],
                                              ln16[:, src, 128 * j:128 * (j + 1)], idb[:])
                        s_t.inc(ins)

                    @blk.vector
                    def _(v, dstbuf=dstbuf, ti=ti, jj=jj):
                        s_t.wait(v)
                        for j4 in range(4):
                            j = jj * 4 + j4
                            ins = v.tensor_copy(dstbuf[:, j, 128 * ti:128 * (ti + 1)],
                                                ps_t[:, 128 * j4:128 * (j4 + 1)])
                        s_v.inc(ins)

        @blk.vector
        def _(v, bs=bs):
            s_v.wait(v)
            for j in range(8):
                ins = v.tensor_sub(xrT[:, j, :], uT[:, j, :], upT[:, j, :])
            s_v.chain(v, ins)
            for j in range(8):
                v.scalar_tensor_tensor(out=xkT[:, j, :], in0=xrT[:, j, :],
                                       scalar=mu_t[:, 2 * j + 1:2 * j + 2],
                                       in1=upT[:, j, :], op0=ALU.mult, op1=ALU.add)
            for j in range(8):
                ins = v.scalar_tensor_tensor(out=xrT[:, j, :], in0=xrT[:, j, :],
                                             scalar=mu_t[:, 2 * j:2 * j + 1],
                                             in1=upT[:, j, :], op0=ALU.mult, op1=ALU.add)
            s_v.inc(ins)

        for i in range(8):
            @blk.tensor
            def _(t, i=i):
                s_v.wait(t)
                s_a.wait(t)
                for j in range(8):
                    ins = t.matmul(ps_m[:, :], wts["wr"][:, j, 128 * i:128 * (i + 1)],
                                   xrT[:, j, :], start=(j == 0), stop=(j == 7))
                s_t.inc(ins)
                for j in range(8):
                    ins = t.matmul(ps_m2[:, :], wts["wk"][:, j, 128 * i:128 * (i + 1)],
                                   xkT[:, j, :], start=(j == 0), stop=(j == 7))
                s_t.inc(ins)

            @blk.scalar
            def _(sc, i=i):
                s_t.wait(sc, s_t.n - 1)
                s_a.inc(sc.activation(out=rT[:, i, :], in_=ps_m[:, :], func=AF.Sigmoid))

            @blk.vector
            def _(v, i=i):
                s_t.wait(v)
                s_a.wait(v, s_a.n - 1)
                s_v.inc(v.tensor_scalar_max(relu_t[:, :], ps_m2[:, :], 0.0))

            @blk.scalar
            def _(sc, i=i):
                s_v.wait(sc)
                s_a.inc(sc.activation(out=kaT[:, i, :], in_=relu_t[:, :], func=AF.Square))

        for i in range(8):
            @blk.tensor
            def _(t, i=i):
                s_a.wait(t)
                s_v.wait(t)
                for j in range(8):
                    ins = t.matmul(ps_m[:, :], wts["wv"][:, j, 128 * i:128 * (i + 1)],
                                   kaT[:, j, :], start=(j == 0), stop=(j == 7))
                s_t.inc(ins)

            @blk.vector
            def _(v, i=i):
                s_t.wait(v)
                s_v.inc(v.tensor_mul(vvT[:, i, :], rT[:, i, :], ps_m[:, :]))

        for ti in range(4):
            @blk.sync
            def _(s, bs=bs, ti=ti):
                s_out.wait(s)
                row0 = bs * SL + ti * 128
                s_ld.dma(s.dma_start(out=xt2[:], in_=x_in[row0:row0 + 128, :]))

            for half in range(2):
                @blk.tensor
                def _(t, ti=ti, half=half):
                    s_v.wait(t)
                    for i4 in range(4):
                        i = half * 4 + i4
                        ins = t.transpose(ps_tf[:, 128 * i4:128 * (i4 + 1)],
                                          vvT[:, i, 128 * ti:128 * (ti + 1)], idf[:])
                    s_t.inc(ins)

                @blk.vector
                def _(v, ti=ti, half=half):
                    s_t.wait(v)
                    s_ld.wait(v)
                    for i4 in range(4):
                        i = half * 4 + i4
                        ins = v.scalar_tensor_tensor(
                            out=xt2[:, 128 * i:128 * (i + 1)],
                            in0=ps_tf[:, 128 * i4:128 * (i4 + 1)],
                            scalar=0.5, in1=xt2[:, 128 * i:128 * (i + 1)],
                            op0=ALU.mult, op1=ALU.add)
                    s_v.inc(ins)

            @blk.scalar
            def _(sc, bs=bs, ti=ti):
                s_v.wait(sc)
                row0 = bs * SL + ti * 128
                s_out.dma(sc.dma_start(out=y_out[row0:row0 + 128, :], in_=xt2[:]))
                s_a.inc(sc.nop())

    if ext:
        return

    @blk.gpsimd
    def _(g):
        s_out.wait(g)

    return p.finish()



def shard_x(x):
    shards = []
    for c in range(NC):
        parts = []
        for b in range(B):
            for slot in range(2):
                st = stripe_of(c, slot)
                parts.append(x[b, st * SL:(st + 1) * SL, :])
        shards.append(np.ascontiguousarray(np.concatenate(parts, axis=0)))
    return shards


def unshard_x(shards):
    x = np.empty((B, T, D), dtype=np.asarray(shards[0]).dtype)
    for c in range(NC):
        sh = np.asarray(shards[c]).reshape(B, 2, SL, D)
        for b in range(B):
            for slot in range(2):
                st = stripe_of(c, slot)
                x[b, st * SL:(st + 1) * SL, :] = sh[b, slot]
    return x


def l0ffn_bidx(core):
    out = np.full((NBS, 1), HUGE, dtype=np.int32)
    for b in range(B):
        for slot in range(2):
            st = stripe_of(core, slot)
            if st == 0:
                continue
            pr, ps = owner_of(st - 1)
            out[b * 2 + slot, 0] = pr * NBS + b * 2 + ps
    return out


_L0FFN_NC = None


def run_l0ffn(xshards, W_r, W_k, W_v, mu_r, mu_k):
    global _L0FFN_NC
    if _L0FFN_NC is None:
        _L0FFN_NC = build_l0ffn()
    mu = np.stack([mu_r, mu_k]).astype(np.float32)
    in_maps = []
    for c in range(NC):
        in_maps.append(dict(x=xshards[c], w_r=W_r, w_k=W_k, w_v=W_v,
                            mu_rk=mu, bidx=l0ffn_bidx(c)))
    res = run_bass_kernel_spmd(_L0FFN_NC, in_maps, core_ids=list(range(NC)))
    return [np.asarray(res.results[c]["y"]) for c in range(NC)]



GAMMA = 1.0 / 32.0
MNEG = -30000.0


def attn_steps(core):
    steps = []
    for b in range(B):
        for slot in range(2):
            s = stripe_of(core, slot)
            for kb in range(s + 1):
                steps.append((b * 2 + slot, kb))
    return steps


def attn_tables(core):
    steps = attn_steps(core)
    ns = len(steps)
    qidx = np.zeros((128, ns * 8), dtype=np.int32)
    kidx = np.zeros((128, ns * 8), dtype=np.int32)
    vidx = np.zeros((128, ns * 4), dtype=np.int32)
    thr = np.zeros((128, ns * 4), dtype=np.float32)
    rf = np.zeros((128, ns), dtype=np.float32)
    rb = np.zeros((128, ns), dtype=np.float32)
    ysc = np.full((128, (ns + 1) * 4), HUGE, dtype=np.int32)
    p = np.arange(128)
    prev_bs = -1
    for k, (bs, kb) in enumerate(steps):
        b, slot = divmod(bs, 2)
        s = stripe_of(core, slot)
        for op in range(8):
            dr = op * 128 + p
            qidx[:, k * 8 + op] = dr * (TPC // SL) + bs
        rk, kslot = owner_of(kb)
        kbs = b * 2 + kslot
        for op in range(8):
            dr = rk * D + op * 128 + p
            kidx[:, k * 8 + op] = dr * (TPC // SL) + kbs
        for op in range(4):
            vidx[:, k * 4 + op] = rk * TPC + kbs * SL + op * 128 + p
        for qi in range(4):
            if kb == s:
                thr[:, k * 4 + qi] = qi * 128 + p
            else:
                thr[:, k * 4 + qi] = 1e4
        newstripe = bs != prev_bs
        rf[:, k] = 0.0 if newstripe else 1.0
        rb[:, k] = MNEG if newstripe else 0.0
        if newstripe and prev_bs >= 0:
            for qi in range(4):
                ysc[:, k * 4 + qi] = prev_bs * SL + qi * 128 + p
        prev_bs = bs
    for qi in range(4):
        ysc[:, ns * 4 + qi] = prev_bs * SL + qi * 128 + p
    return dict(qidx=qidx, kidx=kidx, vidx=vidx, thr=thr, rf=rf, rb=rb, ysc=ysc)


def build_l0attn(debug_dump=False, dbg_steps=None, ext=None):
    p = ext["p"] if ext else Prog()
    nc = p.nc
    ns = 17 * B

    x_in = ext["x_in"] if ext else p.dram("x", [TPC, D], F32, kind="ExternalInput")
    w_q = p.dram("wq_at", [D, D], F32, kind="ExternalInput")
    w_k = p.dram("wk_at", [D, D], F32, kind="ExternalInput")
    w_v = p.dram("wv_at", [D, D], F32, kind="ExternalInput")
    qidx_d = p.dram("qidx", [128, ns * 8], I32, kind="ExternalInput")
    kidx_d = p.dram("kidx", [128, ns * 8], I32, kind="ExternalInput")
    vidx_d = p.dram("vidx", [128, ns * 4], I32, kind="ExternalInput")
    thr_d = p.dram("thr", [128, ns * 4], F32, kind="ExternalInput")
    rf_d = p.dram("rf", [128, ns], F32, kind="ExternalInput")
    rb_d = p.dram("rb", [128, ns], F32, kind="ExternalInput")
    ysc_d = p.dram("ysc", [128, (ns + 1) * 4], I32, kind="ExternalInput")
    y_out = ext["y_out"] if ext else p.dram("y_at", [TPC, D], F32, kind="ExternalOutput")

    lnD = p.dram("lnD", [NBS, 1 + SL, D], BF16)
    qT = p.dram("qT", [D, TPC], BF16)
    agK_in = p.dram("agK_in", [D, TPC], BF16)
    agK = p.dram("agK", [NC * D, TPC], BF16)
    agV_in = p.dram("agV_in", [TPC, D], BF16)
    agV = p.dram("agV", [NC * TPC, D], BF16)
    yD = p.dram("yD", [TPC, D], F32)

    xt = p.sb("xt", [128, 2, D], F32)
    sq = p.sb("sq", [128, 2, D], F32)
    stat = p.sb("stat", [128, 8], F32)
    ln16 = p.sb("ln16", [128, 2, D], BF16)
    wsb = {nm: p.sb("w" + nm, [128, 8, D], BF16) for nm in ("q", "k", "v")}
    lnTw = p.sb("lnTw", [128, 8, SL], BF16)
    otile = p.sb("otile", [128, 2, D], BF16)
    qTs = p.sb("qTs", [128, 8, SL], BF16)
    kTs = p.sb("kTs", [128, 8, SL], BF16)
    vs = p.sb("vs", [128, 4, D], BF16)
    idxcols = p.sb("idxcols", [128, ns * 8], I32)
    idxcols2 = p.sb("idxcols2", [128, ns * 8], I32)
    idxcols3 = p.sb("idxcols3", [128, ns * 4], I32)
    thr_sb = p.sb("thr_sb", [128, ns * 4], F32)
    rf_sb = p.sb("rf_sb", [128, ns], F32)
    rb_sb = p.sb("rb_sb", [128, ns], F32)
    ysc_sb = p.sb("ysc_sb", [128, (ns + 1) * 4], I32)
    kpos = p.sb("kpos", [128, SL], F32)
    kpos_i = p.sb("kpos_i", [128, SL], I32)
    scf = p.sb("scf", [128, SL], F32)
    pmat = p.sb("pmat", [128, SL], BF16)
    pTs = p.sb("pTs", [128, 4, 128], BF16)
    mrow = p.sb("mrow", [128, 16], F32)
    acc = p.sb("acc", [128, 4, D], F32)
    mstate = p.sb("mstate", [128, 4], F32)
    lstate = p.sb("lstate", [128, 4], F32)
    yfin = p.sb("yfin", [128, 4, D], F32)
    idb = p.sb("idb", [128, 128], BF16)
    idf = p.sb("idf", [128, 128], F32)

    ps_t = p.ps("ps_t", [128, 512], BF16)
    ps_s = p.ps("ps_s", [128, 512], F32)
    ps_av = p.ps("ps_av", [128, 512], F32)
    ps_v = p.ps("ps_v", [128, D], F32)

    s_ld = p.sem("s_ld")
    s_st = p.sem("s_st")
    s_g = p.sem("s_g")
    s_gd = p.sem("s_gd")
    s_v = p.sem("s_v")
    s_a = p.sem("s_a")
    s_t = p.sem("s_t")
    s_cc = p.sem("s_cc")
    s_out = p.sem("s_out")

    blk = p.block()
    bregs = {}

    emit_identities(blk, s_g, idb, idf)

    @blk.sync
    def _(s):
        s_ld.dma(s.dma_start(out=idxcols[:], in_=qidx_d))
        s_ld.dma(s.dma_start(out=idxcols2[:], in_=kidx_d))
        s_ld.dma(s.dma_start(out=idxcols3[:], in_=vidx_d))
        s_ld.dma(s.dma_start(out=thr_sb[:], in_=thr_d))
        s_ld.dma(s.dma_start(out=rf_sb[:], in_=rf_d))
        s_ld.dma(s.dma_start(out=rb_sb[:], in_=rb_d))
        s_ld.dma(s.dma_start(out=ysc_sb[:], in_=ysc_d))

    @blk.gpsimd
    def _(g):
        for nm, wap in (("q", w_q), ("k", w_k), ("v", w_v)):
            for kc in range(8):
                s_gd.dma(g.dma_start(out=wsb[nm][:, kc, :],
                                     in_=wap[128 * kc:128 * (kc + 1), :]))
        s_g.chain(g, g.iota(kpos_i[:], pattern=[[1, SL]], base=0, channel_multiplier=0))
        s_g.inc(g.tensor_copy(kpos[:], kpos_i[:]))
        g.memset(acc[:], 0.0)
        g.memset(mrow[:], 0.0)
        g.memset(lstate[:], 1e-30)
        s_g.inc(g.memset(mstate[:], MNEG))
        bregs["yd"] = g.to_reg(TPC - 1)

    emit_ln_pass(p, blk, x_in, lnD, None, (xt, sq, stat, ln16),
                 (s_ld, s_v, s_a, s_st, s_gd), TPC // 128, stage_last=False)

    for bs in range(NBS):
        for ti in range(4):
            @blk.sync
            def _(s, bs=bs, ti=ti):
                if bs == 0 and ti == 0:
                    s_st.wait(s)
                s_v.wait(s)
                s_a.wait(s)
                s_ld.dma(s.dma_start(out=ln16[:, 0, :],
                                     in_=lnD[bs, 1 + ti * 128:1 + (ti + 1) * 128, :]))

            for jj in range(2):
                @blk.tensor
                def _(t, jj=jj):
                    s_ld.wait(t)
                    s_v.wait(t)
                    s_g.wait(t)
                    for j4 in range(4):
                        j = jj * 4 + j4
                        ins = t.transpose(ps_t[:, 128 * j4:128 * (j4 + 1)],
                                          ln16[:, 0, 128 * j:128 * (j + 1)], idb[:])
                    s_t.inc(ins)

                @blk.vector
                def _(v, ti=ti, jj=jj):
                    s_t.wait(v)
                    for j4 in range(4):
                        j = jj * 4 + j4
                        ins = v.tensor_copy(lnTw[:, j, 128 * ti:128 * (ti + 1)],
                                            ps_t[:, 128 * j4:128 * (j4 + 1)])
                    s_v.inc(ins)

        for nm, dstD in (("q", qT), ("k", agK_in)):
            for i in range(8):
                pb = i % 2

                @blk.tensor
                def _(t, nm=nm, i=i):
                    s_v.wait(t)
                    s_a.wait(t)
                    s_gd.wait(t)
                    for j in range(8):
                        ins = t.matmul(ps_s[:, :], wsb[nm][:, j, 128 * i:128 * (i + 1)],
                                       lnTw[:, j, :], start=(j == 0), stop=(j == 7))
                    s_t.inc(ins)

                @blk.vector
                def _(v, pb=pb):
                    s_t.wait(v)
                    s_st.wait(v)
                    s_v.inc(v.tensor_copy(otile[:, pb, 0:SL], ps_s[:, :]))

                @blk.scalar
                def _(sc, nm=nm, dstD=dstD, bs=bs, i=i, pb=pb):
                    s_v.wait(sc)
                    s_st.dma(sc.dma_start(out=dstD[128 * i:128 * (i + 1),
                                                   bs * SL:(bs + 1) * SL],
                                          in_=otile[:, pb, 0:SL]))
                    s_a.inc(sc.nop())

        for ti in range(4):
            @blk.tensor
            def _(t, ti=ti):
                s_v.wait(t)
                s_a.wait(t)
                s_gd.wait(t)
                for half in range(2):
                    for j in range(8):
                        ins = t.matmul(ps_v[:, 512 * half:512 * (half + 1)],
                                       lnTw[:, j, 128 * ti:128 * (ti + 1)],
                                       wsb["v"][:, j, 512 * half:512 * (half + 1)],
                                       start=(j == 0), stop=(j == 7))
                s_t.inc(ins)

            @blk.vector
            def _(v, ti=ti):
                s_t.wait(v)
                s_st.wait(v)
                s_v.inc(v.tensor_copy(otile[:, ti % 2, :], ps_v[:, :]))

            @blk.scalar
            def _(sc, bs=bs, ti=ti):
                s_v.wait(sc)
                s_st.dma(sc.dma_start(out=agV_in[bs * SL + ti * 128:bs * SL + (ti + 1) * 128, :],
                                      in_=otile[:, ti % 2, :]))
                s_a.inc(sc.nop())

    @blk.gpsimd
    def _(g):
        s_st.wait(g)
        s_cc.inc(g.collective_compute("AllGather", ALU.bypass,
                                      replica_groups=[list(range(NC))],
                                      ins=[agK_in], outs=[agK]))
        s_cc.inc(g.collective_compute("AllGather", ALU.bypass,
                                      replica_groups=[list(range(NC))],
                                      ins=[agV_in], outs=[agV]))
        s_cc.wait(g)

    if debug_dump:
        dbg = {nm: p.dram("dbg_" + nm, shape, F32, kind="ExternalOutput")
               for nm, shape in [("yD", [TPC, D]), ("qTd", [D, TPC]), ("agKin", [D, TPC]),
                                 ("wq", [128, 8 * D]), ("lnTw", [128, 8 * SL]),
                                 ("lnDd", [NBS * (1 + SL), D]),
                                 ("scf", [128, SL]), ("pmat", [128, SL]),
                                 ("acc", [128, 4 * D]), ("lstate", [128, 4]),
                                 ("mstate", [128, 4]), ("kTs", [128, 8 * SL]),
                                 ("qTs", [128, 8 * SL]), ("vsb", [128, 4 * D]),
                                 ("mrow", [128, 16])]}
    qT_f = qT.rearrange("a (r c) -> (a r) c", c=SL)
    agK_f = agK.rearrange("a (r c) -> (a r) c", c=SL)
    agV_f = agV
    yD_f = yD

    for k in range(ns + 1 if dbg_steps is None else dbg_steps):
        is_flush = (k == ns)

        @blk.vector
        def _(v, k=k):
            if k == 0:
                s_cc.wait(v)
                s_g.wait(v)
            else:
                s_v.wait(v)
                s_gd.wait(v)
            for qi in range(4):
                ins = v.reciprocal(mrow[:, 8 + qi:9 + qi], lstate[:, qi:qi + 1])
            s_v.chain(v, ins)
            for qi in range(4):
                ins = v.tensor_scalar(out=yfin[:, qi, :], in0=acc[:, qi, :],
                                      scalar1=mrow[:, 8 + qi:9 + qi], scalar2=None,
                                      op0=ALU.mult)
            s_v.inc(ins)

        @blk.gpsimd
        def _(g, k=k):
            s_v.wait(g)
            for qi in range(4):
                ins = g.indirect_dma_start(
                    out=yD_f, out_offset=IndirectOffsetOnAxis(
                        ysc_sb[:, 4 * k + qi:4 * k + qi + 1], axis=0),
                    in_=yfin[:, qi, :], in_offset=None,
                    bounds_check=bregs["yd"], oob_is_err=False)
                s_gd.inc(ins, 16)

        if is_flush:
            break

        @blk.gpsimd
        def _(sc, k=k):
            s_t.wait(sc)
            for op in range(8):
                s_gd.dma(sc.indirect_dma_start(
                    out=qTs[:, op, :], out_offset=None, in_=qT_f,
                    in_offset=IndirectOffsetOnAxis(
                        idxcols[:, 8 * k + op:8 * k + op + 1], axis=0)))
            for op in range(8):
                s_gd.dma(sc.indirect_dma_start(
                    out=kTs[:, op, :], out_offset=None, in_=agK_f,
                    in_offset=IndirectOffsetOnAxis(
                        idxcols2[:, 8 * k + op:8 * k + op + 1], axis=0)))
            for op in range(4):
                s_gd.dma(sc.indirect_dma_start(
                    out=vs[:, op, :], out_offset=None, in_=agV_f,
                    in_offset=IndirectOffsetOnAxis(
                        idxcols3[:, 4 * k + op:4 * k + op + 1], axis=0)))

        for qi in range(4):
            @blk.tensor
            def _(t, k=k, qi=qi):
                if qi == 0:
                    s_gd.wait(t)
                s_v.wait(t)
                for j in range(8):
                    ins = t.matmul(ps_s[:, :], qTs[:, j, 128 * qi:128 * (qi + 1)],
                                   kTs[:, j, :], start=(j == 0), stop=(j == 7))
                s_t.inc(ins)

            @blk.vector
            def _(v, k=k, qi=qi):
                s_t.wait(v)
                s_a.wait(v)
                s_v.chain(v, v.tensor_scalar(out=scf[:], in0=kpos[:],
                                             scalar1=thr_sb[:, 4 * k + qi:4 * k + qi + 1],
                                             scalar2=None, op0=ALU.is_gt))
                s_v.chain(v, v.scalar_tensor_tensor(out=scf[:], in0=scf[:], scalar=MNEG,
                                                    in1=ps_s[:, :], op0=ALU.mult, op1=ALU.add))
                v.reduce_max(mrow[:, 2:3], scf[:], axis=AX.X)
                s_v.chain(v, v.tensor_scalar(out=mrow[:, 1:2], in0=mstate[:, qi:qi + 1],
                                             scalar1=rf_sb[:, k:k + 1], scalar2=rb_sb[:, k:k + 1],
                                             op0=ALU.mult, op1=ALU.add))
                s_v.chain(v, v.tensor_tensor(out=mrow[:, 3:4], in0=mrow[:, 1:2],
                                             in1=mrow[:, 2:3], op=ALU.max))
                v.tensor_copy(mstate[:, qi:qi + 1], mrow[:, 3:4])
                s_v.chain(v, v.tensor_scalar(out=mrow[:, 7:8], in0=mrow[:, 3:4],
                                             scalar1=-GAMMA, scalar2=None, op0=ALU.mult))
                s_v.chain(v, v.tensor_sub(mrow[:, 0:1], mrow[:, 1:2], mrow[:, 3:4]))
                ins = v.tensor_scalar(out=mrow[:, 0:1], in0=mrow[:, 0:1], scalar1=GAMMA,
                                      scalar2=None, op0=ALU.mult)
                s_v.inc(ins)

            @blk.scalar
            def _(sc, k=k, qi=qi):
                s_v.wait(sc)
                sc.activation(out=pmat[:], in_=scf[:], func=AF.Exp, scale=GAMMA,
                              bias=mrow[:, 7:8], accum_out=mrow[:, 6:7])
                ins = sc.activation(out=mrow[:, 4:5], in_=mrow[:, 0:1], func=AF.Exp)
                s_a.inc(ins)

            @blk.vector
            def _(v, k=k, qi=qi):
                s_a.wait(v)
                ins = v.scalar_tensor_tensor(out=lstate[:, qi:qi + 1],
                                             in0=lstate[:, qi:qi + 1],
                                             scalar=mrow[:, 4:5], in1=mrow[:, 6:7],
                                             op0=ALU.mult, op1=ALU.add)
                s_v.inc(ins)

            for kc in range(4):
                pass

            @blk.tensor
            def _(t, k=k, qi=qi):
                s_a.wait(t)
                s_v.wait(t)
                for kc in range(4):
                    ins = t.transpose(ps_t[:, 128 * kc:128 * (kc + 1)],
                                      pmat[:, 128 * kc:128 * (kc + 1)], idb[:])
                s_t.inc(ins)

            @blk.vector
            def _(v, k=k, qi=qi):
                s_t.wait(v)
                for kc in range(4):
                    ins = v.tensor_copy(pTs[:, kc, :], ps_t[:, 128 * kc:128 * (kc + 1)])
                s_v.inc(ins)

            for half in range(2):
                @blk.tensor
                def _(t, k=k, qi=qi, half=half):
                    s_v.wait(t)
                    if half == 0:
                        pass
                    else:
                        s_v.wait(t)
                    for kc in range(4):
                        ins = t.matmul(ps_av[:, :], pTs[:, kc, :],
                                       vs[:, kc, 512 * half:512 * (half + 1)],
                                       start=(kc == 0), stop=(kc == 3))
                    s_t.inc(ins)

                @blk.vector
                def _(v, k=k, qi=qi, half=half):
                    s_t.wait(v)
                    ins = v.scalar_tensor_tensor(
                        out=acc[:, qi, 512 * half:512 * (half + 1)],
                        in0=acc[:, qi, 512 * half:512 * (half + 1)],
                        scalar=mrow[:, 4:5], in1=ps_av[:, :],
                        op0=ALU.mult, op1=ALU.add)
                    s_v.inc(ins)

    if debug_dump:
        @blk.gpsimd
        def _(g):
            s_v.wait(g)
            s_a.wait(g)
            s_t.wait(g)
            s_gd.dma(g.dma_start(out=dbg["scf"], in_=scf[:]))
            s_gd.dma(g.dma_start(out=dbg["pmat"], in_=pmat[:]))
            s_gd.dma(g.dma_start(out=dbg["acc"], in_=acc[:].rearrange("p a b -> p (a b)")))
            s_gd.dma(g.dma_start(out=dbg["lstate"], in_=lstate[:]))
            s_gd.dma(g.dma_start(out=dbg["mstate"], in_=mstate[:]))
            s_gd.dma(g.dma_start(out=dbg["kTs"], in_=kTs[:].rearrange("p a b -> p (a b)")))
            s_gd.dma(g.dma_start(out=dbg["qTs"], in_=qTs[:].rearrange("p a b -> p (a b)")))
            s_gd.dma(g.dma_start(out=dbg["vsb"], in_=vs[:].rearrange("p a b -> p (a b)")))
            s_gd.dma(g.dma_start(out=dbg["mrow"], in_=mrow[:]))
            s_gd.dma(g.dma_start(out=dbg["yD"], in_=yD))
            s_gd.dma(g.dma_start(out=dbg["qTd"], in_=qT))
            s_gd.dma(g.dma_start(out=dbg["agKin"], in_=agK_in))
            s_gd.dma(g.dma_start(out=dbg["wq"], in_=wsb["q"][:].rearrange("p a b -> p (a b)")))
            s_gd.dma(g.dma_start(out=dbg["lnTw"], in_=lnTw[:].rearrange("p a b -> p (a b)")))
            s_gd.dma(g.dma_start(out=dbg["lnDd"], in_=lnD.rearrange("a b c -> (a b) c")))
            s_gd.wait(g)

    for k in range(TPC // 128 if dbg_steps is None else 0):
        pb = k % 2

        @blk.sync
        def _(s, k=k, pb=pb):
            s_gd.wait(s)
            s_out.wait(s)
            s_ld.dma(s.dma_start(out=xt[:, pb, :], in_=x_in[128 * k:128 * (k + 1), :]))
            s_ld.dma(s.dma_start(out=sq[:, pb, :], in_=yD[128 * k:128 * (k + 1), :]))

        @blk.vector
        def _(v, pb=pb):
            s_ld.wait(v)
            s_v.inc(v.tensor_add(xt[:, pb, :], xt[:, pb, :], sq[:, pb, :]))

        @blk.scalar
        def _(sc, k=k, pb=pb):
            s_v.wait(sc)
            s_out.dma(sc.dma_start(out=y_out[128 * k:128 * (k + 1), :], in_=xt[:, pb, :]))
            s_a.inc(sc.nop())

    if ext:
        return

    @blk.gpsimd
    def _(g):
        s_out.wait(g)

    return p.finish()


_L0ATTN_NC = None


def run_l0attn(xshards, w_q, w_k, w_v):
    global _L0ATTN_NC
    if _L0ATTN_NC is None:
        _L0ATTN_NC = build_l0attn()
    in_maps = []
    for c in range(NC):
        tb = attn_tables(c)
        in_maps.append(dict(x=xshards[c], wq_at=w_q, wk_at=w_k, wv_at=w_v,
                            qidx=tb["qidx"], kidx=tb["kidx"], vidx=tb["vidx"],
                            thr=tb["thr"], rf=tb["rf"], rb=tb["rb"], ysc=tb["ysc"]))
    res = run_bass_kernel_spmd(_L0ATTN_NC, in_maps, core_ids=list(range(NC)))
    return [np.asarray(res.results[c]["y_at"]) for c in range(NC)]



DIN = 2048
NSTATE = 16
NDT = DIN // 128


def l1attn_tables(core):
    p = np.arange(128)
    cbidx = np.full((128, NBS * NDT), HUGE, dtype=np.int32)
    for b in range(B):
        for slot in range(2):
            st = stripe_of(core, slot)
            if st == 0:
                continue
            pr, ps = owner_of(st - 1)
            pbs = b * 2 + ps
            for dt in range(NDT):
                cbidx[:, (b * 2 + slot) * NDT + dt] = \
                    pr * (NBS * DIN) + pbs * DIN + dt * 128 + p
    smask = np.zeros((128, SPB * 2), dtype=np.float32)
    for slot in range(2):
        smask[:, stripe_of(core, slot) * 2 + slot] = 1.0
    return cbidx, smask


def build_l1attn(ext=None):
    p = ext["p"] if ext else Prog()
    nc = p.nc

    x_in = ext["x_in"] if ext else p.dram("x", [TPC, D], F32, kind="ExternalInput")
    w_in = p.dram("w_in", [2 * DIN, D], F32, kind="ExternalInput")
    conv_w = p.dram("conv_w", [DIN, 4], F32, kind="ExternalInput")
    conv_b = p.dram("conv_b", [1, DIN], F32, kind="ExternalInput")
    w_x = p.dram("w_x", [33, DIN], F32, kind="ExternalInput")
    w_dt = p.dram("w_dt", [1, DIN], F32, kind="ExternalInput")
    b_dt = p.dram("b_dt", [1, DIN], F32, kind="ExternalInput")
    a_log = p.dram("a_log", [DIN, NSTATE], F32, kind="ExternalInput")
    d_par = p.dram("d_par", [1, DIN], F32, kind="ExternalInput")
    w_out = p.dram("w_out", [D, DIN], F32, kind="ExternalInput")
    cbidx_d = p.dram("cbidx", [128, NBS * NDT], I32, kind="ExternalInput")
    smask_d = p.dram("smask", [128, SPB * 2], F32, kind="ExternalInput")
    y_out = ext["y_out"] if ext else p.dram("y_ma", [TPC, D], F32, kind="ExternalOutput")

    lnD = p.dram("lnD", [NBS, 1 + SL, D], BF16)
    xzpreD = p.dram("xzpreD", [NBS, DIN, SL], BF16)
    zD = p.dram("zD", [NBS, DIN, SL], BF16)
    xcD = p.dram("xcD", [NBS, DIN, SL], BF16)
    csD = p.dram("csD", [NBS, DIN, SL], F32)
    ypD = p.dram("ypD", [NBS, DIN, SL], F32)
    yfD = p.dram("yfD", [NBS, DIN, SL], BF16)
    xpD = p.dram("xpD", [NBS, 33, SL], F32)
    cb_ag_in = p.dram("cb_ag_in", [NBS * DIN, 8], F32)
    cb_ag = p.dram("cb_ag", [NC * NBS * DIN, 8], F32)
    sm_ag_in = p.dram("sm_ag_in", [NBS * DIN, 24], F32)
    sm_ag = p.dram("sm_ag", [NC * NBS * DIN, 24], F32)

    xt = p.sb("xt", [128, 2, D], F32)
    sq = p.sb("sq", [128, 2, D], F32)
    stat = p.sb("stat", [128, 8], F32)
    ln16 = p.sb("ln16", [128, 2, D], BF16)
    woT = p.sb("woT", [128, NDT, D], BF16)
    wxT = p.sb("wxT", [128, NDT, 33], BF16)
    cw = p.sb("cw", [128, NDT, 4], F32)
    cb_c = p.sb("cb_c", [128, NDT], F32)
    wdt_c = p.sb("wdt_c", [128, NDT], F32)
    bdt_c = p.sb("bdt_c", [128, NDT], F32)
    dpar_c = p.sb("dpar_c", [128, NDT], F32)
    asb = p.sb("asb", [128, NDT, NSTATE], F32)
    acol = p.sb("acol", [128, NDT, NSTATE], F32)
    cbidx_sb = p.sb("cbidx_sb", [128, NBS * NDT], I32)
    smask_sb = p.sb("smask_sb", [128, SPB * 2], F32)
    cbnd = p.sb("cbnd", [128, NBS, NDT, 4], F32)
    lnTw = p.sb("lnTw", [128, 8, SL], BF16)
    ztile = p.sb("ztile", [128, 2, SL], BF16)
    onesc = p.sb("onesc", [1, 128], F32)
    onesw = p.sb("onesw", [128, SL], F32)
    selm = p.sb("selm", [33, 33 * 128], F32)
    xt2 = p.sb("xt2", [128, D], F32)
    idb = p.sb("idb", [128, 128], BF16)
    idf = p.sb("idf", [128, 128], F32)

    ps_t = p.ps("ps_t", [128, 512], BF16)
    ps_m = p.ps("ps_m", [128, 512], F32)
    ps_m2 = p.ps("ps_m2", [128, 512], F32)
    ps_o = p.ps("ps_o", [128, 4, 512], F32)
    ps_tf = p.ps("ps_tf", [128, 512], F32)

    s_ld = p.sem("s_ld")
    s_st = p.sem("s_st")
    s_g = p.sem("s_g")
    s_gd = p.sem("s_gd")
    s_v = p.sem("s_v")
    s_a = p.sem("s_a")
    s_t = p.sem("s_t")
    s_cc = p.sem("s_cc")
    s_out = p.sem("s_out")

    blk = p.block()

    tail = ExitStack()
    wload = p.sb_into(tail, "wload", [128, 2 * D], F32)
    wload16 = p.sb_into(tail, "wload16", [128, 2 * D], BF16)
    winT = p.sb_into(tail, "winT", [128, 8, 2 * DIN], BF16)

    emit_identities(blk, s_g, idb, idf)

    @blk.gpsimd
    def _(g):
        s_g.chain(g, g.memset(selm[:], 0.0))
        s_g.inc(g.affine_select(out=selm[:], in_=selm[:], compare_op=ALU.not_equal,
                                fill=1.0, base=0,
                                pattern=[[-1, 33], [0, 128]], channel_multiplier=1))

    @blk.sync
    def _(s):
        s_ld.dma(s.dma_start(out=cbidx_sb[:], in_=cbidx_d))
        s_ld.dma(s.dma_start(out=smask_sb[:], in_=smask_d))
        s_ld.dma(s.dma_start(out=cw[:], in_=conv_w.rearrange("(a p) c -> p a c", p=128)))
        s_ld.dma(s.dma_start(out=asb[:], in_=a_log.rearrange("(a p) c -> p a c", p=128)))
        for dt in range(NDT):
            load_col(s, s_ld, cb_c[:, dt:dt + 1], conv_b[0:1, 128 * dt:128 * (dt + 1)])
            load_col(s, s_ld, wdt_c[:, dt:dt + 1], w_dt[0:1, 128 * dt:128 * (dt + 1)])
            load_col(s, s_ld, bdt_c[:, dt:dt + 1], b_dt[0:1, 128 * dt:128 * (dt + 1)])
            load_col(s, s_ld, dpar_c[:, dt:dt + 1], d_par[0:1, 128 * dt:128 * (dt + 1)])


    @blk.scalar
    def _(sc):
        s_ld.wait(sc)
        s_a.inc(sc.activation(out=acol[:], in_=asb[:], func=AF.Exp))

    @blk.vector
    def _(v):
        s_a.wait(v)
        v.tensor_scalar_mul(acol[:], acol[:], -1.0)
        v.memset(onesc[:], 1.0)
        v.memset(onesw[:], 1.0)
        s_v.inc(v.memset(cbnd[:], 0.0))

    emit_weightT(p, blk, w_in, winT, 32, 8,
                 (wload, wload16, ps_t, idb), (s_ld, s_v, s_t, s_g))
    emit_weightT(p, blk, w_out, woT, 8, 16,
                 (wload, wload16, ps_t, idb), (s_ld, s_v, s_t, s_g))

    for half in range(2):
        @blk.sync
        def _(s, half=half):
            s_v.wait(s)
            s_t.wait(s)
            s_ld.dma(s.dma_start(out=wload[0:33, 0:D],
                                 in_=w_x[:, D * half:D * (half + 1)]))

        @blk.vector
        def _(v, half=half):
            s_ld.wait(v)
            s_v.inc(v.tensor_copy(wload16[0:33, 0:D], wload[0:33, 0:D]))

        for d8 in range(8):
            @blk.tensor
            def _(t, half=half, d8=d8):
                s_v.wait(t)
                s_g.wait(t)
                ins = t.transpose(ps_t[:, 0:33],
                                  wload16[0:33, 128 * d8:128 * (d8 + 1)],
                                  idb[0:33, 0:33])
                s_t.inc(ins)

            @blk.vector
            def _(v, half=half, d8=d8):
                s_t.wait(v)
                s_v.inc(v.tensor_copy(wxT[:, half * 8 + d8, :], ps_t[:, 0:33]))

    emit_ln_pass(p, blk, x_in, lnD, None, (xt, sq, stat, ln16),
                 (s_ld, s_v, s_a, s_st, s_gd), TPC // 128, stage_last=False)

    for bs in range(NBS):
        for ti in range(4):
            @blk.sync
            def _(s, bs=bs, ti=ti):
                if bs == 0 and ti == 0:
                    s_st.wait(s)
                s_v.wait(s)
                s_a.wait(s)
                s_ld.dma(s.dma_start(out=ln16[:, 0, :],
                                     in_=lnD[bs, 1 + ti * 128:1 + (ti + 1) * 128, :]))

            for jj in range(2):
                @blk.tensor
                def _(t, jj=jj):
                    s_ld.wait(t)
                    s_v.wait(t)
                    s_g.wait(t)
                    for j4 in range(4):
                        j = jj * 4 + j4
                        ins = t.transpose(ps_t[:, 128 * j4:128 * (j4 + 1)],
                                          ln16[:, 0, 128 * j:128 * (j + 1)], idb[:])
                    s_t.inc(ins)

                @blk.vector
                def _(v, ti=ti, jj=jj):
                    s_t.wait(v)
                    for j4 in range(4):
                        j = jj * 4 + j4
                        ins = v.tensor_copy(lnTw[:, j, 128 * ti:128 * (ti + 1)],
                                            ps_t[:, 128 * j4:128 * (j4 + 1)])
                    s_v.inc(ins)

        for i in range(32):
            pb = i % 2

            @blk.tensor
            def _(t, i=i, pb=pb):
                s_v.wait(t)
                s_a.wait(t)
                s_st.wait(t)
                ps = ps_m if pb == 0 else ps_m2
                for j in range(8):
                    ins = t.matmul(ps[:, :], winT[:, j, 128 * i:128 * (i + 1)],
                                   lnTw[:, j, :], start=(j == 0), stop=(j == 7))
                s_t.inc(ins)

            if i < 16:
                @blk.vector
                def _(v, i=i, pb=pb):
                    s_t.wait(v)
                    s_st.wait(v)
                    ps = ps_m if pb == 0 else ps_m2
                    s_v.inc(v.tensor_copy(ztile[:, pb, :], ps[:, :]))

                @blk.scalar
                def _(sc, bs=bs, i=i, pb=pb):
                    s_v.wait(sc)
                    s_st.dma(sc.dma_start(out=xzpreD[bs, 128 * i:128 * (i + 1), :],
                                          in_=ztile[:, pb, :]))
                    s_a.inc(sc.nop())
            else:
                @blk.scalar
                def _(sc, bs=bs, i=i, pb=pb):
                    s_t.wait(sc)
                    s_st.wait(sc)
                    ps = ps_m if pb == 0 else ps_m2
                    s_a.inc(sc.activation(out=ztile[:, pb, :], in_=ps[:, :], func=AF.Silu))

                @blk.scalar
                def _(sc, bs=bs, i=i, pb=pb):
                    s_a.wait(sc)
                    s_st.dma(sc.dma_start(out=zD[bs, 128 * (i - 16):128 * (i - 15), :],
                                          in_=ztile[:, pb, :]))
                    s_a.inc(sc.nop())

    tail.close()

    convbuf = p.sb("convbuf", [128, 2, SL + 3], BF16)
    xcT = p.sb("xcT", [128, NDT, SL], BF16)
    xp = p.sb("xp", [33, SL], F32)
    Bb = p.sb("Bb", [128, NSTATE, SL], BF16)
    Cb = p.sb("Cb", [128, NSTATE, SL], BF16)
    deltaT = p.sb("deltaT", [128, 2, SL], F32)
    dsc1 = p.sb("dsc1", [128, SL], F32)
    dsc2 = p.sb("dsc2", [128, 2, SL], F32)
    dcv = p.sb("dcv", [128, 2, SL], F32)
    dpre_b = p.sb("dpre_b", [128, SL], F32)
    dAt = p.sb("dAt", [128, 2, SL], F32)
    m0t = p.sb("m0t", [128, SL], BF16)
    hseq = p.sb("hseq", [128, 2, SL], BF16)
    z1t = p.sb("z1t", [128, SL], BF16)
    ytile = p.sb("ytile", [128, 2, SL], F32)
    cstile = p.sb("cstile", [128, 2, SL], F32)
    dxc = p.sb("dxc", [128, SL], BF16)
    hstage = p.sb("hstage", [128, NDT, 17], F32)
    smbuf = p.sb("smbuf", [128, NDT, 17], F32)
    hinit_all = p.sb("hinit_all", [128, NDT * NSTATE], F32)
    hexp = p.sb("hexp", [128, NDT * NSTATE], F32)
    hmine = p.sb("hmine", [128, NBS, NDT, NSTATE], F32)
    corr = p.sb("corr", [128, SL], F32)
    sz_t = p.sb("sz_t", [128, 2, SL], BF16)
    xcr = p.sb("xcr", [128, 2, SL], BF16)
    yf_t = p.sb("yf_t", [128, 2, SL], BF16)
    outT = p.sb("outT_l1", [128, 8, SL], BF16)

    @blk.gpsimd
    def _(g):
        s_st.wait(g)
        for bs in range(NBS):
            for dt in range(NDT):
                s_gd.dma(g.dma_start(
                    out=cb_ag_in[bs * DIN + dt * 128:bs * DIN + (dt + 1) * 128, 0:3],
                    in_=xzpreD[bs, dt * 128:(dt + 1) * 128, SL - 3:SL]))
        s_gd.wait(g)
        s_cc.inc(g.collective_compute("AllGather", ALU.bypass,
                                      replica_groups=[list(range(NC))],
                                      ins=[cb_ag_in], outs=[cb_ag]))
        s_cc.wait(g)
        s_v.wait(g)
        breg = g.to_reg(NC * NBS * DIN - 1)
        for bs in range(NBS):
            for dt in range(NDT):
                s_gd.dma(g.indirect_dma_start(
                    out=cbnd[:, bs, dt, 0:3], out_offset=None, in_=cb_ag,
                    in_offset=IndirectOffsetOnAxis(
                        cbidx_sb[:, bs * NDT + dt:bs * NDT + dt + 1], axis=0),
                    bounds_check=breg, oob_is_err=False))
        s_gd.wait(g)

    s_cvl = p.sem("s_cvl")
    s_tap = p.sem("s_tap")
    s_silu = p.sem("s_silu")
    s_sp = p.sem("s_sp")
    s_dp = p.sem("s_dp")
    s_fin = p.sem("s_fin")
    s_exp = p.sem("s_exp")
    s_scan = p.sem("s_scan")
    s_gc = p.sem("s_gc")
    s_hq = p.sem("s_hq")
    s_ga = p.sem("s_ga")

    cvl0, tap0, sil0 = s_cvl.n, s_tap.n, s_silu.n
    sp0, dp0, fin0 = s_sp.n, s_dp.n, s_fin.n
    exp0, scan0, hq0, ga0, gc0 = s_exp.n, s_scan.n, s_hq.n, s_ga.n, s_gc.n
    gd_xc, gd_cs, gd_yp, gd_stg = {}, {}, {}, {}
    xp_t = {}

    def conv_load(g):
        bs_, dt_ = divmod(g, NDT)
        @blk.sync
        def _(s, bs=bs_, dt=dt_, g=g):
            if g == 0:
                s_st.wait(s)
            if g >= 2:
                s_tap.wait(s, tap0 + g - 1)
            s_cvl.dma(s.dma_start(out=convbuf[:, g % 2, 3:3 + SL],
                                  in_=xzpreD[bs, dt * 128:(dt + 1) * 128, :]))

    def conv_tap(g):
        bs_, dt_ = divmod(g, NDT)
        @blk.vector
        def _(v, bs=bs_, dt=dt_, g=g):
            if g == 0:
                s_gd.wait(v)
            s_cvl.wait(v, cvl0 + 16 * (g + 1))
            if g >= 2:
                s_silu.wait(v, sil0 + g - 1)
            s_v.chain(v, v.tensor_copy(convbuf[:, g % 2, 0:3],
                                       cbnd[:, bs, dt, 0:3]))
            s_v.chain(v, v.tensor_scalar(
                out=dcv[:, g % 2, :], in0=convbuf[:, g % 2, 0:SL],
                scalar1=cw[:, dt, 0:1], scalar2=None, op0=ALU.mult))
            for j in range(1, 4):
                ins = v.scalar_tensor_tensor(
                    out=dcv[:, g % 2, :], in0=convbuf[:, g % 2, j:j + SL],
                    scalar=cw[:, dt, j:j + 1], in1=dcv[:, g % 2, :],
                    op0=ALU.mult, op1=ALU.add)
                if j < 3:
                    s_v.chain(v, ins)
            s_tap.inc(ins)

    def emit_silu(sce, g):
        bs_, dt_ = divmod(g, NDT)
        s_tap.wait(sce, tap0 + g + 1)
        if bs_ >= 1:
            s_t.wait(sce, xp_t[bs_ - 1])
            s_fin.wait(sce, fin0 + (bs_ - 1) * NDT + dt_ + 1)
            s_gd.wait(sce, gd_xc[(bs_ - 1) * NDT + dt_])
        s_silu.inc(sce.activation(out=xcT[:, dt_, :], in_=dcv[:, g % 2, :],
                                  func=AF.Silu, bias=cb_c[:, dt_:dt_ + 1]))

    def xc_store(g):
        bs_, dt_ = divmod(g, NDT)
        @blk.gpsimd
        def _(gp, bs=bs_, dt=dt_, g=g):
            s_silu.wait(gp, sil0 + g + 1)
            s_gd.dma(gp.dma_start(out=xcD[bs, dt * 128:(dt + 1) * 128, :],
                                  in_=xcT[:, dt, :]))
            gd_xc[g] = s_gd.n

    for dt in range(NDT):
        conv_load(dt)
        conv_tap(dt)

    @blk.scalar
    def _(sce):
        for dt in range(NDT):
            emit_silu(sce, dt)

    for dt in range(NDT):
        xc_store(dt)

    for bs in range(NBS):
        g0 = bs * NDT
        @blk.tensor
        def _(t, bs=bs):
            s_silu.wait(t, sil0 + 16 * (bs + 1))
            s_v.wait(t)
            for dt in range(NDT):
                ins = t.matmul(ps_m[0:33, :], wxT[:, dt, :], xcT[:, dt, :],
                               start=(dt == 0), stop=(dt == NDT - 1))
            s_t.inc(ins)
            xp_t[bs] = s_t.n

        @blk.vector
        def _(v, bs=bs):
            s_t.wait(v)
            s_v.inc(v.tensor_copy(xp[:, :], ps_m[0:33, :]))

        @blk.gpsimd
        def _(gp, bs=bs):
            s_v.wait(gp)
            s_gd.dma(gp.dma_start(out=xpD[bs], in_=xp[:, :]))
            gd_stg[('xpd', bs)] = s_gd.n

        for n in range(NSTATE):
            @blk.tensor
            def _(t, bs=bs, n=n):
                s_v.wait(t)
                ins = t.matmul(ps_m[:, :], selm[:, 128 * n:128 * (n + 1)],
                               xp[:, :], start=True, stop=True)
                s_t.inc(ins)
                ins = t.matmul(ps_m2[:, :],
                               selm[:, 128 * (NSTATE + n):128 * (NSTATE + n + 1)],
                               xp[:, :], start=True, stop=True)
                s_t.inc(ins)

            @blk.vector
            def _(v, bs=bs, n=n):
                s_t.wait(v)
                v.tensor_copy(Bb[:, n, :], ps_m[:, :])
                s_v.inc(v.tensor_copy(Cb[:, n, :], ps_m2[:, :]))

        @blk.tensor
        def _(t, bs=bs):
            s_v.wait(t)
            ins = t.matmul(ps_m[:, :], selm[:, 128 * 32:128 * 33],
                           xp[:, :], start=True, stop=True)
            s_t.inc(ins)

        @blk.vector
        def _(v, bs=bs):
            s_t.wait(v)
            s_v.inc(v.tensor_copy(dpre_b[:], ps_m[:, :]))

        def emit_prep(g):
            bs_, dt_ = divmod(g, NDT)
            @blk.vector
            def _(v, bs=bs_, dt=dt_, g=g):
                if g >= 2:
                    s_exp.wait(v, exp0 + 16 * (g - 1))
                s_v.chain(v, v.tensor_scalar(out=deltaT[:, g % 2, :], in0=dpre_b[:],
                                             scalar1=wdt_c[:, dt:dt + 1],
                                             scalar2=bdt_c[:, dt:dt + 1],
                                             op0=ALU.mult, op1=ALU.add))
                s_v.chain(v, v.tensor_scalar_max(dsc1[:], deltaT[:, g % 2, :], 0.0))
                ins = v.scalar_tensor_tensor(out=dsc2[:, g % 2, :],
                                             in0=deltaT[:, g % 2, :], scalar=-1.0,
                                             in1=deltaT[:, g % 2, :],
                                             op0=ALU.mult, op1=ALU.max)
                s_dp.inc(ins)

        emit_prep(g0)

        for dt in range(NDT):
            g = g0 + dt
            u0 = 16 * g

            conv_load(g + NDT) if bs + 1 < NBS else None

            @blk.vector
            def _(v, bs=bs, dt=dt, g=g):
                s_sp.wait(v, sp0 + g + 1)
                if bs >= 1:
                    s_gd.wait(v, gd_stg[(bs - 1) * NDT + dt])
                if g >= 2:
                    s_gd.wait(v, gd_cs[g - 2])
                s_v.chain(v, v.tensor_add(deltaT[:, g % 2, :], dsc1[:],
                                          dsc2[:, g % 2, :]))
                s_v.chain(v, v.tensor_tensor_scan(out=cstile[:, g % 2, :],
                                                  data0=onesw[:],
                                                  data1=deltaT[:, g % 2, :],
                                                  initial=0.0,
                                                  op0=ALU.mult, op1=ALU.add))
                s_v.chain(v, v.tensor_mul(dxc[:], deltaT[:, g % 2, :],
                                          xcT[:, dt, :]))
                ins = v.tensor_copy(hstage[:, dt, 16:17],
                                    cstile[:, g % 2, SL - 1:SL])
                s_fin.inc(ins)

            if dt + 1 < NDT:
                emit_prep(g + 1)

            @blk.scalar
            def _(sce, bs=bs, dt=dt, g=g, u0=u0):
                s_dp.wait(sce, dp0 + g + 1)
                s_a.chain(sce, sce.activation(out=dsc2[:, g % 2, :],
                                              in_=dsc2[:, g % 2, :],
                                              func=AF.Exp, scale=-1.0))
                s_sp.inc(sce.activation(out=dsc2[:, g % 2, :],
                                        in_=dsc2[:, g % 2, :],
                                        func=AF.Ln, bias=1.0))
                s_fin.wait(sce, fin0 + g + 1)
                for n in range(NSTATE):
                    u = u0 + n
                    if u >= 2:
                        s_scan.wait(sce, scan0 + u - 1)
                    s_exp.inc(sce.activation(out=dAt[:, u % 2, :],
                                             in_=deltaT[:, g % 2, :], func=AF.Exp,
                                             scale=acol[:, dt, n:n + 1]))
                if bs + 1 < NBS:
                    emit_silu(sce, (bs + 1) * NDT + dt)

            @blk.vector
            def _(v, bs=bs, dt=dt, g=g, u0=u0):
                for n in range(NSTATE):
                    u = u0 + n
                    s_exp.wait(v, exp0 + u + 1)
                    if n == 0 and g >= 2:
                        s_gd.wait(v, gd_yp[g - 2])
                    s_v.chain(v, v.tensor_mul(m0t[:], dxc[:], Bb[:, n, :]))
                    s_v.chain(v, v.tensor_tensor_scan(out=hseq[:, u % 2, :],
                                                      data0=dAt[:, u % 2, :],
                                                      data1=m0t[:], initial=0.0,
                                                      op0=ALU.mult, op1=ALU.add))
                    s_v.chain(v, v.tensor_mul(z1t[:], Cb[:, n, :],
                                              hseq[:, u % 2, :]))
                    if n == 0:
                        s_v.chain(v, v.tensor_copy(ytile[:, g % 2, :], z1t[:]))
                    else:
                        s_v.chain(v, v.tensor_add(ytile[:, g % 2, :],
                                                  ytile[:, g % 2, :], z1t[:]))
                    s_scan.inc(v.tensor_copy(hstage[:, dt, n:n + 1],
                                             hseq[:, u % 2, SL - 1:SL]))

            if bs + 1 < NBS:
                conv_tap((bs + 1) * NDT + dt)

            @blk.gpsimd
            def _(gp, bs=bs, dt=dt, g=g):
                s_fin.wait(gp, fin0 + g + 1)
                s_gd.dma(gp.dma_start(out=csD[bs, dt * 128:(dt + 1) * 128, :],
                                      in_=cstile[:, g % 2, :]))
                gd_cs[g] = s_gd.n
                s_scan.wait(gp, scan0 + 16 * (g + 1))
                s_gd.dma(gp.dma_start(out=ypD[bs, dt * 128:(dt + 1) * 128, :],
                                      in_=ytile[:, g % 2, :]))
                gd_yp[g] = s_gd.n
                s_gd.dma(gp.dma_start(
                    out=sm_ag_in[bs * DIN + dt * 128:bs * DIN + (dt + 1) * 128,
                                 0:17],
                    in_=hstage[:, dt, :]))
                gd_stg[g] = s_gd.n

            if bs + 1 < NBS:
                xc_store((bs + 1) * NDT + dt)

    @blk.gpsimd
    def _(g):
        s_gd.wait(g)
        s_st.wait(g)
        s_cc.inc(g.collective_compute("AllGather", ALU.bypass,
                                      replica_groups=[list(range(NC))],
                                      ins=[sm_ag_in], outs=[sm_ag]))
        s_cc.wait(g)

    for b in range(B):
        @blk.vector
        def _(v, b=b):
            if b == 0:
                s_cc.wait(v)
                v.memset(hmine[:], 0.0)
            s_v.inc(v.memset(hinit_all[:], 0.0))

        for s_idx in range(SPB):
            r_s, slot_s = owner_of(s_idx)
            base = r_s * (NBS * DIN) + (b * 2 + slot_s) * DIN

            @blk.sync
            def _(s, base=base):
                s_v.wait(s)
                s_cc.wait(s)
                s_ld.dma(s.dma_start(
                    out=smbuf[:],
                    in_=sm_ag[base:base + DIN, 0:17].rearrange("(a p) c -> p a c", p=128)))

            @blk.vector
            def _(v, b=b, s_idx=s_idx):
                s_ld.wait(v)
                for slot in range(2):
                    s_v.chain(v, v.scalar_tensor_tensor(
                        out=hmine[:, b * 2 + slot, :, :].rearrange("p a c -> p (a c)"),
                        in0=hinit_all[:],
                        scalar=smask_sb[:, s_idx * 2 + slot:s_idx * 2 + slot + 1],
                        in1=hmine[:, b * 2 + slot, :, :].rearrange("p a c -> p (a c)"),
                        op0=ALU.mult, op1=ALU.add))
                dtot_rep = smbuf[:, :, 16:17].broadcast_to([128, NDT, NSTATE])
                s_v.inc(v.tensor_mul(hexp[:].rearrange("p (a c) -> p a c", a=NDT),
                                     acol[:, :, :], dtot_rep))

            @blk.scalar
            def _(sc):
                s_v.wait(sc)
                s_a.inc(sc.activation(out=hexp[:], in_=hexp[:], func=AF.Exp))

            @blk.vector
            def _(v):
                s_a.wait(v)
                s_v.chain(v, v.tensor_mul(hinit_all[:], hexp[:], hinit_all[:]))
                s_v.inc(v.tensor_add(
                    hinit_all[:].rearrange("p (a c) -> p a c", a=NDT),
                    hinit_all[:].rearrange("p (a c) -> p a c", a=NDT),
                    smbuf[:, :, 0:16]))


    s_ept = p.sem("s_ept")
    s_cor = p.sem("s_cor")
    s_ctl = p.sem("s_ctl")
    s_cl = p.sem("s_cl")
    ept0, cor0, ctl0, cl0 = s_ept.n, s_cor.n, s_ctl.n, s_cl.n
    ga1 = s_ga.n
    gd_yf = {}

    for bs in range(NBS):
        @blk.sync
        def _(s, bs=bs):
            s_t.wait(s)
            s_gd.wait(s, gd_stg[('xpd', bs)])
            s_ld.dma(s.dma_start(out=xp[:, :], in_=xpD[bs]))

        for n in range(NSTATE):
            @blk.tensor
            def _(t, bs=bs, n=n):
                s_v.wait(t)
                if n == 0:
                    s_ld.wait(t)
                ins = t.matmul(ps_m[:, :],
                               selm[:, 128 * (NSTATE + n):128 * (NSTATE + n + 1)],
                               xp[:, :], start=True, stop=True)
                s_t.inc(ins)

            @blk.vector
            def _(v, n=n):
                s_t.wait(v)
                s_v.inc(v.tensor_copy(Cb[:, n, :], ps_m[:, :]))

        for dt in range(NDT):
            h = bs * NDT + dt

            @blk.sync
            def _(s, bs=bs, dt=dt, h=h):
                if h == 0:
                    s_cc.wait(s)
                    s_st.wait(s)
                if h >= 2:
                    s_ept.wait(s, ept0 + 16 * (h - 1))
                    s_ctl.wait(s, ctl0 + h - 1)
                s_cl.dma(s.dma_start(out=cstile[:, h % 2, :],
                                     in_=csD[bs, dt * 128:(dt + 1) * 128, :]))
                s_cl.dma(s.dma_start(out=ytile[:, h % 2, :],
                                     in_=ypD[bs, dt * 128:(dt + 1) * 128, :]))
                s_cl.dma(s.dma_start(out=xcr[:, h % 2, :],
                                     in_=xcD[bs, dt * 128:(dt + 1) * 128, :]))
                s_cl.dma(s.dma_start(out=sz_t[:, h % 2, :],
                                     in_=zD[bs, dt * 128:(dt + 1) * 128, :]))

            @blk.scalar
            def _(sce, bs=bs, dt=dt, h=h):
                s_cl.wait(sce, cl0 + 64 * h + 16)
                for n in range(NSTATE):
                    u = 16 * h + n
                    if u >= 2:
                        s_cor.wait(sce, cor0 + u - 1)
                    s_ept.inc(sce.activation(out=dAt[:, u % 2, :],
                                             in_=cstile[:, h % 2, :], func=AF.Exp,
                                             scale=acol[:, dt, n:n + 1]))

            @blk.vector
            def _(v, bs=bs, dt=dt, h=h):
                for n in range(NSTATE):
                    u = 16 * h + n
                    s_ept.wait(v, ept0 + u + 1)
                    if n == 0:
                        s_cl.wait(v, cl0 + 64 * h + 32)
                    s_v.chain(v, v.scalar_tensor_tensor(
                        out=corr[:], in0=dAt[:, u % 2, :],
                        scalar=hmine[:, bs, dt, n:n + 1],
                        in1=Cb[:, n, :], op0=ALU.mult, op1=ALU.mult))
                    s_cor.inc(v.tensor_add(ytile[:, h % 2, :],
                                           ytile[:, h % 2, :], corr[:]))
                s_cl.wait(v, cl0 + 64 * (h + 1))
                if h >= 2:
                    s_gd.wait(v, gd_yf[h - 2])
                s_v.chain(v, v.scalar_tensor_tensor(
                    out=ytile[:, h % 2, :], in0=xcr[:, h % 2, :],
                    scalar=dpar_c[:, dt:dt + 1],
                    in1=ytile[:, h % 2, :], op0=ALU.mult, op1=ALU.add))
                s_ctl.inc(v.tensor_mul(yf_t[:, h % 2, :], ytile[:, h % 2, :],
                                       sz_t[:, h % 2, :]))

            @blk.gpsimd
            def _(gp, bs=bs, dt=dt, h=h):
                s_ctl.wait(gp, ctl0 + h + 1)
                s_gd.dma(gp.dma_start(out=yfD[bs, dt * 128:(dt + 1) * 128, :],
                                      in_=yf_t[:, h % 2, :]))
                gd_yf[h] = s_gd.n

    for bs in range(NBS):
        for rnd in range(2):
            for dt in range(NDT):
                @blk.sync
                def _(s, bs=bs, rnd=rnd, dt=dt):
                    s_t.wait(s)
                    if bs == 0 and rnd == 0 and dt == 0:
                        s_gd.wait(s)
                    s_ld.dma(s.dma_start(out=yf_t[:, dt % 2, :],
                                         in_=yfD[bs, dt * 128:(dt + 1) * 128, :]))

                @blk.tensor
                def _(t, bs=bs, rnd=rnd, dt=dt):
                    s_ld.wait(t)
                    if dt == 0:
                        s_v.wait(t)
                    for dm4 in range(4):
                        dm = rnd * 4 + dm4
                        ins = t.matmul(ps_o[:, dm4, :], woT[:, dt, 128 * dm:128 * (dm + 1)],
                                       yf_t[:, dt % 2, :], start=(dt == 0),
                                       stop=(dt == NDT - 1))
                    s_t.inc(ins)

            @blk.vector
            def _(v, bs=bs, rnd=rnd):
                s_t.wait(v)
                for dm4 in range(4):
                    ins = v.tensor_copy(outT[:, rnd * 4 + dm4, :], ps_o[:, dm4, :])
                s_v.inc(ins)

        for ti in range(4):
            @blk.sync
            def _(s, bs=bs, ti=ti):
                s_out.wait(s)
                row0 = bs * SL + ti * 128
                s_ld.dma(s.dma_start(out=xt2[:], in_=x_in[row0:row0 + 128, :]))

            for half in range(2):
                @blk.tensor
                def _(t, ti=ti, half=half):
                    s_v.wait(t)
                    for i4 in range(4):
                        i = half * 4 + i4
                        ins = t.transpose(ps_tf[:, 128 * i4:128 * (i4 + 1)] if False
                                          else ps_t[:, 128 * i4:128 * (i4 + 1)],
                                          outT[:, i, 128 * ti:128 * (ti + 1)], idb[:])
                    s_t.inc(ins)

                @blk.vector
                def _(v, ti=ti, half=half):
                    s_t.wait(v)
                    s_ld.wait(v)
                    for i4 in range(4):
                        i = half * 4 + i4
                        ins = v.tensor_add(xt2[:, 128 * i:128 * (i + 1)],
                                           xt2[:, 128 * i:128 * (i + 1)],
                                           ps_t[:, 128 * i4:128 * (i4 + 1)])
                    s_v.inc(ins)

            @blk.scalar
            def _(sc, bs=bs, ti=ti):
                s_v.wait(sc)
                row0 = bs * SL + ti * 128
                s_out.dma(sc.dma_start(out=y_out[row0:row0 + 128, :], in_=xt2[:]))

    if ext:
        return

    @blk.gpsimd
    def _(g):
        s_out.wait(g)

    return p.finish()


_L1ATTN_NC = None


def run_l1attn(xshards, W_in, conv_w, conv_b, W_x, w_dt, b_dt, A_log, D_param, W_out):
    global _L1ATTN_NC
    if _L1ATTN_NC is None:
        _L1ATTN_NC = build_l1attn()
    in_maps = []
    for c in range(NC):
        cbidx, smask = l1attn_tables(c)
        in_maps.append(dict(
            x=xshards[c], w_in=W_in, conv_w=conv_w.reshape(DIN, 4),
            conv_b=conv_b.reshape(1, DIN), w_x=W_x, w_dt=w_dt.reshape(1, DIN),
            b_dt=b_dt.reshape(1, DIN), a_log=A_log, d_par=D_param.reshape(1, DIN),
            w_out=W_out, cbidx=cbidx, smask=smask))
    res = run_bass_kernel_spmd(_L1ATTN_NC, in_maps, core_ids=list(range(NC)))
    return [np.asarray(res.results[c]["y_ma"]) for c in range(NC)]



NE = 8
KSEL = 8192
FWIN = 256
NTOK = NC * TPC
FF = 4096


def l1ffn_tables(core):
    p = np.arange(128)
    myslot_g = (core * 128 + p).astype(np.int32).reshape(128, 1)
    mycomb = np.zeros((128, NE), dtype=np.int32)
    for e in range(NE):
        mycomb[:, e] = e * 1024 + core * 128 + p
    return myslot_g, mycomb


def build_l1ffn(debug=False, ext=None):
    p = ext["p"] if ext else Prog()
    nc = p.nc

    x_in = ext["x_in"] if ext else p.dram("x", [TPC, D], F32, kind="ExternalInput")
    wg = p.dram("wg", [D, NE], F32, kind="ExternalInput")
    wg_my = p.dram("wg_my", [D, 1], F32, kind="ExternalInput")
    w1 = p.dram("w1", [D, FF], F32, kind="ExternalInput")
    w2 = p.dram("w2", [FF, D], F32, kind="ExternalInput")
    b1 = p.dram("b1", [1, FF], F32, kind="ExternalInput")
    b2 = p.dram("b2", [1, D], F32, kind="ExternalInput")
    msg_idx = p.dram("msg_idx", [128, 1], I32, kind="ExternalInput")
    mc_idx = p.dram("mc_idx", [128, NE], I32, kind="ExternalInput")
    y_out = ext["y_out"] if ext else p.dram("y_mo", [TPC, D], F32, kind="ExternalOutput")

    lnD = p.dram("lnD", [NBS, 1 + SL, D], BF16)
    lnfD = p.dram("lnfD", [TPC, D], F32)
    ln_ag_in = p.dram("ln_ag_in", [TPC, D], BF16)
    agLN = p.dram("agLN", [NTOK, D], BF16)
    s_ag_in = p.dram("s_ag_in", [TPC, NE], F32)
    agS = p.dram("agS", [NTOK, NE], F32)
    slotD = p.dram("slotD", [NE, NTOK], I32)
    myslotD = p.dram("myslotD", [NTOK], I32)
    myslotC = p.dram("myslotC", [NE, TPC], I32)
    XeD = p.dram("XeD", [KSEL, D], BF16)
    y_ag_in = p.dram("y_ag_in", [KSEL, D], BF16)
    agY = p.dram("agY", [NE * KSEL, D], BF16)

    if debug:
        dbg_slot = p.dram("dbg_slot", [NE, NTOK], I32, kind="ExternalOutput")
        dbg_s = p.dram("dbg_s", [NTOK, NE], F32, kind="ExternalOutput")
        dbg_xe = p.dram("dbg_xe", [KSEL, D], BF16, kind="ExternalOutput")
        dbg_yag = p.dram("dbg_yag", [KSEL, D], F32, kind="ExternalOutput")
        dbg_msl = p.dram("dbg_msl", [NTOK], I32, kind="ExternalOutput")
        dbg_bis = p.dram("dbg_bis", [1, 8 * NE], F32, kind="ExternalOutput")
        dbg_hib = p.dram("dbg_hib", [128, NE], F32, kind="ExternalOutput")
        dbg_cntp = p.dram("dbg_cntp", [128, NE], F32, kind="ExternalOutput")
        dbg_selg = p.dram("dbg_selg", [128, 256], F32, kind="ExternalOutput")
        dbg_slotf = p.dram("dbg_slotf", [128, 256], F32, kind="ExternalOutput")
        dbg_pcs = p.dram("dbg_pcs", [128, 256], F32, kind="ExternalOutput")
        dbg_sloti = p.dram("dbg_sloti", [128, 256], I32, kind="ExternalOutput")
    xt = p.sb("xt", [128, 2, D], F32)
    sq = p.sb("sq", [128, 2, D], F32)
    stat = p.sb("stat", [128, 8], F32)
    ln16 = p.sb("ln16", [128, 2, D], BF16)
    wgsb = p.sb("wgsb", [128, 8, NE], F32)
    wgmy = p.sb("wgmy", [128, 8, 1], BF16)
    b1c = p.sb("b1c", [128, 32], F32)
    b2c = p.sb("b2c", [128, 8], F32)
    msg_sb = p.sb("msg_sb", [128, 1], I32)
    mc_sb = p.sb("mc_sb", [128, NE], I32)
    bis = p.sb("bis", [1, 8 * NE], F32)
    bisd = p.sb("bisd", [1, NE], I32)
    cntp = p.sb("cntp", [128, NE], F32)
    midb = p.sb("midb", [128, NE], F32)
    lob = p.sb("lob", [128, NE], F32)
    hib = p.sb("hib", [128, NE], F32)
    ones1 = p.sb("ones1", [128, 1], F32)
    onesc = p.sb("onesc", [1, 128], F32)
    lstrict = p.sb("lstrict", [128, 128], F32)
    tots = p.sb("tots", [128, 1], F32)
    excl = p.sb("excl", [128, 1], F32)
    rneed_b = p.sb("rneed_b", [128, 1], F32)
    idxcol = p.sb("idxcol", [128, 2, 1], I32)
    xbl = p.sb("xbl", [128, 2, D], BF16)
    acc = p.sb("acc", [128, 2, D], BF16)
    xt2 = p.sb("xt2", [128, 2, D], F32)
    idb = p.sb("idb", [128, 128], BF16)
    idf = p.sb("idf", [128, 128], F32)

    ga = ExitStack()

    def gA(name, shape, dtype):
        return p.sb_into(ga, name, shape, dtype)

    lnf = gA("lnf", [128, 2, D], F32)
    lnTwf = gA("lnTwf", [128, 8, SL], F32)
    stile = gA("stile", [8, 2, SL], F32)
    stok = gA("stok", [128, 2, 8], F32)
    ssb = gA("ssb", [128, 256, NE], F32)
    junk = gA("junk", [128, 256], F32)
    onesr = gA("onesr", [128, 256], F32)
    huget = gA("huget", [128, 256], F32)
    selg = gA("selg", [128, 256], F32)
    tiem = gA("tiem", [128, 256], F32)
    pcs = gA("pcs", [128, 256], F32)
    sranks = gA("sranks", [128, 256], F32)
    slotf = gA("slotf", [128, 256], F32)
    sloti = gA("sloti", [128, 256], I32)
    seli = gA("seli", [128, 256], I32)
    msg_cols = gA("msg_cols", [128, 256], I32)

    ps_t = p.ps("ps_t", [128, 512], BF16)
    ps_tf = p.ps("ps_tf", [128, 512], F32)
    ps_m = p.ps("ps_m", [128, 512], F32)
    ps_m2 = p.ps("ps_m2", [128, 512], F32)
    ps_m3 = p.ps("ps_m3", [128, 512], F32)

    s_ld = p.sem("s_ld")
    s_st = p.sem("s_st")
    s_g = p.sem("s_g")
    s_gd = p.sem("s_gd")
    s_v = p.sem("s_v")
    s_a = p.sem("s_a")
    s_t = p.sem("s_t")
    s_cc = p.sem("s_cc")
    s_out = p.sem("s_out")

    blk = p.block()
    emit_identities(blk, s_g, idb, idf)

    @blk.gpsimd
    def _(g):
        s_g.chain(g, g.memset(lstrict[:], 0.0))
        s_g.inc(g.affine_select(out=lstrict[:], in_=lstrict[:],
                                compare_op=ALU.is_ge, fill=1.0, base=0,
                                pattern=[[-1, 128]], channel_multiplier=1))
        for j in range(8):
            s_gd.dma(g.dma_start(out=wgmy[:, j, :], in_=wg_my[128 * j:128 * (j + 1), :]))

    @blk.sync
    def _(s):
        s_ld.dma(s.dma_start(out=msg_sb[:], in_=msg_idx))
        s_ld.dma(s.dma_start(out=mc_sb[:], in_=mc_idx))
        for j in range(8):
            s_ld.dma(s.dma_start(out=wgsb[:, j, :], in_=wg[128 * j:128 * (j + 1), :]))
        for j in range(32):
            load_col(s, s_ld, b1c[:, j:j + 1], b1[0:1, 128 * j:128 * (j + 1)])
        for j in range(8):
            load_col(s, s_ld, b2c[:, j:j + 1], b2[0:1, 128 * j:128 * (j + 1)])

    @blk.vector
    def _(v):
        v.memset(ones1[:], 1.0)
        v.memset(onesc[:], 1.0)
        v.memset(onesr[:], 1.0)
        s_v.inc(v.memset(huget[:], float(HUGE)))

    for k in range(TPC // 128):
        bs, ti = divmod(k, 4)
        pb = k % 2

        @blk.sync
        def _(s, k=k, pb=pb):
            s_v.wait(s)
            s_ld.dma(s.dma_start(out=xt[:, pb, :], in_=x_in[k * 128:(k + 1) * 128, :]))

        @blk.vector
        def _(v, pb=pb):
            s_ld.wait(v)
            s_a.wait(v)
            s_v.chain(v, v.tensor_reduce(out=stat[:, 0:1], in_=xt[:, pb, :],
                                         axis=AX.X, op=ALU.add))
            s_v.chain(v, v.tensor_scalar_mul(stat[:, 2:3], stat[:, 0:1], 1.0 / D))
            s_v.inc(v.tensor_scalar(out=sq[:, pb, :], in0=xt[:, pb, :],
                                    scalar1=stat[:, 2:3], scalar2=None,
                                    op0=ALU.subtract))

        @blk.scalar
        def _(sc, pb=pb):
            s_v.wait(sc)
            s_a.inc(sc.activation(out=sq[:, pb, :], in_=sq[:, pb, :], func=AF.Square,
                                  accum_out=stat[:, 1:2]))

        @blk.vector
        def _(v):
            s_a.wait(v)
            s_v.inc(v.tensor_scalar(out=stat[:, 3:4], in0=stat[:, 1:2],
                                    scalar1=1.0 / D, scalar2=EPS,
                                    op0=ALU.mult, op1=ALU.add))

        @blk.scalar
        def _(sc):
            s_v.wait(sc)
            s_a.inc(sc.activation(out=stat[:, 4:5], in_=stat[:, 3:4], func=AF.Sqrt))

        @blk.vector
        def _(v, pb=pb):
            s_a.wait(v)
            s_st.wait(v)
            s_v.chain(v, v.reciprocal(stat[:, 5:6], stat[:, 4:5]))
            s_v.chain(v, v.tensor_scalar(out=sq[:, pb, :], in0=xt[:, pb, :],
                                         scalar1=stat[:, 2:3], scalar2=None,
                                         op0=ALU.subtract))
            s_v.chain(v, v.tensor_scalar(out=lnf[:, pb, :], in0=sq[:, pb, :],
                                         scalar1=stat[:, 5:6], scalar2=None,
                                         op0=ALU.mult))
            s_v.inc(v.tensor_copy(ln16[:, pb, :], lnf[:, pb, :]))

        @blk.scalar
        def _(sc, k=k, pb=pb):
            s_v.wait(sc)
            s_st.dma(sc.dma_start(out=ln_ag_in[k * 128:(k + 1) * 128, :],
                                  in_=ln16[:, pb, :]))
            s_st.dma(sc.dma_start(out=lnfD[k * 128:(k + 1) * 128, :],
                                  in_=lnf[:, pb, :]))

    for bs in range(NBS):
        for ti in range(4):
            @blk.sync
            def _(s, bs=bs, ti=ti):
                s_v.wait(s)
                s_a.wait(s)
                if bs == 0 and ti == 0:
                    s_st.wait(s)
                row0 = bs * SL + ti * 128
                s_ld.dma(s.dma_start(out=lnf[:, 0, :], in_=lnfD[row0:row0 + 128, :]))

            for jj in range(2):
                @blk.tensor
                def _(t, jj=jj):
                    s_ld.wait(t)
                    s_v.wait(t)
                    s_g.wait(t)
                    for j4 in range(4):
                        j = jj * 4 + j4
                        ins = t.transpose(ps_tf[:, 128 * j4:128 * (j4 + 1)],
                                          lnf[:, 0, 128 * j:128 * (j + 1)], idf[:])
                    s_t.inc(ins)

                @blk.vector
                def _(v, ti=ti, jj=jj):
                    s_t.wait(v)
                    for j4 in range(4):
                        j = jj * 4 + j4
                        ins = v.tensor_copy(lnTwf[:, j, 128 * ti:128 * (ti + 1)],
                                            ps_tf[:, 128 * j4:128 * (j4 + 1)])
                    s_v.inc(ins)

        @blk.tensor
        def _(t, bs=bs):
            s_v.wait(t)
            for j in range(8):
                ins = t.matmul(ps_m[0:8, :], wgsb[:, j, :], lnTwf[:, j, :],
                               start=(j == 0), stop=(j == 7))
            s_t.inc(ins)

        @blk.vector
        def _(v, bs=bs):
            s_t.wait(v)
            s_a.wait(v)
            s_v.inc(v.tensor_copy(stile[:, bs % 2, :], ps_m[0:8, :]))

        for ti in range(4):
            @blk.tensor
            def _(t, bs=bs, ti=ti):
                s_v.wait(t)
                s_a.wait(t)
                ins = t.transpose(ps_tf[:, 0:8],
                                  stile[0:8, bs % 2, 128 * ti:128 * (ti + 1)],
                                  idf[0:8, 0:8])
                s_t.inc(ins)

            @blk.vector
            def _(v, bs=bs, ti=ti):
                s_t.wait(v)
                s_st.wait(v)
                s_v.inc(v.tensor_copy(stok[:, ti % 2, :], ps_tf[:, 0:8]))

            @blk.scalar
            def _(sc, bs=bs, ti=ti):
                s_v.wait(sc)
                row0 = bs * SL + ti * 128
                s_st.dma(sc.dma_start(out=s_ag_in[row0:row0 + 128, :],
                                      in_=stok[:, ti % 2, :]))

    @blk.gpsimd
    def _(g):
        s_st.wait(g)
        s_cc.inc(g.collective_compute("AllGather", ALU.bypass,
                                      replica_groups=[list(range(NC))],
                                      ins=[ln_ag_in], outs=[agLN]))
        s_cc.inc(g.collective_compute("AllGather", ALU.bypass,
                                      replica_groups=[list(range(NC))],
                                      ins=[s_ag_in], outs=[agS]))
        s_cc.wait(g)

    @blk.sync
    def _(s):
        s_cc.wait(s)
        s_ld.dma(s.dma_start(out=ssb[:],
                             in_=agS.rearrange("(p r) e -> p r e", p=128)))

    BIS_ITERS = 34
    def bcol(name, e):
        i = {"lo": 0, "hi": 1, "mid": 2, "cnt": 3, "dir": 4, "rneed": 5,
             "c1": 6}[name]
        return bis[0:1, i * NE + e:i * NE + e + 1]

    @blk.vector
    def _(v):
        s_ld.wait(v)
        v.memset(bis[0:1, 0 * NE:1 * NE], -100.0)
        s_v.inc(v.memset(bis[0:1, 1 * NE:2 * NE], 100.0))

    for it in range(BIS_ITERS):
        @blk.vector
        def _(v, it=it):
            if it > 0:
                s_t.wait(v)
                s_v.chain(v, v.tensor_copy(bis[0:1, 3 * NE:4 * NE], ps_m[0:1, 0:NE]))
                s_v.chain(v, v.tensor_scalar(out=bisd[0:1, :],
                                             in0=bis[0:1, 3 * NE:4 * NE],
                                             scalar1=float(KSEL), scalar2=None,
                                             op0=ALU.is_ge))
                s_v.chain(v, v.select(bis[0:1, 0 * NE:1 * NE], bisd[0:1, :],
                                      bis[0:1, 2 * NE:3 * NE], bis[0:1, 0 * NE:1 * NE]))
                s_v.chain(v, v.select(bis[0:1, 1 * NE:2 * NE], bisd[0:1, :],
                                      bis[0:1, 1 * NE:2 * NE], bis[0:1, 2 * NE:3 * NE]))
            s_v.chain(v, v.tensor_add(bis[0:1, 2 * NE:3 * NE], bis[0:1, 0 * NE:1 * NE],
                                      bis[0:1, 1 * NE:2 * NE]))
            s_v.inc(v.tensor_scalar_mul(bis[0:1, 2 * NE:3 * NE],
                                        bis[0:1, 2 * NE:3 * NE], 0.5))

        @blk.tensor
        def _(t, it=it):
            s_v.wait(t)
            ins = t.matmul(ps_m2[:, 0:NE], onesc[:, :], bis[0:1, 2 * NE:3 * NE],
                           start=True, stop=True)
            s_t.inc(ins)

        @blk.vector
        def _(v, it=it):
            s_t.wait(v)
            s_v.chain(v, v.tensor_copy(midb[:], ps_m2[:, 0:NE]))
            for e in range(NE):
                s_v.chain(v, v.tensor_scalar(
                    out=junk[:], in0=ssb[:, :, e], scalar1=midb[:, e:e + 1],
                    scalar2=0.0, op0=ALU.is_gt, op1=ALU.add,
                    accum_out=cntp[:, e:e + 1]))
            s_v.inc(v.engine_nop())

        @blk.tensor
        def _(t, it=it):
            s_v.wait(t)
            ins = t.matmul(ps_m[0:1, 0:NE], ones1[:, 0:1],
                           cntp[:, :], start=True, stop=True)
            s_t.inc(ins)

    @blk.vector
    def _(v):
        s_t.wait(v)
        s_v.chain(v, v.tensor_copy(bis[0:1, 3 * NE:4 * NE], ps_m[0:1, 0:NE]))
        s_v.chain(v, v.tensor_scalar(out=bisd[0:1, :],
                                     in0=bis[0:1, 3 * NE:4 * NE],
                                     scalar1=float(KSEL), scalar2=None,
                                     op0=ALU.is_ge))
        s_v.chain(v, v.select(bis[0:1, 0 * NE:1 * NE], bisd[0:1, :],
                              bis[0:1, 2 * NE:3 * NE], bis[0:1, 0 * NE:1 * NE]))
        s_v.inc(v.select(bis[0:1, 1 * NE:2 * NE], bisd[0:1, :],
                         bis[0:1, 1 * NE:2 * NE], bis[0:1, 2 * NE:3 * NE]))

    @blk.tensor
    def _(t):
        s_v.wait(t)
        ins = t.matmul(ps_m2[:, 0:NE], onesc[:, :], bis[0:1, 0 * NE:1 * NE],
                       start=True, stop=True)
        s_t.inc(ins)
        ins = t.matmul(ps_tf[:, 0:NE], onesc[:, :], bis[0:1, 1 * NE:2 * NE],
                       start=True, stop=True)
        s_t.inc(ins)

    @blk.vector
    def _(v):
        s_t.wait(v)
        v.tensor_copy(lob[:], ps_m2[:, 0:NE])
        s_v.chain(v, v.tensor_copy(hib[:], ps_tf[:, 0:NE]))
        for e in range(NE):
            s_v.chain(v, v.tensor_scalar(
                out=junk[:], in0=ssb[:, :, e], scalar1=hib[:, e:e + 1],
                scalar2=0.0, op0=ALU.is_gt, op1=ALU.add,
                accum_out=cntp[:, e:e + 1]))
        s_v.inc(v.engine_nop())

    @blk.tensor
    def _(t):
        s_v.wait(t)
        ins = t.matmul(ps_m[0:1, 0:NE], ones1[:, 0:1], cntp[:, :],
                       start=True, stop=True)
        s_t.inc(ins)

    @blk.vector
    def _(v):
        s_t.wait(v)
        s_v.chain(v, v.tensor_copy(bis[0:1, 6 * NE:7 * NE], ps_m[0:1, 0:NE]))
        s_v.chain(v, v.tensor_scalar(out=bis[0:1, 5 * NE:6 * NE],
                                     in0=bis[0:1, 6 * NE:7 * NE],
                                     scalar1=-1.0, scalar2=float(KSEL),
                                     op0=ALU.mult, op1=ALU.add))
        s_v.inc(v.engine_nop())

    for e in range(NE):
        @blk.tensor
        def _(t, e=e):
            s_v.wait(t)
            ins = t.matmul(ps_m2[:, 0:1], onesc[:, :], bcol("rneed", e),
                           start=True, stop=True)
            s_t.inc(ins)

        @blk.vector
        def _(v, e=e):
            s_t.wait(v)
            s_v.chain(v, v.tensor_copy(rneed_b[:], ps_m2[:, 0:1]))
            s_v.chain(v, v.tensor_scalar(out=selg[:], in0=ssb[:, :, e],
                                         scalar1=hib[:, e:e + 1], scalar2=None,
                                         op0=ALU.is_gt))
            s_v.chain(v, v.tensor_scalar(out=tiem[:], in0=ssb[:, :, e],
                                         scalar1=lob[:, e:e + 1], scalar2=None,
                                         op0=ALU.is_gt))
            s_v.chain(v, v.tensor_sub(tiem[:], tiem[:], selg[:]))
            s_v.chain(v, v.tensor_tensor_scan(out=pcs[:], data0=onesr[:],
                                              data1=tiem[:], initial=0.0,
                                              op0=ALU.mult, op1=ALU.add))
            s_v.inc(v.tensor_copy(tots[:], pcs[:, 255:256]))

        @blk.tensor
        def _(t, e=e):
            s_v.wait(t)
            ins = t.matmul(ps_m2[:, 0:1], lstrict[:, :], tots[:, 0:1],
                           start=True, stop=True)
            s_t.inc(ins)

        @blk.vector
        def _(v, e=e):
            s_t.wait(v)
            s_v.chain(v, v.tensor_copy(excl[:], ps_m2[:, 0:1]))
            s_v.chain(v, v.tensor_scalar(out=sranks[:], in0=pcs[:],
                                         scalar1=excl[:, 0:1], scalar2=-1.0,
                                         op0=ALU.add, op1=ALU.add))
            s_v.chain(v, v.tensor_scalar(out=sranks[:], in0=sranks[:],
                                         scalar1=rneed_b[:, 0:1], scalar2=None,
                                         op0=ALU.is_lt))
            s_v.chain(v, v.tensor_mul(sranks[:], sranks[:], tiem[:]))
            s_v.chain(v, v.tensor_add(selg[:], selg[:], sranks[:]))
            s_v.chain(v, v.tensor_tensor_scan(out=pcs[:], data0=onesr[:],
                                              data1=selg[:], initial=0.0,
                                              op0=ALU.mult, op1=ALU.add))
            s_v.inc(v.tensor_copy(tots[:], pcs[:, 255:256]))

        @blk.tensor
        def _(t, e=e):
            s_v.wait(t)
            ins = t.matmul(ps_m2[:, 0:1], lstrict[:, :], tots[:, 0:1],
                           start=True, stop=True)
            s_t.inc(ins)

        @blk.vector
        def _(v, e=e):
            s_t.wait(v)
            s_v.chain(v, v.tensor_copy(excl[:], ps_m2[:, 0:1]))
            s_v.chain(v, v.tensor_scalar(out=slotf[:], in0=pcs[:],
                                         scalar1=excl[:, 0:1], scalar2=-1.0,
                                         op0=ALU.add, op1=ALU.add))
            s_v.chain(v, v.tensor_scalar(out=sranks[:], in0=slotf[:],
                                         scalar1=-float(HUGE), scalar2=None,
                                         op0=ALU.add))
            s_v.chain(v, v.tensor_mul(sranks[:], sranks[:], selg[:]))
            s_v.chain(v, v.tensor_scalar(out=slotf[:], in0=sranks[:],
                                         scalar1=float(HUGE), scalar2=None,
                                         op0=ALU.add))
            s_v.inc(v.tensor_copy(sloti[:], slotf[:]))

        @blk.scalar
        def _(sc, e=e):
            s_v.wait(sc)
            s_st.dma(sc.dma_start(out=slotD[e].rearrange("(p r) -> p r", p=128),
                                  in_=sloti[:]))

    @blk.gpsimd
    def _(g):
        s_st.wait(g)
        s_ld.wait(g)
        breg = g.to_reg(NE * 128 - 1)
        bregs2 = g.to_reg(NE * 1024 - 1)
        s_gd.dma(g.indirect_dma_start(
            out=msg_cols[:], out_offset=None,
            in_=slotD.rearrange("a b -> (a b)").unsqueeze(1) if False
            else slotD.rearrange("e (q c) -> (e q) c", c=256),
            in_offset=IndirectOffsetOnAxis(msg_sb[:, 0:1], axis=0),
            bounds_check=breg, oob_is_err=False))
        s_gd.wait(g)
        s_gd.dma(g.dma_start(out=myslotD.rearrange("(p r) -> p r", p=128),
                             in_=msg_cols[:]))
        for e in range(NE):
            s_gd.dma(g.indirect_dma_start(
                out=msg_cols[:, 0:32], out_offset=None,
                in_=slotD.rearrange("e (q c) -> (e q) c", c=32),
                in_offset=IndirectOffsetOnAxis(mc_sb[:, e:e + 1], axis=0),
                bounds_check=bregs2, oob_is_err=False))
            s_gd.wait(g)
            s_gd.dma(g.dma_start(out=myslotC[e].rearrange("(p r) -> p r", p=128),
                                 in_=msg_cols[:, 0:32]))
            s_gd.wait(g)

    @blk.gpsimd
    def _(g):
        breg3 = g.to_reg(KSEL - 1)
        loads = []
        for pb2 in range(NTOK // 128):
            if pb2 < 2:
                s_gd.dma(g.dma_start(
                    out=idxcol[:, pb2 % 2, :],
                    in_=myslotD[pb2 * 128:(pb2 + 1) * 128].unsqueeze(1)))
                loads.append(s_gd.n)
        for pb2 in range(NTOK // 128):
            s_gd.dma(g.dma_start(out=xbl[:, pb2 % 2, :],
                                 in_=agLN[pb2 * 128:(pb2 + 1) * 128, :]))
            g.wait_ge(s_gd.h, s_gd.n)
            s_gd.dma(g.indirect_dma_start(
                out=XeD, out_offset=IndirectOffsetOnAxis(idxcol[:, pb2 % 2, :], axis=0),
                in_=xbl[:, pb2 % 2, :], in_offset=None,
                bounds_check=breg3, oob_is_err=False))
            nxt = pb2 + 2
            if nxt < NTOK // 128:
                s_gd.dma(g.dma_start(
                    out=idxcol[:, nxt % 2, :],
                    in_=myslotD[nxt * 128:(nxt + 1) * 128].unsqueeze(1)))
                loads.append(s_gd.n)
        s_gd.wait(g)

    if debug:
        @blk.gpsimd
        def _(g):
            s_out.wait(g) if False else None
            s_gd.dma(g.dma_start(out=dbg_slot, in_=slotD))
            s_gd.dma(g.dma_start(out=dbg_s, in_=agS))
            s_gd.dma(g.dma_start(out=dbg_xe, in_=XeD))
            s_gd.dma(g.dma_start(out=dbg_msl, in_=myslotD))
            s_gd.dma(g.dma_start(out=dbg_bis, in_=bis[:]))
            s_gd.dma(g.dma_start(out=dbg_hib, in_=hib[:]))
            s_gd.dma(g.dma_start(out=dbg_cntp, in_=cntp[:]))
            s_gd.dma(g.dma_start(out=dbg_selg, in_=selg[:]))
            s_gd.dma(g.dma_start(out=dbg_slotf, in_=slotf[:]))
            s_gd.dma(g.dma_start(out=dbg_pcs, in_=pcs[:]))
            s_gd.dma(g.dma_start(out=dbg_sloti, in_=sloti[:]))
            s_gd.wait(g)
        ga.close()
        return p.finish()

    ga.close()

    w1sb = p.sb("w1sb", [128, 8, FF], BF16)
    w2sb = p.sb("w2sb", [128, 32, D], BF16)
    xeT = p.sb("xeT", [128, 8, FWIN], BF16)
    xet_tok = p.sb("xet_tok", [128, 2, D], BF16)
    hidT = p.sb("hidT", [128, 32, FWIN], BF16)
    grow = p.sb("grow", [1, FWIN], F32)
    gbs = p.sb("gbs", [128, FWIN], F32)
    otmp = p.sb("otmp", [128, FWIN], F32)
    oT = p.sb("oT", [128, 8, FWIN], BF16)
    otok = p.sb("otok", [128, 2, D], BF16)

    @blk.gpsimd
    def _(g):
        for j in range(8):
            s_gd.dma(g.dma_start(out=w1sb[:, j, :], in_=w1[128 * j:128 * (j + 1), :]))
        for j in range(32):
            s_gd.dma(g.dma_start(out=w2sb[:, j, :], in_=w2[128 * j:128 * (j + 1), :]))

    for w in range(KSEL // FWIN):
        for ti in range(2):
            @blk.sync
            def _(s, w=w, ti=ti):
                s_v.wait(s)
                s_a.wait(s)
                if w == 0 and ti == 0:
                    s_gd.wait(s)
                row0 = w * FWIN + ti * 128
                s_ld.dma(s.dma_start(out=xet_tok[:, ti % 2, :],
                                     in_=XeD[row0:row0 + 128, :]))

            for jj in range(2):
                @blk.tensor
                def _(t, ti=ti, jj=jj):
                    s_ld.wait(t)
                    s_v.wait(t)
                    s_g.wait(t)
                    for j4 in range(4):
                        j = jj * 4 + j4
                        ins = t.transpose(ps_t[:, 128 * j4:128 * (j4 + 1)],
                                          xet_tok[:, ti % 2, 128 * j:128 * (j + 1)],
                                          idb[:])
                    s_t.inc(ins)

                @blk.vector
                def _(v, ti=ti, jj=jj):
                    s_t.wait(v)
                    for j4 in range(4):
                        j = jj * 4 + j4
                        ins = v.tensor_copy(xeT[:, j, 128 * ti:128 * (ti + 1)],
                                            ps_t[:, 128 * j4:128 * (j4 + 1)])
                    s_v.inc(ins)

        @blk.tensor
        def _(t, w=w):
            s_v.wait(t)
            s_gd.wait(t)
            for j in range(8):
                ins = t.matmul(ps_m2[0:1, 0:FWIN], wgmy[:, j, :], xeT[:, j, :],
                               start=(j == 0), stop=(j == 7))
            s_t.inc(ins)

        @blk.scalar
        def _(sc, w=w):
            s_t.wait(sc)
            s_a.inc(sc.activation(out=grow[:], in_=ps_m2[0:1, 0:FWIN], func=AF.Sigmoid))

        a_hid0 = s_a.n

        @blk.tensor
        def _(t, w=w):
            s_a.wait(t, a_hid0)
            ins = t.matmul(ps_m2[:, 0:FWIN], onesc[:, :], grow[:, :],
                           start=True, stop=True)
            s_t.inc(ins)

        @blk.vector
        def _(v, w=w):
            s_t.wait(v)
            s_v.inc(v.tensor_copy(gbs[:], ps_m2[:, 0:FWIN]))

        v_w2 = s_v.n

        for fb in range(32):
            pm = ps_m if fb % 2 == 0 else ps_m3

            @blk.tensor
            def _(t, w=w, fb=fb, pm=pm):
                s_a.wait(t, a_hid0 + fb - 1 if fb >= 2 else a_hid0 - 1)
                for j in range(8):
                    ins = t.matmul(pm[:, 0:FWIN], w1sb[:, j, 128 * fb:128 * (fb + 1)],
                                   xeT[:, j, :], start=(j == 0), stop=(j == 7))
                s_t.inc(ins)

            @blk.scalar
            def _(sc, w=w, fb=fb, pm=pm):
                s_t.wait(sc)
                s_a.inc(sc.activation(out=hidT[:, fb, :], in_=pm[:, 0:FWIN],
                                      func=AF.Gelu, bias=b1c[:, fb:fb + 1]))

        a_w2 = s_a.n

        for hb in range(8):
            pm = ps_m if hb % 2 == 0 else ps_m3

            @blk.tensor
            def _(t, w=w, hb=hb, pm=pm):
                if hb == 0:
                    s_a.wait(t, a_w2)
                    s_v.wait(t, v_w2)
                elif hb >= 2:
                    s_v.wait(t, v_w2 + 2 * (hb - 1))
                for j in range(32):
                    ins = t.matmul(pm[:, 0:FWIN], w2sb[:, j, 128 * hb:128 * (hb + 1)],
                                   hidT[:, j, :], start=(j == 0), stop=(j == 31))
                s_t.inc(ins)

            @blk.vector
            def _(v, w=w, hb=hb, pm=pm):
                s_t.wait(v)
                s_v.chain(v, v.tensor_scalar(out=otmp[:], in0=pm[:, 0:FWIN],
                                             scalar1=b2c[:, hb:hb + 1], scalar2=None,
                                             op0=ALU.add))
                s_v.inc(v.tensor_mul(oT[:, hb, :], otmp[:], gbs[:]))

        for ti in range(2):
            for half in range(2):
                @blk.tensor
                def _(t, w=w, ti=ti, half=half):
                    s_v.wait(t)
                    s_gd.wait(t)
                    for i4 in range(4):
                        i = half * 4 + i4
                        ins = t.transpose(ps_t[:, 128 * i4:128 * (i4 + 1)],
                                          oT[:, i, 128 * ti:128 * (ti + 1)], idb[:])
                    s_t.inc(ins)

                @blk.vector
                def _(v, ti=ti, half=half):
                    s_t.wait(v)
                    for i4 in range(4):
                        i = half * 4 + i4
                        ins = v.tensor_copy(otok[:, ti % 2, 128 * i:128 * (i + 1)],
                                            ps_t[:, 128 * i4:128 * (i4 + 1)])
                    s_v.inc(ins)

            @blk.gpsimd
            def _(g, w=w, ti=ti):
                s_v.wait(g)
                row0 = w * FWIN + ti * 128
                s_gd.dma(g.dma_start(out=y_ag_in[row0:row0 + 128, :],
                                     in_=otok[:, ti % 2, :]))

    @blk.gpsimd
    def _(g):
        s_st.wait(g)
        s_cc.inc(g.collective_compute("AllGather", ALU.bypass,
                                      replica_groups=[list(range(NC))],
                                      ins=[y_ag_in], outs=[agY]))
        s_cc.wait(g)

    bregs_c = {}

    @blk.gpsimd
    def _(g):
        s_cc.wait(g)
        bregs_c["y"] = g.to_reg(NE * KSEL - 1)

    for k in range(TPC // 128):
        pb = k % 2

        @blk.gpsimd
        def _(g, k=k, pb=pb):
            s_v.wait(g)
            s_g.chain(g, g.memset(acc[:, pb, :], 0.0))
            for e in range(NE):
                s_gd.dma(g.dma_start(out=idxcol[:, pb, :],
                                     in_=myslotC[e, k * 128:(k + 1) * 128].unsqueeze(1)))
                s_gd.wait(g)
                s_g.chain(g, g.tensor_single_scalar(out=idxcol[:, pb, :],
                                                    in_=idxcol[:, pb, :],
                                                    scalar=e * KSEL, op=ALU.add))
                s_gd.dma(g.indirect_dma_start(
                    out=acc[:, pb, :], out_offset=None, in_=agY,
                    in_offset=IndirectOffsetOnAxis(idxcol[:, pb, :], axis=0),
                    bounds_check=bregs_c["y"], oob_is_err=False,
                    compute_op=ALU.add))
                s_gd.wait(g)
            s_g.inc(g.nop())

        @blk.sync
        def _(s, k=k, pb=pb):
            s_out.wait(s)
            s_ld.dma(s.dma_start(out=xt2[:, pb, :], in_=x_in[k * 128:(k + 1) * 128, :]))

        @blk.vector
        def _(v, k=k, pb=pb):
            s_ld.wait(v)
            s_g.wait(v)
            s_gd.wait(v)
            s_v.chain(v, v.tensor_copy(sq[:, pb, :], acc[:, pb, :]))
            ins = v.scalar_tensor_tensor(out=xt2[:, pb, :], in0=sq[:, pb, :],
                                         scalar=0.5, in1=xt2[:, pb, :],
                                         op0=ALU.mult, op1=ALU.add)
            s_v.inc(ins)

        @blk.scalar
        def _(sc, k=k, pb=pb):
            s_v.wait(sc)
            s_out.dma(sc.dma_start(out=y_out[k * 128:(k + 1) * 128, :],
                                   in_=xt2[:, pb, :]))

    if ext:
        return

    @blk.gpsimd
    def _(g):
        s_out.wait(g)
        s_gd.wait(g)

    return p.finish()


_L1FFN_NC = None


def run_l1ffn(xshards, Wg, W1e_list, b1e_list, W2e_list, b2e_list):
    global _L1FFN_NC
    if _L1FFN_NC is None:
        _L1FFN_NC = build_l1ffn()
    in_maps = []
    for c in range(NC):
        msg, mcomb = l1ffn_tables(c)
        in_maps.append(dict(
            x=xshards[c], wg=Wg, wg_my=np.ascontiguousarray(Wg[:, c:c + 1]),
            w1=W1e_list[c], b1=b1e_list[c].reshape(1, FF),
            w2=W2e_list[c], b2=b2e_list[c].reshape(1, D),
            msg_idx=msg, mc_idx=mcomb))
    res = run_bass_kernel_spmd(_L1FFN_NC, in_maps, core_ids=list(range(NC)))
    return [np.asarray(res.results[c]["y_mo"]) for c in range(NC)]




def kernel_unfused(L0_attn_x, L0_attn_w_q, L0_attn_w_k, L0_attn_w_v,
           L0_ffn_W_r, L0_ffn_W_k, L0_ffn_W_v, L0_ffn_mu_r, L0_ffn_mu_k,
           L1_attn_W_in, L1_attn_conv_w, L1_attn_conv_b, L1_attn_W_x,
           L1_attn_w_dt, L1_attn_b_dt, L1_attn_A_log, L1_attn_D_param,
           L1_attn_W_out, L1_ffn_Wg, L1_ffn_W1, L1_ffn_b1, L1_ffn_W2,
           L1_ffn_b2):
    f = np.asarray
    x = np.ascontiguousarray(f(L0_attn_x), dtype=np.float32)
    sh = shard_x(x)
    w1l = [np.ascontiguousarray(f(L1_ffn_W1)[c]) for c in range(NE)]
    b1l = [np.ascontiguousarray(f(L1_ffn_b1)[c]) for c in range(NE)]
    w2l = [np.ascontiguousarray(f(L1_ffn_W2)[c]) for c in range(NE)]
    b2l = [np.ascontiguousarray(f(L1_ffn_b2)[c]) for c in range(NE)]

    sh = run_l0ffn(sh, f(L0_ffn_W_r), f(L0_ffn_W_k), f(L0_ffn_W_v),
                   f(L0_ffn_mu_r), f(L0_ffn_mu_k))
    sh = run_l0attn(sh, f(L0_attn_w_q), f(L0_attn_w_k), f(L0_attn_w_v))
    sh = run_l0ffn(sh, f(L0_ffn_W_r), f(L0_ffn_W_k), f(L0_ffn_W_v),
                   f(L0_ffn_mu_r), f(L0_ffn_mu_k))
    sh = run_l1ffn(sh, f(L1_ffn_Wg), w1l, b1l, w2l, b2l)
    sh = run_l1attn(sh, f(L1_attn_W_in), f(L1_attn_conv_w), f(L1_attn_conv_b),
                    f(L1_attn_W_x), f(L1_attn_w_dt), f(L1_attn_b_dt),
                    f(L1_attn_A_log), f(L1_attn_D_param), f(L1_attn_W_out))
    sh = run_l1ffn(sh, f(L1_ffn_Wg), w1l, b1l, w2l, b2l)
    return unshard_x(sh)



def emit_barrier(p, blk):
    sems = [p.sem(n) for n in ("s_ld", "s_st", "s_g", "s_gd", "s_v", "s_a",
                               "s_t", "s_cc", "s_out")]
    sbar = p.sem("s_bar")

    @blk.gpsimd
    def _(g):
        for sm in sems:
            sm.wait(g)
        sbar.inc(g.nop())

    for sec in (blk.sync, blk.vector, blk.scalar, blk.tensor):
        @sec
        def _(eng):
            sbar.wait(eng)


def build_fused():
    p = Prog()
    x_in = p.dram("x", [TPC, D], F32, kind="ExternalInput")
    y_out = p.dram("y", [TPC, D], F32, kind="ExternalOutput")
    xI0 = p.dram("xI0", [TPC, D], F32)
    xI1 = p.dram("xI1", [TPC, D], F32)
    blk = p.block()

    chain_io = [(x_in, xI0), (xI0, xI1), (xI1, xI0), (xI0, xI1), (xI1, xI0),
                (xI0, y_out)]
    builders = [build_l0ffn, build_l0attn, build_l0ffn, build_l1ffn,
                build_l1attn, build_l1ffn]
    for (xi, yo), bld in zip(chain_io, builders):
        p.push_scope()
        bld(ext=dict(p=p, x_in=xi, y_out=yo))
        p.pop_scope()
        emit_barrier(p, blk)

    s_out = p.sem("s_out")

    @blk.gpsimd
    def _(g):
        s_out.wait(g)

    return p.finish()


_FUSED_NC = None


def kernel_fused(L0_attn_x, L0_attn_w_q, L0_attn_w_k, L0_attn_w_v,
                 L0_ffn_W_r, L0_ffn_W_k, L0_ffn_W_v, L0_ffn_mu_r, L0_ffn_mu_k,
                 L1_attn_W_in, L1_attn_conv_w, L1_attn_conv_b, L1_attn_W_x,
                 L1_attn_w_dt, L1_attn_b_dt, L1_attn_A_log, L1_attn_D_param,
                 L1_attn_W_out, L1_ffn_Wg, L1_ffn_W1, L1_ffn_b1, L1_ffn_W2,
                 L1_ffn_b2):
    global _FUSED_NC
    if _FUSED_NC is None:
        _FUSED_NC = build_fused()
    f = np.asarray
    x = np.ascontiguousarray(f(L0_attn_x), dtype=np.float32)
    sh = shard_x(x)
    mu = np.stack([f(L0_ffn_mu_r), f(L0_ffn_mu_k)]).astype(np.float32)
    Wg = f(L1_ffn_Wg)
    in_maps = []
    for c in range(NC):
        tb = attn_tables(c)
        cbidx, smask = l1attn_tables(c)
        msg, mcomb = l1ffn_tables(c)
        im = dict(x=sh[c])
        in_maps.append(im)
        im.update(dict(
            w_r=f(L0_ffn_W_r), w_k=f(L0_ffn_W_k), w_v=f(L0_ffn_W_v),
            mu_rk=mu, bidx=l0ffn_bidx(c),
            wq_at=f(L0_attn_w_q), wk_at=f(L0_attn_w_k), wv_at=f(L0_attn_w_v),
        ))
        im.update(dict(
            qidx=tb["qidx"], kidx=tb["kidx"], vidx=tb["vidx"], thr=tb["thr"],
            rf=tb["rf"], rb=tb["rb"], ysc=tb["ysc"],
            w_in=f(L1_attn_W_in), conv_w=f(L1_attn_conv_w).reshape(DIN, 4),
            conv_b=f(L1_attn_conv_b).reshape(1, DIN), w_x=f(L1_attn_W_x),
            w_dt=f(L1_attn_w_dt).reshape(1, DIN),
            b_dt=f(L1_attn_b_dt).reshape(1, DIN), a_log=f(L1_attn_A_log),
            d_par=f(L1_attn_D_param).reshape(1, DIN), w_out=f(L1_attn_W_out),
            cbidx=cbidx, smask=smask,
            wg=Wg, wg_my=np.ascontiguousarray(Wg[:, c:c + 1]),
            w1=np.ascontiguousarray(f(L1_ffn_W1)[c]),
            b1=np.ascontiguousarray(f(L1_ffn_b1)[c]).reshape(1, FF),
            w2=np.ascontiguousarray(f(L1_ffn_W2)[c]),
            b2=np.ascontiguousarray(f(L1_ffn_b2)[c]).reshape(1, D),
            msg_idx=msg, mc_idx=mcomb))
    res = run_bass_kernel_spmd(_FUSED_NC, in_maps, core_ids=list(range(NC)))
    return unshard_x([np.asarray(res.results[c]["y"]) for c in range(NC)])


kernel = kernel_fused


# version 05
# speedup vs baseline: 1.4690x; 1.0848x over previous
import numpy as np
from contextlib import ExitStack

import concourse.bass as bass
import concourse.mybir as mybir
from concourse.bass import IndirectOffsetOnAxis
from concourse.bass_utils import run_bass_kernel_spmd

F32 = mybir.dt.float32
BF16 = mybir.dt.bfloat16
I32 = mybir.dt.int32
AF = mybir.ActivationFunctionType
ALU = mybir.AluOpType
AX = mybir.AxisListType

NC = 8
B = 4
T = 8192
D = 1024
SL = 512
SPB = T // SL
NBS = B * 2
TPC = B * 2 * SL
EPS = 1e-5
HUGE = 1 << 17


def stripe_of(core, slot):
    return core if slot == 0 else (SPB - 1 - core)


def owner_of(stripe):
    return (stripe, 0) if stripe < NC else (SPB - 1 - stripe, 1)


class Sem:
    def __init__(self, ctx, nc, name):
        self.h = ctx.enter_context(nc.semaphore(name))
        self.n = 0

    def inc(self, instr, k=1):
        instr.then_inc(self.h, k)
        self.n += k
        return instr

    def dma(self, instr):
        return self.inc(instr, 16)

    def wait(self, eng, n=None):
        eng.wait_ge(self.h, self.n if n is None else n)

    def chain(self, eng, instr):
        self.inc(instr, 1)
        self.wait(eng)
        return instr


class Prog:
    def __init__(self):
        self.nc = bass.Bass("TRN2", target_bir_lowering=False, debug=False)
        self.ctx = ExitStack()
        self.ctx.__enter__()
        self._sems = {}
        self._drams = {}
        self._names = set()
        self._scopes = []
        self._blk = None

    def _uniq(self, name):
        n, i = name, 0
        while n in self._names:
            i += 1
            n = f"{name}_{i}"
        self._names.add(n)
        return n

    def push_scope(self):
        st = ExitStack()
        self._scopes.append(st)
        return st

    def pop_scope(self):
        self._scopes.pop().close()

    def _stack(self):
        return self._scopes[-1] if self._scopes else self.ctx

    def sem(self, name):
        if name not in self._sems:
            self._sems[name] = Sem(self.ctx, self.nc, name)
        return self._sems[name]

    def sb(self, name, shape, dtype):
        return self._stack().enter_context(
            self.nc.sbuf_tensor(self._uniq(name), shape, dtype))

    def sb_into(self, stack, name, shape, dtype):
        return stack.enter_context(
            self.nc.sbuf_tensor(self._uniq(name), shape, dtype))

    def ps(self, name, shape, dtype):
        return self._stack().enter_context(
            self.nc.psum_tensor(self._uniq(name), shape, dtype))

    def dram(self, name, shape, dtype, kind="Internal"):
        if name in self._drams:
            ap, sh, dt = self._drams[name]
            assert list(sh) == list(shape) and dt == dtype, name
            return ap
        ap = self.nc.dram_tensor(name, shape, dtype, kind=kind).ap()
        self._drams[name] = (ap, shape, dtype)
        return ap

    def block(self):
        if self._blk is None:
            self._blk = self.ctx.enter_context(self.nc.Block())
        return self._blk

    def finish(self):
        self.ctx.close()
        return self.nc


def load_col(eng, sem, dst_col, src_row):
    sem.dma(eng.dma_start(out=dst_col, in_=src_row.rearrange("a b -> b a")))


def emit_identities(blk, s_g, idb, idf):
    @blk.gpsimd
    def _(g):
        g.memset(idb[:], 0.0)
        s_g.inc(g.memset(idf[:], 0.0))
        s_g.wait(g)
        g.affine_select(out=idb[:], in_=idb[:], compare_op=ALU.not_equal,
                        fill=1.0, base=0, pattern=[[-1, 128]], channel_multiplier=1)
        s_g.inc(g.affine_select(out=idf[:], in_=idf[:], compare_op=ALU.not_equal,
                                fill=1.0, base=0, pattern=[[-1, 128]],
                                channel_multiplier=1))


def emit_weightT(p, blk, wap, dst, dout_blocks, din_blocks, bufs, sems):
    wload, wload16, ps_t, idb = bufs
    s_ld, s_v, s_t, s_g = sems
    for i in range(dout_blocks):
        @blk.sync
        def _(s, i=i):
            s_v.wait(s)
            s_ld.dma(s.dma_start(out=wload[:, 0:128 * din_blocks],
                                 in_=wap[128 * i:128 * (i + 1), 0:128 * din_blocks]))

        @blk.vector
        def _(v, i=i):
            s_ld.wait(v)
            s_t.wait(v)
            s_v.inc(v.tensor_copy(wload16[:, 0:128 * din_blocks],
                                  wload[:, 0:128 * din_blocks]))

        for jj in range((din_blocks + 3) // 4):
            nj = min(4, din_blocks - jj * 4)

            @blk.tensor
            def _(t, i=i, jj=jj, nj=nj):
                s_v.wait(t)
                s_g.wait(t)
                for j4 in range(nj):
                    j = jj * 4 + j4
                    ins = t.transpose(ps_t[:, 128 * j4:128 * (j4 + 1)],
                                      wload16[:, 128 * j:128 * (j + 1)], idb[:])
                s_t.inc(ins)

            @blk.vector
            def _(v, i=i, jj=jj, nj=nj):
                s_t.wait(v)
                for j4 in range(nj):
                    j = jj * 4 + j4
                    ins = v.tensor_copy(dst[:, j, 128 * i:128 * (i + 1)],
                                        ps_t[:, 128 * j4:128 * (j4 + 1)])
                s_v.inc(ins)


def emit_ln_pass(p, blk, x_src, lnD, ag_in, bufs, sems, n_tiles,
                 lnD_rowmaj=True, stage_last=True):
    xt, sq, stat, ln16 = bufs
    s_ld, s_v, s_a, s_st, s_gd = sems
    for k in range(n_tiles):
        bs, ti = divmod(k, 4)
        pb = k % 2

        @blk.sync
        def _(s, k=k, pb=pb):
            s_v.wait(s)
            s_ld.dma(s.dma_start(out=xt[:, pb, :], in_=x_src[k * 128:(k + 1) * 128, :]))

        @blk.vector
        def _(v, pb=pb):
            s_ld.wait(v)
            s_a.wait(v)
            s_v.chain(v, v.tensor_reduce(out=stat[:, 0:1], in_=xt[:, pb, :],
                                         axis=AX.X, op=ALU.add))
            s_v.chain(v, v.tensor_scalar_mul(stat[:, 2:3], stat[:, 0:1], 1.0 / D))
            s_v.inc(v.tensor_scalar(out=sq[:, pb, :], in0=xt[:, pb, :], scalar1=stat[:, 2:3],
                                    scalar2=None, op0=ALU.subtract))

        @blk.scalar
        def _(sc, pb=pb):
            s_v.wait(sc)
            s_a.inc(sc.activation(out=sq[:, pb, :], in_=sq[:, pb, :], func=AF.Square,
                                  accum_out=stat[:, 1:2]))

        @blk.vector
        def _(v):
            s_a.wait(v)
            s_v.inc(v.tensor_scalar(out=stat[:, 3:4], in0=stat[:, 1:2],
                                    scalar1=1.0 / D, scalar2=EPS,
                                    op0=ALU.mult, op1=ALU.add))

        @blk.scalar
        def _(sc):
            s_v.wait(sc)
            s_a.inc(sc.activation(out=stat[:, 4:5], in_=stat[:, 3:4], func=AF.Sqrt))

        @blk.vector
        def _(v, pb=pb):
            s_a.wait(v)
            s_st.wait(v)
            s_gd.wait(v)
            s_v.chain(v, v.reciprocal(stat[:, 5:6], stat[:, 4:5]))
            s_v.chain(v, v.tensor_scalar(out=sq[:, pb, :], in0=xt[:, pb, :],
                                         scalar1=stat[:, 2:3], scalar2=None,
                                         op0=ALU.subtract))
            s_v.inc(v.tensor_scalar(out=ln16[:, pb, :], in0=sq[:, pb, :], scalar1=stat[:, 5:6],
                                    scalar2=None, op0=ALU.mult))

        @blk.scalar
        def _(sc, bs=bs, ti=ti, pb=pb):
            s_v.wait(sc)
            s_st.dma(sc.dma_start(out=lnD[bs, 1 + ti * 128:1 + (ti + 1) * 128, :],
                                  in_=ln16[:, pb, :]))

        if stage_last and ti == 3:
            @blk.gpsimd
            def _(g, bs=bs, pb=pb):
                s_v.wait(g)
                s_gd.dma(g.dma_start(out=ag_in[bs:bs + 1, :],
                                     in_=ln16[127:128, pb, :]))



def build_l0ffn(ext=None):
    p = ext["p"] if ext else Prog()
    nc = p.nc

    x_in = ext["x_in"] if ext else p.dram("x", [TPC, D], F32, kind="ExternalInput")
    w_r = p.dram("w_r", [D, D], F32, kind="ExternalInput")
    w_k = p.dram("w_k", [D, D], F32, kind="ExternalInput")
    w_v = p.dram("w_v", [D, D], F32, kind="ExternalInput")
    mu_rk = p.dram("mu_rk", [2, D], F32, kind="ExternalInput")
    bidx = p.dram("bidx", [NBS, 1], I32, kind="ExternalInput")
    y_out = ext["y_out"] if ext else p.dram("y", [TPC, D], F32, kind="ExternalOutput")

    lnD = p.dram("lnD", [NBS, 1 + SL, D], BF16)
    ag_in = p.dram("ag_in", [NBS, D], BF16)
    ag_out = p.dram("ag_out", [NC * NBS, D], BF16)

    xt = p.sb("xt", [128, 2, D], F32)
    xt2 = p.sb("xt2", [128, D], F32)
    sq = p.sb("sq", [128, 2, D], F32)
    stat = p.sb("stat", [128, 8], F32)
    ln16 = p.sb("ln16", [128, 2, D], BF16)
    wload = p.sb("wload", [128, D], F32)
    wload16 = p.sb("wload16", [128, D], BF16)
    wts = {nm: p.sb(nm + "T", [128, 8, D], BF16) for nm in ("wr", "wk", "wv")}
    mu_t = p.sb("mu_t", [128, 16], F32)
    bidx_sb = p.sb("bidx_sb", [NBS, 1], I32)
    brow = p.sb("brow", [NBS, D], BF16)
    uT = p.sb("uT", [128, 8, SL], BF16)
    upT = p.sb("upT", [128, 8, SL], BF16)
    xrT = p.sb("xrT", [128, 8, SL], BF16)
    xkT = p.sb("xkT", [128, 8, SL], BF16)
    rT = p.sb("rT", [128, 8, SL], BF16)
    kaT = p.sb("kaT", [128, 8, SL], BF16)
    vvT = p.sb("vvT", [128, 8, SL], F32)
    relu_t = p.sb("relu_t", [128, SL], F32)
    idb = p.sb("idb", [128, 128], BF16)
    idf = p.sb("idf", [128, 128], F32)

    ps_t = p.ps("ps_t", [128, 512], BF16)
    ps_tf = p.ps("ps_tf", [128, 512], F32)
    ps_m = p.ps("ps_m", [128, 512], F32)
    ps_m2 = p.ps("ps_m2", [128, 512], F32)

    s_ld = p.sem("s_ld")
    s_st = p.sem("s_st")
    s_g = p.sem("s_g")
    s_gd = p.sem("s_gd")
    s_v = p.sem("s_v")
    s_a = p.sem("s_a")
    s_t = p.sem("s_t")
    s_cc = p.sem("s_cc")
    s_out = p.sem("s_out")

    blk = p.block()

    emit_identities(blk, s_g, idb, idf)

    @blk.sync
    def _(s):
        s_ld.dma(s.dma_start(out=bidx_sb[:], in_=bidx))
        for ch in range(8):
            load_col(s, s_ld, mu_t[:, 2 * ch:2 * ch + 1],
                     mu_rk[0:1, 128 * ch:128 * (ch + 1)])
            load_col(s, s_ld, mu_t[:, 2 * ch + 1:2 * ch + 2],
                     mu_rk[1:2, 128 * ch:128 * (ch + 1)])

    for nm, wap in (("wr", w_r), ("wk", w_k), ("wv", w_v)):
        emit_weightT(p, blk, wap, wts[nm], 8, 8,
                     (wload, wload16, ps_t, idb), (s_ld, s_v, s_t, s_g))

    emit_ln_pass(p, blk, x_in, lnD, ag_in, (xt, sq, stat, ln16),
                 (s_ld, s_v, s_a, s_st, s_gd), TPC // 128)

    @blk.gpsimd
    def _(g):
        s_gd.wait(g)
        s_st.wait(g)
        s_cc.inc(g.collective_compute(
            "AllGather", ALU.bypass, replica_groups=[list(range(NC))],
            ins=[ag_in], outs=[ag_out]))
        s_cc.wait(g)
        s_g.inc(g.memset(brow[:], 0.0))
        s_g.wait(g)
        s_ld.wait(g)
        s_gd.dma(g.indirect_dma_start(
            out=brow[:], out_offset=None, in_=ag_out,
            in_offset=IndirectOffsetOnAxis(bidx_sb[:, 0:1], axis=0),
            bounds_check=NC * NBS - 1, oob_is_err=False))
        s_gd.wait(g)
        for bs in range(NBS):
            s_gd.dma(g.dma_start(out=lnD[bs, 0:1, :], in_=brow[bs:bs + 1, :]))

    for bs in range(NBS):
        for ti in range(4):
            @blk.sync
            def _(s, bs=bs, ti=ti):
                if ti == 0:
                    s_g.wait(s)
                    s_gd.wait(s)
                    s_st.wait(s)
                    s_v.wait(s)
                else:
                    s_v.wait(s)
                s_ld.dma(s.dma_start(out=ln16[:, 0, :],
                                     in_=lnD[bs, 1 + ti * 128:1 + (ti + 1) * 128, :]))
                s_ld.dma(s.dma_start(out=ln16[:, 1, :],
                                     in_=lnD[bs, ti * 128:(ti + 1) * 128, :]))

            for src, dstbuf in ((0, uT), (1, upT)):
                for jj in range(2):
                    @blk.tensor
                    def _(t, src=src, jj=jj):
                        s_ld.wait(t)
                        s_v.wait(t)
                        s_g.wait(t)
                        for j4 in range(4):
                            j = jj * 4 + j4
                            ins = t.transpose(ps_t[:, 128 * j4:128 * (j4 + 1)],
                                              ln16[:, src, 128 * j:128 * (j + 1)], idb[:])
                        s_t.inc(ins)

                    @blk.vector
                    def _(v, dstbuf=dstbuf, ti=ti, jj=jj):
                        s_t.wait(v)
                        for j4 in range(4):
                            j = jj * 4 + j4
                            ins = v.tensor_copy(dstbuf[:, j, 128 * ti:128 * (ti + 1)],
                                                ps_t[:, 128 * j4:128 * (j4 + 1)])
                        s_v.inc(ins)

        @blk.vector
        def _(v, bs=bs):
            s_v.wait(v)
            for j in range(8):
                ins = v.tensor_sub(xrT[:, j, :], uT[:, j, :], upT[:, j, :])
            s_v.chain(v, ins)
            for j in range(8):
                v.scalar_tensor_tensor(out=xkT[:, j, :], in0=xrT[:, j, :],
                                       scalar=mu_t[:, 2 * j + 1:2 * j + 2],
                                       in1=upT[:, j, :], op0=ALU.mult, op1=ALU.add)
            for j in range(8):
                ins = v.scalar_tensor_tensor(out=xrT[:, j, :], in0=xrT[:, j, :],
                                             scalar=mu_t[:, 2 * j:2 * j + 1],
                                             in1=upT[:, j, :], op0=ALU.mult, op1=ALU.add)
            s_v.inc(ins)

        for i in range(8):
            @blk.tensor
            def _(t, i=i):
                s_v.wait(t)
                s_a.wait(t)
                for j in range(8):
                    ins = t.matmul(ps_m[:, :], wts["wr"][:, j, 128 * i:128 * (i + 1)],
                                   xrT[:, j, :], start=(j == 0), stop=(j == 7))
                s_t.inc(ins)
                for j in range(8):
                    ins = t.matmul(ps_m2[:, :], wts["wk"][:, j, 128 * i:128 * (i + 1)],
                                   xkT[:, j, :], start=(j == 0), stop=(j == 7))
                s_t.inc(ins)

            @blk.scalar
            def _(sc, i=i):
                s_t.wait(sc, s_t.n - 1)
                s_a.inc(sc.activation(out=rT[:, i, :], in_=ps_m[:, :], func=AF.Sigmoid))

            @blk.vector
            def _(v, i=i):
                s_t.wait(v)
                s_a.wait(v, s_a.n - 1)
                s_v.inc(v.tensor_scalar_max(relu_t[:, :], ps_m2[:, :], 0.0))

            @blk.scalar
            def _(sc, i=i):
                s_v.wait(sc)
                s_a.inc(sc.activation(out=kaT[:, i, :], in_=relu_t[:, :], func=AF.Square))

        for i in range(8):
            @blk.tensor
            def _(t, i=i):
                s_a.wait(t)
                s_v.wait(t)
                for j in range(8):
                    ins = t.matmul(ps_m[:, :], wts["wv"][:, j, 128 * i:128 * (i + 1)],
                                   kaT[:, j, :], start=(j == 0), stop=(j == 7))
                s_t.inc(ins)

            @blk.vector
            def _(v, i=i):
                s_t.wait(v)
                s_v.inc(v.tensor_mul(vvT[:, i, :], rT[:, i, :], ps_m[:, :]))

        for ti in range(4):
            @blk.sync
            def _(s, bs=bs, ti=ti):
                s_out.wait(s)
                row0 = bs * SL + ti * 128
                s_ld.dma(s.dma_start(out=xt2[:], in_=x_in[row0:row0 + 128, :]))

            for half in range(2):
                @blk.tensor
                def _(t, ti=ti, half=half):
                    s_v.wait(t)
                    for i4 in range(4):
                        i = half * 4 + i4
                        ins = t.transpose(ps_tf[:, 128 * i4:128 * (i4 + 1)],
                                          vvT[:, i, 128 * ti:128 * (ti + 1)], idf[:])
                    s_t.inc(ins)

                @blk.vector
                def _(v, ti=ti, half=half):
                    s_t.wait(v)
                    s_ld.wait(v)
                    for i4 in range(4):
                        i = half * 4 + i4
                        ins = v.scalar_tensor_tensor(
                            out=xt2[:, 128 * i:128 * (i + 1)],
                            in0=ps_tf[:, 128 * i4:128 * (i4 + 1)],
                            scalar=0.5, in1=xt2[:, 128 * i:128 * (i + 1)],
                            op0=ALU.mult, op1=ALU.add)
                    s_v.inc(ins)

            @blk.scalar
            def _(sc, bs=bs, ti=ti):
                s_v.wait(sc)
                row0 = bs * SL + ti * 128
                s_out.dma(sc.dma_start(out=y_out[row0:row0 + 128, :], in_=xt2[:]))
                s_a.inc(sc.nop())

    if ext:
        return

    @blk.gpsimd
    def _(g):
        s_out.wait(g)

    return p.finish()



def shard_x(x):
    shards = []
    for c in range(NC):
        parts = []
        for b in range(B):
            for slot in range(2):
                st = stripe_of(c, slot)
                parts.append(x[b, st * SL:(st + 1) * SL, :])
        shards.append(np.ascontiguousarray(np.concatenate(parts, axis=0)))
    return shards


def unshard_x(shards):
    x = np.empty((B, T, D), dtype=np.asarray(shards[0]).dtype)
    for c in range(NC):
        sh = np.asarray(shards[c]).reshape(B, 2, SL, D)
        for b in range(B):
            for slot in range(2):
                st = stripe_of(c, slot)
                x[b, st * SL:(st + 1) * SL, :] = sh[b, slot]
    return x


def l0ffn_bidx(core):
    out = np.full((NBS, 1), HUGE, dtype=np.int32)
    for b in range(B):
        for slot in range(2):
            st = stripe_of(core, slot)
            if st == 0:
                continue
            pr, ps = owner_of(st - 1)
            out[b * 2 + slot, 0] = pr * NBS + b * 2 + ps
    return out


_L0FFN_NC = None


def run_l0ffn(xshards, W_r, W_k, W_v, mu_r, mu_k):
    global _L0FFN_NC
    if _L0FFN_NC is None:
        _L0FFN_NC = build_l0ffn()
    mu = np.stack([mu_r, mu_k]).astype(np.float32)
    in_maps = []
    for c in range(NC):
        in_maps.append(dict(x=xshards[c], w_r=W_r, w_k=W_k, w_v=W_v,
                            mu_rk=mu, bidx=l0ffn_bidx(c)))
    res = run_bass_kernel_spmd(_L0FFN_NC, in_maps, core_ids=list(range(NC)))
    return [np.asarray(res.results[c]["y"]) for c in range(NC)]



GAMMA = 1.0 / 32.0
MNEG = -30000.0


def attn_steps(core):
    steps = []
    for b in range(B):
        for slot in range(2):
            s = stripe_of(core, slot)
            for kb in range(s + 1):
                steps.append((b * 2 + slot, kb))
    return steps


def attn_tables(core):
    steps = attn_steps(core)
    ns = len(steps)
    qidx = np.zeros((128, ns * 8), dtype=np.int32)
    kidx = np.zeros((128, ns * 8), dtype=np.int32)
    vidx = np.zeros((128, ns * 4), dtype=np.int32)
    thr = np.zeros((128, ns * 4), dtype=np.float32)
    rf = np.zeros((128, ns), dtype=np.float32)
    rb = np.zeros((128, ns), dtype=np.float32)
    ysc = np.full((128, (ns + 1) * 4), HUGE, dtype=np.int32)
    p = np.arange(128)
    prev_bs = -1
    for k, (bs, kb) in enumerate(steps):
        b, slot = divmod(bs, 2)
        s = stripe_of(core, slot)
        for op in range(8):
            dr = op * 128 + p
            qidx[:, k * 8 + op] = dr * (TPC // SL) + bs
        rk, kslot = owner_of(kb)
        kbs = b * 2 + kslot
        for op in range(8):
            dr = rk * D + op * 128 + p
            kidx[:, k * 8 + op] = dr * (TPC // SL) + kbs
        for op in range(4):
            vidx[:, k * 4 + op] = rk * TPC + kbs * SL + op * 128 + p
        for qi in range(4):
            if kb == s:
                thr[:, k * 4 + qi] = qi * 128 + p
            else:
                thr[:, k * 4 + qi] = 1e4
        newstripe = bs != prev_bs
        rf[:, k] = 0.0 if newstripe else 1.0
        rb[:, k] = MNEG if newstripe else 0.0
        if newstripe and prev_bs >= 0:
            for qi in range(4):
                ysc[:, k * 4 + qi] = prev_bs * SL + qi * 128 + p
        prev_bs = bs
    for qi in range(4):
        ysc[:, ns * 4 + qi] = prev_bs * SL + qi * 128 + p
    return dict(qidx=qidx, kidx=kidx, vidx=vidx, thr=thr, rf=rf, rb=rb, ysc=ysc)


def build_l0attn(debug_dump=False, dbg_steps=None, ext=None):
    p = ext["p"] if ext else Prog()
    nc = p.nc
    ns = 17 * B

    x_in = ext["x_in"] if ext else p.dram("x", [TPC, D], F32, kind="ExternalInput")
    w_q = p.dram("wq_at", [D, D], F32, kind="ExternalInput")
    w_k = p.dram("wk_at", [D, D], F32, kind="ExternalInput")
    w_v = p.dram("wv_at", [D, D], F32, kind="ExternalInput")
    qidx_d = p.dram("qidx", [128, ns * 8], I32, kind="ExternalInput")
    kidx_d = p.dram("kidx", [128, ns * 8], I32, kind="ExternalInput")
    vidx_d = p.dram("vidx", [128, ns * 4], I32, kind="ExternalInput")
    thr_d = p.dram("thr", [128, ns * 4], F32, kind="ExternalInput")
    rf_d = p.dram("rf", [128, ns], F32, kind="ExternalInput")
    rb_d = p.dram("rb", [128, ns], F32, kind="ExternalInput")
    ysc_d = p.dram("ysc", [128, (ns + 1) * 4], I32, kind="ExternalInput")
    y_out = ext["y_out"] if ext else p.dram("y_at", [TPC, D], F32, kind="ExternalOutput")

    lnD = p.dram("lnD", [NBS, 1 + SL, D], BF16)
    qT = p.dram("qT", [D, TPC], BF16)
    agK_in = p.dram("agK_in", [D, TPC], BF16)
    agK = p.dram("agK", [NC * D, TPC], BF16)
    agV_in = p.dram("agV_in", [TPC, D], BF16)
    agV = p.dram("agV", [NC * TPC, D], BF16)
    yD = p.dram("yD", [TPC, D], F32)

    xt = p.sb("xt", [128, 2, D], F32)
    sq = p.sb("sq", [128, 2, D], F32)
    stat = p.sb("stat", [128, 8], F32)
    ln16 = p.sb("ln16", [128, 2, D], BF16)
    wsb = {nm: p.sb("w" + nm, [128, 8, D], BF16) for nm in ("q", "k", "v")}
    lnTw = p.sb("lnTw", [128, 8, SL], BF16)
    otile = p.sb("otile", [128, 2, D], BF16)
    qTs = p.sb("qTs", [128, 8, SL], BF16)
    kTs = p.sb("kTs", [128, 8, SL], BF16)
    vs = p.sb("vs", [128, 4, D], BF16)
    idxcols = p.sb("idxcols", [128, ns * 8], I32)
    idxcols2 = p.sb("idxcols2", [128, ns * 8], I32)
    idxcols3 = p.sb("idxcols3", [128, ns * 4], I32)
    thr_sb = p.sb("thr_sb", [128, ns * 4], F32)
    rf_sb = p.sb("rf_sb", [128, ns], F32)
    rb_sb = p.sb("rb_sb", [128, ns], F32)
    ysc_sb = p.sb("ysc_sb", [128, (ns + 1) * 4], I32)
    kpos = p.sb("kpos", [128, SL], F32)
    kpos_i = p.sb("kpos_i", [128, SL], I32)
    scf = p.sb("scf", [128, SL], F32)
    pmat = p.sb("pmat", [128, SL], BF16)
    pTs = p.sb("pTs", [128, 4, 128], BF16)
    mrow = p.sb("mrow", [128, 16], F32)
    acc = p.sb("acc", [128, 4, D], F32)
    mstate = p.sb("mstate", [128, 4], F32)
    lstate = p.sb("lstate", [128, 4], F32)
    yfin = p.sb("yfin", [128, 4, D], F32)
    idb = p.sb("idb", [128, 128], BF16)
    idf = p.sb("idf", [128, 128], F32)

    ps_t = p.ps("ps_t", [128, 512], BF16)
    ps_s = p.ps("ps_s", [128, 512], F32)
    ps_av = p.ps("ps_av", [128, 512], F32)
    ps_v = p.ps("ps_v", [128, D], F32)

    s_ld = p.sem("s_ld")
    s_st = p.sem("s_st")
    s_g = p.sem("s_g")
    s_gd = p.sem("s_gd")
    s_v = p.sem("s_v")
    s_a = p.sem("s_a")
    s_t = p.sem("s_t")
    s_cc = p.sem("s_cc")
    s_out = p.sem("s_out")

    blk = p.block()
    bregs = {}

    emit_identities(blk, s_g, idb, idf)

    @blk.sync
    def _(s):
        s_ld.dma(s.dma_start(out=idxcols[:], in_=qidx_d))
        s_ld.dma(s.dma_start(out=idxcols2[:], in_=kidx_d))
        s_ld.dma(s.dma_start(out=idxcols3[:], in_=vidx_d))
        s_ld.dma(s.dma_start(out=thr_sb[:], in_=thr_d))
        s_ld.dma(s.dma_start(out=rf_sb[:], in_=rf_d))
        s_ld.dma(s.dma_start(out=rb_sb[:], in_=rb_d))
        s_ld.dma(s.dma_start(out=ysc_sb[:], in_=ysc_d))

    @blk.gpsimd
    def _(g):
        for nm, wap in (("q", w_q), ("k", w_k), ("v", w_v)):
            for kc in range(8):
                s_gd.dma(g.dma_start(out=wsb[nm][:, kc, :],
                                     in_=wap[128 * kc:128 * (kc + 1), :]))
        s_g.chain(g, g.iota(kpos_i[:], pattern=[[1, SL]], base=0, channel_multiplier=0))
        s_g.inc(g.tensor_copy(kpos[:], kpos_i[:]))
        g.memset(acc[:], 0.0)
        g.memset(mrow[:], 0.0)
        g.memset(lstate[:], 1e-30)
        s_g.inc(g.memset(mstate[:], MNEG))
        bregs["yd"] = g.to_reg(TPC - 1)

    emit_ln_pass(p, blk, x_in, lnD, None, (xt, sq, stat, ln16),
                 (s_ld, s_v, s_a, s_st, s_gd), TPC // 128, stage_last=False)

    for bs in range(NBS):
        for ti in range(4):
            @blk.sync
            def _(s, bs=bs, ti=ti):
                if bs == 0 and ti == 0:
                    s_st.wait(s)
                s_v.wait(s)
                s_a.wait(s)
                s_ld.dma(s.dma_start(out=ln16[:, 0, :],
                                     in_=lnD[bs, 1 + ti * 128:1 + (ti + 1) * 128, :]))

            for jj in range(2):
                @blk.tensor
                def _(t, jj=jj):
                    s_ld.wait(t)
                    s_v.wait(t)
                    s_g.wait(t)
                    for j4 in range(4):
                        j = jj * 4 + j4
                        ins = t.transpose(ps_t[:, 128 * j4:128 * (j4 + 1)],
                                          ln16[:, 0, 128 * j:128 * (j + 1)], idb[:])
                    s_t.inc(ins)

                @blk.vector
                def _(v, ti=ti, jj=jj):
                    s_t.wait(v)
                    for j4 in range(4):
                        j = jj * 4 + j4
                        ins = v.tensor_copy(lnTw[:, j, 128 * ti:128 * (ti + 1)],
                                            ps_t[:, 128 * j4:128 * (j4 + 1)])
                    s_v.inc(ins)

        for nm, dstD in (("q", qT), ("k", agK_in)):
            for i in range(8):
                pb = i % 2

                @blk.tensor
                def _(t, nm=nm, i=i):
                    s_v.wait(t)
                    s_a.wait(t)
                    s_gd.wait(t)
                    for j in range(8):
                        ins = t.matmul(ps_s[:, :], wsb[nm][:, j, 128 * i:128 * (i + 1)],
                                       lnTw[:, j, :], start=(j == 0), stop=(j == 7))
                    s_t.inc(ins)

                @blk.vector
                def _(v, pb=pb):
                    s_t.wait(v)
                    s_st.wait(v)
                    s_v.inc(v.tensor_copy(otile[:, pb, 0:SL], ps_s[:, :]))

                @blk.scalar
                def _(sc, nm=nm, dstD=dstD, bs=bs, i=i, pb=pb):
                    s_v.wait(sc)
                    s_st.dma(sc.dma_start(out=dstD[128 * i:128 * (i + 1),
                                                   bs * SL:(bs + 1) * SL],
                                          in_=otile[:, pb, 0:SL]))
                    s_a.inc(sc.nop())

        for ti in range(4):
            @blk.tensor
            def _(t, ti=ti):
                s_v.wait(t)
                s_a.wait(t)
                s_gd.wait(t)
                for half in range(2):
                    for j in range(8):
                        ins = t.matmul(ps_v[:, 512 * half:512 * (half + 1)],
                                       lnTw[:, j, 128 * ti:128 * (ti + 1)],
                                       wsb["v"][:, j, 512 * half:512 * (half + 1)],
                                       start=(j == 0), stop=(j == 7))
                s_t.inc(ins)

            @blk.vector
            def _(v, ti=ti):
                s_t.wait(v)
                s_st.wait(v)
                s_v.inc(v.tensor_copy(otile[:, ti % 2, :], ps_v[:, :]))

            @blk.scalar
            def _(sc, bs=bs, ti=ti):
                s_v.wait(sc)
                s_st.dma(sc.dma_start(out=agV_in[bs * SL + ti * 128:bs * SL + (ti + 1) * 128, :],
                                      in_=otile[:, ti % 2, :]))
                s_a.inc(sc.nop())

    @blk.gpsimd
    def _(g):
        s_st.wait(g)
        s_cc.inc(g.collective_compute("AllGather", ALU.bypass,
                                      replica_groups=[list(range(NC))],
                                      ins=[agK_in], outs=[agK]))
        s_cc.inc(g.collective_compute("AllGather", ALU.bypass,
                                      replica_groups=[list(range(NC))],
                                      ins=[agV_in], outs=[agV]))
        s_cc.wait(g)

    if debug_dump:
        dbg = {nm: p.dram("dbg_" + nm, shape, F32, kind="ExternalOutput")
               for nm, shape in [("yD", [TPC, D]), ("qTd", [D, TPC]), ("agKin", [D, TPC]),
                                 ("wq", [128, 8 * D]), ("lnTw", [128, 8 * SL]),
                                 ("lnDd", [NBS * (1 + SL), D]),
                                 ("scf", [128, SL]), ("pmat", [128, SL]),
                                 ("acc", [128, 4 * D]), ("lstate", [128, 4]),
                                 ("mstate", [128, 4]), ("kTs", [128, 8 * SL]),
                                 ("qTs", [128, 8 * SL]), ("vsb", [128, 4 * D]),
                                 ("mrow", [128, 16])]}
    qT_f = qT.rearrange("a (r c) -> (a r) c", c=SL)
    agK_f = agK.rearrange("a (r c) -> (a r) c", c=SL)
    agV_f = agV
    yD_f = yD

    for k in range(ns + 1 if dbg_steps is None else dbg_steps):
        is_flush = (k == ns)

        @blk.vector
        def _(v, k=k):
            if k == 0:
                s_cc.wait(v)
                s_g.wait(v)
            else:
                s_v.wait(v)
                s_gd.wait(v)
            for qi in range(4):
                ins = v.reciprocal(mrow[:, 8 + qi:9 + qi], lstate[:, qi:qi + 1])
            s_v.chain(v, ins)
            for qi in range(4):
                ins = v.tensor_scalar(out=yfin[:, qi, :], in0=acc[:, qi, :],
                                      scalar1=mrow[:, 8 + qi:9 + qi], scalar2=None,
                                      op0=ALU.mult)
            s_v.inc(ins)

        @blk.gpsimd
        def _(g, k=k):
            s_v.wait(g)
            for qi in range(4):
                ins = g.indirect_dma_start(
                    out=yD_f, out_offset=IndirectOffsetOnAxis(
                        ysc_sb[:, 4 * k + qi:4 * k + qi + 1], axis=0),
                    in_=yfin[:, qi, :], in_offset=None,
                    bounds_check=bregs["yd"], oob_is_err=False)
                s_gd.inc(ins, 16)

        if is_flush:
            break

        @blk.gpsimd
        def _(sc, k=k):
            s_t.wait(sc)
            for op in range(8):
                s_gd.dma(sc.indirect_dma_start(
                    out=qTs[:, op, :], out_offset=None, in_=qT_f,
                    in_offset=IndirectOffsetOnAxis(
                        idxcols[:, 8 * k + op:8 * k + op + 1], axis=0)))
            for op in range(8):
                s_gd.dma(sc.indirect_dma_start(
                    out=kTs[:, op, :], out_offset=None, in_=agK_f,
                    in_offset=IndirectOffsetOnAxis(
                        idxcols2[:, 8 * k + op:8 * k + op + 1], axis=0)))
            for op in range(4):
                s_gd.dma(sc.indirect_dma_start(
                    out=vs[:, op, :], out_offset=None, in_=agV_f,
                    in_offset=IndirectOffsetOnAxis(
                        idxcols3[:, 4 * k + op:4 * k + op + 1], axis=0)))

        for qi in range(4):
            @blk.tensor
            def _(t, k=k, qi=qi):
                if qi == 0:
                    s_gd.wait(t)
                s_v.wait(t)
                for j in range(8):
                    ins = t.matmul(ps_s[:, :], qTs[:, j, 128 * qi:128 * (qi + 1)],
                                   kTs[:, j, :], start=(j == 0), stop=(j == 7))
                s_t.inc(ins)

            @blk.vector
            def _(v, k=k, qi=qi):
                s_t.wait(v)
                s_a.wait(v)
                s_v.chain(v, v.tensor_scalar(out=scf[:], in0=kpos[:],
                                             scalar1=thr_sb[:, 4 * k + qi:4 * k + qi + 1],
                                             scalar2=None, op0=ALU.is_gt))
                s_v.chain(v, v.scalar_tensor_tensor(out=scf[:], in0=scf[:], scalar=MNEG,
                                                    in1=ps_s[:, :], op0=ALU.mult, op1=ALU.add))
                v.reduce_max(mrow[:, 2:3], scf[:], axis=AX.X)
                s_v.chain(v, v.tensor_scalar(out=mrow[:, 1:2], in0=mstate[:, qi:qi + 1],
                                             scalar1=rf_sb[:, k:k + 1], scalar2=rb_sb[:, k:k + 1],
                                             op0=ALU.mult, op1=ALU.add))
                s_v.chain(v, v.tensor_tensor(out=mrow[:, 3:4], in0=mrow[:, 1:2],
                                             in1=mrow[:, 2:3], op=ALU.max))
                v.tensor_copy(mstate[:, qi:qi + 1], mrow[:, 3:4])
                s_v.chain(v, v.tensor_scalar(out=mrow[:, 7:8], in0=mrow[:, 3:4],
                                             scalar1=-GAMMA, scalar2=None, op0=ALU.mult))
                s_v.chain(v, v.tensor_sub(mrow[:, 0:1], mrow[:, 1:2], mrow[:, 3:4]))
                ins = v.tensor_scalar(out=mrow[:, 0:1], in0=mrow[:, 0:1], scalar1=GAMMA,
                                      scalar2=None, op0=ALU.mult)
                s_v.inc(ins)

            @blk.scalar
            def _(sc, k=k, qi=qi):
                s_v.wait(sc)
                sc.activation(out=pmat[:], in_=scf[:], func=AF.Exp, scale=GAMMA,
                              bias=mrow[:, 7:8], accum_out=mrow[:, 6:7])
                ins = sc.activation(out=mrow[:, 4:5], in_=mrow[:, 0:1], func=AF.Exp)
                s_a.inc(ins)

            @blk.vector
            def _(v, k=k, qi=qi):
                s_a.wait(v)
                ins = v.scalar_tensor_tensor(out=lstate[:, qi:qi + 1],
                                             in0=lstate[:, qi:qi + 1],
                                             scalar=mrow[:, 4:5], in1=mrow[:, 6:7],
                                             op0=ALU.mult, op1=ALU.add)
                s_v.inc(ins)

            for kc in range(4):
                pass

            @blk.tensor
            def _(t, k=k, qi=qi):
                s_a.wait(t)
                s_v.wait(t)
                for kc in range(4):
                    ins = t.transpose(ps_t[:, 128 * kc:128 * (kc + 1)],
                                      pmat[:, 128 * kc:128 * (kc + 1)], idb[:])
                s_t.inc(ins)

            @blk.vector
            def _(v, k=k, qi=qi):
                s_t.wait(v)
                for kc in range(4):
                    ins = v.tensor_copy(pTs[:, kc, :], ps_t[:, 128 * kc:128 * (kc + 1)])
                s_v.inc(ins)

            for half in range(2):
                @blk.tensor
                def _(t, k=k, qi=qi, half=half):
                    s_v.wait(t)
                    if half == 0:
                        pass
                    else:
                        s_v.wait(t)
                    for kc in range(4):
                        ins = t.matmul(ps_av[:, :], pTs[:, kc, :],
                                       vs[:, kc, 512 * half:512 * (half + 1)],
                                       start=(kc == 0), stop=(kc == 3))
                    s_t.inc(ins)

                @blk.vector
                def _(v, k=k, qi=qi, half=half):
                    s_t.wait(v)
                    ins = v.scalar_tensor_tensor(
                        out=acc[:, qi, 512 * half:512 * (half + 1)],
                        in0=acc[:, qi, 512 * half:512 * (half + 1)],
                        scalar=mrow[:, 4:5], in1=ps_av[:, :],
                        op0=ALU.mult, op1=ALU.add)
                    s_v.inc(ins)

    if debug_dump:
        @blk.gpsimd
        def _(g):
            s_v.wait(g)
            s_a.wait(g)
            s_t.wait(g)
            s_gd.dma(g.dma_start(out=dbg["scf"], in_=scf[:]))
            s_gd.dma(g.dma_start(out=dbg["pmat"], in_=pmat[:]))
            s_gd.dma(g.dma_start(out=dbg["acc"], in_=acc[:].rearrange("p a b -> p (a b)")))
            s_gd.dma(g.dma_start(out=dbg["lstate"], in_=lstate[:]))
            s_gd.dma(g.dma_start(out=dbg["mstate"], in_=mstate[:]))
            s_gd.dma(g.dma_start(out=dbg["kTs"], in_=kTs[:].rearrange("p a b -> p (a b)")))
            s_gd.dma(g.dma_start(out=dbg["qTs"], in_=qTs[:].rearrange("p a b -> p (a b)")))
            s_gd.dma(g.dma_start(out=dbg["vsb"], in_=vs[:].rearrange("p a b -> p (a b)")))
            s_gd.dma(g.dma_start(out=dbg["mrow"], in_=mrow[:]))
            s_gd.dma(g.dma_start(out=dbg["yD"], in_=yD))
            s_gd.dma(g.dma_start(out=dbg["qTd"], in_=qT))
            s_gd.dma(g.dma_start(out=dbg["agKin"], in_=agK_in))
            s_gd.dma(g.dma_start(out=dbg["wq"], in_=wsb["q"][:].rearrange("p a b -> p (a b)")))
            s_gd.dma(g.dma_start(out=dbg["lnTw"], in_=lnTw[:].rearrange("p a b -> p (a b)")))
            s_gd.dma(g.dma_start(out=dbg["lnDd"], in_=lnD.rearrange("a b c -> (a b) c")))
            s_gd.wait(g)

    for k in range(TPC // 128 if dbg_steps is None else 0):
        pb = k % 2

        @blk.sync
        def _(s, k=k, pb=pb):
            s_gd.wait(s)
            s_out.wait(s)
            s_ld.dma(s.dma_start(out=xt[:, pb, :], in_=x_in[128 * k:128 * (k + 1), :]))
            s_ld.dma(s.dma_start(out=sq[:, pb, :], in_=yD[128 * k:128 * (k + 1), :]))

        @blk.vector
        def _(v, pb=pb):
            s_ld.wait(v)
            s_v.inc(v.tensor_add(xt[:, pb, :], xt[:, pb, :], sq[:, pb, :]))

        @blk.scalar
        def _(sc, k=k, pb=pb):
            s_v.wait(sc)
            s_out.dma(sc.dma_start(out=y_out[128 * k:128 * (k + 1), :], in_=xt[:, pb, :]))
            s_a.inc(sc.nop())

    if ext:
        return

    @blk.gpsimd
    def _(g):
        s_out.wait(g)

    return p.finish()


_L0ATTN_NC = None


def run_l0attn(xshards, w_q, w_k, w_v):
    global _L0ATTN_NC
    if _L0ATTN_NC is None:
        _L0ATTN_NC = build_l0attn()
    in_maps = []
    for c in range(NC):
        tb = attn_tables(c)
        in_maps.append(dict(x=xshards[c], wq_at=w_q, wk_at=w_k, wv_at=w_v,
                            qidx=tb["qidx"], kidx=tb["kidx"], vidx=tb["vidx"],
                            thr=tb["thr"], rf=tb["rf"], rb=tb["rb"], ysc=tb["ysc"]))
    res = run_bass_kernel_spmd(_L0ATTN_NC, in_maps, core_ids=list(range(NC)))
    return [np.asarray(res.results[c]["y_at"]) for c in range(NC)]



DIN = 2048
NSTATE = 16
NDT = DIN // 128


def l1attn_tables(core):
    p = np.arange(128)
    cbidx = np.full((128, NBS * NDT), HUGE, dtype=np.int32)
    for b in range(B):
        for slot in range(2):
            st = stripe_of(core, slot)
            if st == 0:
                continue
            pr, ps = owner_of(st - 1)
            pbs = b * 2 + ps
            for dt in range(NDT):
                cbidx[:, (b * 2 + slot) * NDT + dt] = \
                    pr * (NBS * DIN) + pbs * DIN + dt * 128 + p
    smask = np.zeros((128, SPB * 2), dtype=np.float32)
    for slot in range(2):
        smask[:, stripe_of(core, slot) * 2 + slot] = 1.0
    return cbidx, smask


def build_l1attn(ext=None):
    p = ext["p"] if ext else Prog()
    nc = p.nc

    x_in = ext["x_in"] if ext else p.dram("x", [TPC, D], F32, kind="ExternalInput")
    w_in = p.dram("w_in", [2 * DIN, D], F32, kind="ExternalInput")
    conv_w = p.dram("conv_w", [DIN, 4], F32, kind="ExternalInput")
    conv_b = p.dram("conv_b", [1, DIN], F32, kind="ExternalInput")
    w_x = p.dram("w_x", [33, DIN], F32, kind="ExternalInput")
    w_dt = p.dram("w_dt", [1, DIN], F32, kind="ExternalInput")
    b_dt = p.dram("b_dt", [1, DIN], F32, kind="ExternalInput")
    a_log = p.dram("a_log", [DIN, NSTATE], F32, kind="ExternalInput")
    d_par = p.dram("d_par", [1, DIN], F32, kind="ExternalInput")
    w_out = p.dram("w_out", [D, DIN], F32, kind="ExternalInput")
    cbidx_d = p.dram("cbidx", [128, NBS * NDT], I32, kind="ExternalInput")
    smask_d = p.dram("smask", [128, SPB * 2], F32, kind="ExternalInput")
    y_out = ext["y_out"] if ext else p.dram("y_ma", [TPC, D], F32, kind="ExternalOutput")

    lnD = p.dram("lnD", [NBS, 1 + SL, D], BF16)
    xzpreD = p.dram("xzpreD", [NBS, DIN, SL], BF16)
    zD = p.dram("zD", [NBS, DIN, SL], BF16)
    xcD = p.dram("xcD", [NBS, DIN, SL], BF16)
    csD = p.dram("csD", [NBS, DIN, SL], F32)
    ypD = p.dram("ypD", [NBS, DIN, SL], F32)
    yfD = p.dram("yfD", [NBS, DIN, SL], BF16)
    xpD = p.dram("xpD", [NBS, 33, SL], F32)
    cb_ag_in = p.dram("cb_ag_in", [NBS * DIN, 8], F32)
    cb_ag = p.dram("cb_ag", [NC * NBS * DIN, 8], F32)
    sm_ag_in = p.dram("sm_ag_in", [NBS * DIN, 24], F32)
    sm_ag = p.dram("sm_ag", [NC * NBS * DIN, 24], F32)

    xt = p.sb("xt", [128, 2, D], F32)
    sq = p.sb("sq", [128, 2, D], F32)
    stat = p.sb("stat", [128, 8], F32)
    ln16 = p.sb("ln16", [128, 2, D], BF16)
    woT = p.sb("woT", [128, NDT, D], BF16)
    wxT = p.sb("wxT", [128, NDT, 33], BF16)
    cw = p.sb("cw", [128, NDT, 4], F32)
    cb_c = p.sb("cb_c", [128, NDT], F32)
    wdt_c = p.sb("wdt_c", [128, NDT], F32)
    bdt_c = p.sb("bdt_c", [128, NDT], F32)
    dpar_c = p.sb("dpar_c", [128, NDT], F32)
    asb = p.sb("asb", [128, NDT, NSTATE], F32)
    acol = p.sb("acol", [128, NDT, NSTATE], F32)
    cbidx_sb = p.sb("cbidx_sb", [128, NBS * NDT], I32)
    smask_sb = p.sb("smask_sb", [128, SPB * 2], F32)
    cbnd = p.sb("cbnd", [128, NBS, NDT, 4], F32)
    lnTw = p.sb("lnTw", [128, 8, SL], BF16)
    ztile = p.sb("ztile", [128, 2, SL], BF16)
    onesc = p.sb("onesc", [1, 128], F32)
    onesw = p.sb("onesw", [128, SL], F32)
    selm = p.sb("selm", [33, 33 * 128], F32)
    xt2 = p.sb("xt2", [128, D], F32)
    idb = p.sb("idb", [128, 128], BF16)
    idf = p.sb("idf", [128, 128], F32)

    ps_t = p.ps("ps_t", [128, 512], BF16)
    ps_m = p.ps("ps_m", [128, 512], F32)
    ps_m2 = p.ps("ps_m2", [128, 512], F32)
    ps_o = p.ps("ps_o", [128, 4, 512], F32)
    ps_tf = p.ps("ps_tf", [128, 512], F32)

    s_ld = p.sem("s_ld")
    s_st = p.sem("s_st")
    s_g = p.sem("s_g")
    s_gd = p.sem("s_gd")
    s_v = p.sem("s_v")
    s_a = p.sem("s_a")
    s_t = p.sem("s_t")
    s_cc = p.sem("s_cc")
    s_out = p.sem("s_out")

    blk = p.block()

    tail = ExitStack()
    wload = p.sb_into(tail, "wload", [128, 2 * D], F32)
    wload16 = p.sb_into(tail, "wload16", [128, 2 * D], BF16)
    winT = p.sb_into(tail, "winT", [128, 8, 2 * DIN], BF16)

    emit_identities(blk, s_g, idb, idf)

    @blk.gpsimd
    def _(g):
        s_g.chain(g, g.memset(selm[:], 0.0))
        s_g.inc(g.affine_select(out=selm[:], in_=selm[:], compare_op=ALU.not_equal,
                                fill=1.0, base=0,
                                pattern=[[-1, 33], [0, 128]], channel_multiplier=1))

    @blk.sync
    def _(s):
        s_ld.dma(s.dma_start(out=cbidx_sb[:], in_=cbidx_d))
        s_ld.dma(s.dma_start(out=smask_sb[:], in_=smask_d))
        s_ld.dma(s.dma_start(out=cw[:], in_=conv_w.rearrange("(a p) c -> p a c", p=128)))
        s_ld.dma(s.dma_start(out=asb[:], in_=a_log.rearrange("(a p) c -> p a c", p=128)))
        for dt in range(NDT):
            load_col(s, s_ld, cb_c[:, dt:dt + 1], conv_b[0:1, 128 * dt:128 * (dt + 1)])
            load_col(s, s_ld, wdt_c[:, dt:dt + 1], w_dt[0:1, 128 * dt:128 * (dt + 1)])
            load_col(s, s_ld, bdt_c[:, dt:dt + 1], b_dt[0:1, 128 * dt:128 * (dt + 1)])
            load_col(s, s_ld, dpar_c[:, dt:dt + 1], d_par[0:1, 128 * dt:128 * (dt + 1)])


    @blk.scalar
    def _(sc):
        s_ld.wait(sc)
        s_a.inc(sc.activation(out=acol[:], in_=asb[:], func=AF.Exp))

    @blk.vector
    def _(v):
        s_a.wait(v)
        v.tensor_scalar_mul(acol[:], acol[:], -1.0)
        v.memset(onesc[:], 1.0)
        v.memset(onesw[:], 1.0)
        s_v.inc(v.memset(cbnd[:], 0.0))

    emit_weightT(p, blk, w_in, winT, 32, 8,
                 (wload, wload16, ps_t, idb), (s_ld, s_v, s_t, s_g))
    emit_weightT(p, blk, w_out, woT, 8, 16,
                 (wload, wload16, ps_t, idb), (s_ld, s_v, s_t, s_g))

    for half in range(2):
        @blk.sync
        def _(s, half=half):
            s_v.wait(s)
            s_t.wait(s)
            s_ld.dma(s.dma_start(out=wload[0:33, 0:D],
                                 in_=w_x[:, D * half:D * (half + 1)]))

        @blk.vector
        def _(v, half=half):
            s_ld.wait(v)
            s_v.inc(v.tensor_copy(wload16[0:33, 0:D], wload[0:33, 0:D]))

        for d8 in range(8):
            @blk.tensor
            def _(t, half=half, d8=d8):
                s_v.wait(t)
                s_g.wait(t)
                ins = t.transpose(ps_t[:, 0:33],
                                  wload16[0:33, 128 * d8:128 * (d8 + 1)],
                                  idb[0:33, 0:33])
                s_t.inc(ins)

            @blk.vector
            def _(v, half=half, d8=d8):
                s_t.wait(v)
                s_v.inc(v.tensor_copy(wxT[:, half * 8 + d8, :], ps_t[:, 0:33]))

    emit_ln_pass(p, blk, x_in, lnD, None, (xt, sq, stat, ln16),
                 (s_ld, s_v, s_a, s_st, s_gd), TPC // 128, stage_last=False)

    for bs in range(NBS):
        for ti in range(4):
            @blk.sync
            def _(s, bs=bs, ti=ti):
                if bs == 0 and ti == 0:
                    s_st.wait(s)
                s_v.wait(s)
                s_a.wait(s)
                s_ld.dma(s.dma_start(out=ln16[:, 0, :],
                                     in_=lnD[bs, 1 + ti * 128:1 + (ti + 1) * 128, :]))

            for jj in range(2):
                @blk.tensor
                def _(t, jj=jj):
                    s_ld.wait(t)
                    s_v.wait(t)
                    s_g.wait(t)
                    for j4 in range(4):
                        j = jj * 4 + j4
                        ins = t.transpose(ps_t[:, 128 * j4:128 * (j4 + 1)],
                                          ln16[:, 0, 128 * j:128 * (j + 1)], idb[:])
                    s_t.inc(ins)

                @blk.vector
                def _(v, ti=ti, jj=jj):
                    s_t.wait(v)
                    for j4 in range(4):
                        j = jj * 4 + j4
                        ins = v.tensor_copy(lnTw[:, j, 128 * ti:128 * (ti + 1)],
                                            ps_t[:, 128 * j4:128 * (j4 + 1)])
                    s_v.inc(ins)

        for i in range(32):
            pb = i % 2

            @blk.tensor
            def _(t, i=i, pb=pb):
                s_v.wait(t)
                s_a.wait(t)
                s_st.wait(t)
                ps = ps_m if pb == 0 else ps_m2
                for j in range(8):
                    ins = t.matmul(ps[:, :], winT[:, j, 128 * i:128 * (i + 1)],
                                   lnTw[:, j, :], start=(j == 0), stop=(j == 7))
                s_t.inc(ins)

            if i < 16:
                @blk.vector
                def _(v, i=i, pb=pb):
                    s_t.wait(v)
                    s_st.wait(v)
                    ps = ps_m if pb == 0 else ps_m2
                    s_v.inc(v.tensor_copy(ztile[:, pb, :], ps[:, :]))

                @blk.scalar
                def _(sc, bs=bs, i=i, pb=pb):
                    s_v.wait(sc)
                    s_st.dma(sc.dma_start(out=xzpreD[bs, 128 * i:128 * (i + 1), :],
                                          in_=ztile[:, pb, :]))
                    s_a.inc(sc.nop())
            else:
                @blk.scalar
                def _(sc, bs=bs, i=i, pb=pb):
                    s_t.wait(sc)
                    s_st.wait(sc)
                    ps = ps_m if pb == 0 else ps_m2
                    s_a.inc(sc.activation(out=ztile[:, pb, :], in_=ps[:, :], func=AF.Silu))

                @blk.scalar
                def _(sc, bs=bs, i=i, pb=pb):
                    s_a.wait(sc)
                    s_st.dma(sc.dma_start(out=zD[bs, 128 * (i - 16):128 * (i - 15), :],
                                          in_=ztile[:, pb, :]))
                    s_a.inc(sc.nop())

    tail.close()

    convbuf = p.sb("convbuf", [128, 2, SL + 3], BF16)
    xcT = p.sb("xcT", [128, NDT, SL], BF16)
    xp = p.sb("xp", [33, SL], F32)
    Bb = p.sb("Bb", [128, NSTATE, SL], BF16)
    Cb = p.sb("Cb", [128, NSTATE, SL], BF16)
    deltaT = p.sb("deltaT", [128, 2, SL], F32)
    dsc1 = p.sb("dsc1", [128, SL], F32)
    dsc2 = p.sb("dsc2", [128, 2, SL], F32)
    dcv = p.sb("dcv", [128, 2, SL], F32)
    dpre_b = p.sb("dpre_b", [128, SL], F32)
    dAt = p.sb("dAt", [128, 2, SL], F32)
    m0t = p.sb("m0t", [128, SL], BF16)
    hseq = p.sb("hseq", [128, 2, SL], BF16)
    z1t = p.sb("z1t", [128, SL], BF16)
    ytile = p.sb("ytile", [128, 2, SL], F32)
    cstile = p.sb("cstile", [128, 2, SL], F32)
    dxc = p.sb("dxc", [128, SL], BF16)
    hstage = p.sb("hstage", [128, NDT, 17], F32)
    smbuf = p.sb("smbuf", [128, NDT, 17], F32)
    hinit_all = p.sb("hinit_all", [128, NDT * NSTATE], F32)
    hexp = p.sb("hexp", [128, NDT * NSTATE], F32)
    hmine = p.sb("hmine", [128, NBS, NDT, NSTATE], F32)
    corr = p.sb("corr", [128, SL], F32)
    sz_t = p.sb("sz_t", [128, 2, SL], BF16)
    xcr = p.sb("xcr", [128, 2, SL], BF16)
    yf_t = p.sb("yf_t", [128, 2, SL], BF16)
    outT = p.sb("outT_l1", [128, 8, SL], BF16)

    @blk.gpsimd
    def _(g):
        s_st.wait(g)
        for bs in range(NBS):
            for dt in range(NDT):
                s_gd.dma(g.dma_start(
                    out=cb_ag_in[bs * DIN + dt * 128:bs * DIN + (dt + 1) * 128, 0:3],
                    in_=xzpreD[bs, dt * 128:(dt + 1) * 128, SL - 3:SL]))
        s_gd.wait(g)
        s_cc.inc(g.collective_compute("AllGather", ALU.bypass,
                                      replica_groups=[list(range(NC))],
                                      ins=[cb_ag_in], outs=[cb_ag]))
        s_cc.wait(g)
        s_v.wait(g)
        breg = g.to_reg(NC * NBS * DIN - 1)
        for bs in range(NBS):
            for dt in range(NDT):
                s_gd.dma(g.indirect_dma_start(
                    out=cbnd[:, bs, dt, 0:3], out_offset=None, in_=cb_ag,
                    in_offset=IndirectOffsetOnAxis(
                        cbidx_sb[:, bs * NDT + dt:bs * NDT + dt + 1], axis=0),
                    bounds_check=breg, oob_is_err=False))
        s_gd.wait(g)

    s_cvl = p.sem("s_cvl")
    s_tap = p.sem("s_tap")
    s_silu = p.sem("s_silu")
    s_sp = p.sem("s_sp")
    s_dp = p.sem("s_dp")
    s_fin = p.sem("s_fin")
    s_exp = p.sem("s_exp")
    s_scan = p.sem("s_scan")
    s_gc = p.sem("s_gc")
    s_hq = p.sem("s_hq")
    s_ga = p.sem("s_ga")

    cvl0, tap0, sil0 = s_cvl.n, s_tap.n, s_silu.n
    sp0, dp0, fin0 = s_sp.n, s_dp.n, s_fin.n
    exp0, scan0, hq0, ga0, gc0 = s_exp.n, s_scan.n, s_hq.n, s_ga.n, s_gc.n
    gd_xc, gd_cs, gd_yp, gd_stg = {}, {}, {}, {}
    xp_t = {}

    def conv_load(g):
        bs_, dt_ = divmod(g, NDT)
        @blk.sync
        def _(s, bs=bs_, dt=dt_, g=g):
            if g == 0:
                s_st.wait(s)
            if g >= 2:
                s_tap.wait(s, tap0 + g - 1)
            s_cvl.dma(s.dma_start(out=convbuf[:, g % 2, 3:3 + SL],
                                  in_=xzpreD[bs, dt * 128:(dt + 1) * 128, :]))

    def conv_tap(g):
        bs_, dt_ = divmod(g, NDT)
        @blk.vector
        def _(v, bs=bs_, dt=dt_, g=g):
            if g == 0:
                s_gd.wait(v)
            s_cvl.wait(v, cvl0 + 16 * (g + 1))
            if g >= 2:
                s_silu.wait(v, sil0 + g - 1)
            s_v.chain(v, v.tensor_copy(convbuf[:, g % 2, 0:3],
                                       cbnd[:, bs, dt, 0:3]))
            s_v.chain(v, v.tensor_scalar(
                out=dcv[:, g % 2, :], in0=convbuf[:, g % 2, 0:SL],
                scalar1=cw[:, dt, 0:1], scalar2=None, op0=ALU.mult))
            for j in range(1, 4):
                ins = v.scalar_tensor_tensor(
                    out=dcv[:, g % 2, :], in0=convbuf[:, g % 2, j:j + SL],
                    scalar=cw[:, dt, j:j + 1], in1=dcv[:, g % 2, :],
                    op0=ALU.mult, op1=ALU.add)
                if j < 3:
                    s_v.chain(v, ins)
            s_tap.inc(ins)

    def emit_silu(sce, g):
        bs_, dt_ = divmod(g, NDT)
        s_tap.wait(sce, tap0 + g + 1)
        if bs_ >= 1:
            s_t.wait(sce, xp_t[bs_ - 1])
            s_fin.wait(sce, fin0 + (bs_ - 1) * NDT + dt_ + 1)
            s_gd.wait(sce, gd_xc[(bs_ - 1) * NDT + dt_])
        s_silu.inc(sce.activation(out=xcT[:, dt_, :], in_=dcv[:, g % 2, :],
                                  func=AF.Silu, bias=cb_c[:, dt_:dt_ + 1]))

    def xc_store(g):
        bs_, dt_ = divmod(g, NDT)
        @blk.gpsimd
        def _(gp, bs=bs_, dt=dt_, g=g):
            s_silu.wait(gp, sil0 + g + 1)
            s_gd.dma(gp.dma_start(out=xcD[bs, dt * 128:(dt + 1) * 128, :],
                                  in_=xcT[:, dt, :]))
            gd_xc[g] = s_gd.n

    for dt in range(NDT):
        conv_load(dt)
        conv_tap(dt)

    @blk.scalar
    def _(sce):
        for dt in range(NDT):
            emit_silu(sce, dt)

    for dt in range(NDT):
        xc_store(dt)

    for bs in range(NBS):
        g0 = bs * NDT
        @blk.tensor
        def _(t, bs=bs):
            s_silu.wait(t, sil0 + 16 * (bs + 1))
            s_v.wait(t)
            for dt in range(NDT):
                ins = t.matmul(ps_m[0:33, :], wxT[:, dt, :], xcT[:, dt, :],
                               start=(dt == 0), stop=(dt == NDT - 1))
            s_t.inc(ins)
            xp_t[bs] = s_t.n

        @blk.vector
        def _(v, bs=bs):
            s_t.wait(v)
            s_v.inc(v.tensor_copy(xp[:, :], ps_m[0:33, :]))

        @blk.gpsimd
        def _(gp, bs=bs):
            s_v.wait(gp)
            s_gd.dma(gp.dma_start(out=xpD[bs], in_=xp[:, :]))
            gd_stg[('xpd', bs)] = s_gd.n

        for n in range(NSTATE):
            @blk.tensor
            def _(t, bs=bs, n=n):
                s_v.wait(t)
                ins = t.matmul(ps_m[:, :], selm[:, 128 * n:128 * (n + 1)],
                               xp[:, :], start=True, stop=True)
                s_t.inc(ins)
                ins = t.matmul(ps_m2[:, :],
                               selm[:, 128 * (NSTATE + n):128 * (NSTATE + n + 1)],
                               xp[:, :], start=True, stop=True)
                s_t.inc(ins)

            @blk.vector
            def _(v, bs=bs, n=n):
                s_t.wait(v)
                v.tensor_copy(Bb[:, n, :], ps_m[:, :])
                s_v.inc(v.tensor_copy(Cb[:, n, :], ps_m2[:, :]))

        @blk.tensor
        def _(t, bs=bs):
            s_v.wait(t)
            ins = t.matmul(ps_m[:, :], selm[:, 128 * 32:128 * 33],
                           xp[:, :], start=True, stop=True)
            s_t.inc(ins)

        @blk.vector
        def _(v, bs=bs):
            s_t.wait(v)
            s_v.inc(v.tensor_copy(dpre_b[:], ps_m[:, :]))

        def emit_prep(g):
            bs_, dt_ = divmod(g, NDT)
            @blk.vector
            def _(v, bs=bs_, dt=dt_, g=g):
                if g >= 2:
                    s_exp.wait(v, exp0 + 16 * (g - 1))
                s_v.chain(v, v.tensor_scalar(out=deltaT[:, g % 2, :], in0=dpre_b[:],
                                             scalar1=wdt_c[:, dt:dt + 1],
                                             scalar2=bdt_c[:, dt:dt + 1],
                                             op0=ALU.mult, op1=ALU.add))
                s_v.chain(v, v.tensor_scalar_max(dsc1[:], deltaT[:, g % 2, :], 0.0))
                ins = v.scalar_tensor_tensor(out=dsc2[:, g % 2, :],
                                             in0=deltaT[:, g % 2, :], scalar=-1.0,
                                             in1=deltaT[:, g % 2, :],
                                             op0=ALU.mult, op1=ALU.max)
                s_dp.inc(ins)

        emit_prep(g0)

        for dt in range(NDT):
            g = g0 + dt
            u0 = 16 * g

            conv_load(g + NDT) if bs + 1 < NBS else None

            @blk.vector
            def _(v, bs=bs, dt=dt, g=g):
                s_sp.wait(v, sp0 + g + 1)
                if bs >= 1:
                    s_gd.wait(v, gd_stg[(bs - 1) * NDT + dt])
                if g >= 2:
                    s_gd.wait(v, gd_cs[g - 2])
                s_v.chain(v, v.tensor_add(deltaT[:, g % 2, :], dsc1[:],
                                          dsc2[:, g % 2, :]))
                s_v.chain(v, v.tensor_tensor_scan(out=cstile[:, g % 2, :],
                                                  data0=onesw[:],
                                                  data1=deltaT[:, g % 2, :],
                                                  initial=0.0,
                                                  op0=ALU.mult, op1=ALU.add))
                s_v.chain(v, v.tensor_mul(dxc[:], deltaT[:, g % 2, :],
                                          xcT[:, dt, :]))
                ins = v.tensor_copy(hstage[:, dt, 16:17],
                                    cstile[:, g % 2, SL - 1:SL])
                s_fin.inc(ins)

            if dt + 1 < NDT:
                emit_prep(g + 1)

            @blk.scalar
            def _(sce, bs=bs, dt=dt, g=g, u0=u0):
                s_dp.wait(sce, dp0 + g + 1)
                s_a.chain(sce, sce.activation(out=dsc2[:, g % 2, :],
                                              in_=dsc2[:, g % 2, :],
                                              func=AF.Exp, scale=-1.0))
                s_sp.inc(sce.activation(out=dsc2[:, g % 2, :],
                                        in_=dsc2[:, g % 2, :],
                                        func=AF.Ln, bias=1.0))
                s_fin.wait(sce, fin0 + g + 1)
                for n in range(NSTATE):
                    u = u0 + n
                    if u >= 2:
                        s_scan.wait(sce, scan0 + u - 1)
                    s_exp.inc(sce.activation(out=dAt[:, u % 2, :],
                                             in_=deltaT[:, g % 2, :], func=AF.Exp,
                                             scale=acol[:, dt, n:n + 1]))
                if bs + 1 < NBS:
                    emit_silu(sce, (bs + 1) * NDT + dt)

            @blk.vector
            def _(v, bs=bs, dt=dt, g=g, u0=u0):
                for n in range(NSTATE):
                    u = u0 + n
                    s_exp.wait(v, exp0 + u + 1)
                    if n == 0 and g >= 2:
                        s_gd.wait(v, gd_yp[g - 2])
                    s_v.chain(v, v.tensor_mul(m0t[:], dxc[:], Bb[:, n, :]))
                    s_v.chain(v, v.tensor_tensor_scan(out=hseq[:, u % 2, :],
                                                      data0=dAt[:, u % 2, :],
                                                      data1=m0t[:], initial=0.0,
                                                      op0=ALU.mult, op1=ALU.add))
                    s_v.chain(v, v.tensor_mul(z1t[:], Cb[:, n, :],
                                              hseq[:, u % 2, :]))
                    if n == 0:
                        s_v.chain(v, v.tensor_copy(ytile[:, g % 2, :], z1t[:]))
                    else:
                        s_v.chain(v, v.tensor_add(ytile[:, g % 2, :],
                                                  ytile[:, g % 2, :], z1t[:]))
                    s_scan.inc(v.tensor_copy(hstage[:, dt, n:n + 1],
                                             hseq[:, u % 2, SL - 1:SL]))

            if bs + 1 < NBS:
                conv_tap((bs + 1) * NDT + dt)

            @blk.gpsimd
            def _(gp, bs=bs, dt=dt, g=g):
                s_fin.wait(gp, fin0 + g + 1)
                s_gd.dma(gp.dma_start(out=csD[bs, dt * 128:(dt + 1) * 128, :],
                                      in_=cstile[:, g % 2, :]))
                gd_cs[g] = s_gd.n
                s_scan.wait(gp, scan0 + 16 * (g + 1))
                s_gd.dma(gp.dma_start(out=ypD[bs, dt * 128:(dt + 1) * 128, :],
                                      in_=ytile[:, g % 2, :]))
                gd_yp[g] = s_gd.n
                s_gd.dma(gp.dma_start(
                    out=sm_ag_in[bs * DIN + dt * 128:bs * DIN + (dt + 1) * 128,
                                 0:17],
                    in_=hstage[:, dt, :]))
                gd_stg[g] = s_gd.n

            if bs + 1 < NBS:
                xc_store((bs + 1) * NDT + dt)

    @blk.gpsimd
    def _(g):
        s_gd.wait(g)
        s_st.wait(g)
        s_cc.inc(g.collective_compute("AllGather", ALU.bypass,
                                      replica_groups=[list(range(NC))],
                                      ins=[sm_ag_in], outs=[sm_ag]))
        s_cc.wait(g)

    for b in range(B):
        @blk.vector
        def _(v, b=b):
            if b == 0:
                s_cc.wait(v)
                v.memset(hmine[:], 0.0)
            s_v.inc(v.memset(hinit_all[:], 0.0))

        for s_idx in range(SPB):
            r_s, slot_s = owner_of(s_idx)
            base = r_s * (NBS * DIN) + (b * 2 + slot_s) * DIN

            @blk.sync
            def _(s, base=base):
                s_v.wait(s)
                s_cc.wait(s)
                s_ld.dma(s.dma_start(
                    out=smbuf[:],
                    in_=sm_ag[base:base + DIN, 0:17].rearrange("(a p) c -> p a c", p=128)))

            @blk.vector
            def _(v, b=b, s_idx=s_idx):
                s_ld.wait(v)
                for slot in range(2):
                    s_v.chain(v, v.scalar_tensor_tensor(
                        out=hmine[:, b * 2 + slot, :, :].rearrange("p a c -> p (a c)"),
                        in0=hinit_all[:],
                        scalar=smask_sb[:, s_idx * 2 + slot:s_idx * 2 + slot + 1],
                        in1=hmine[:, b * 2 + slot, :, :].rearrange("p a c -> p (a c)"),
                        op0=ALU.mult, op1=ALU.add))
                dtot_rep = smbuf[:, :, 16:17].broadcast_to([128, NDT, NSTATE])
                s_v.inc(v.tensor_mul(hexp[:].rearrange("p (a c) -> p a c", a=NDT),
                                     acol[:, :, :], dtot_rep))

            @blk.scalar
            def _(sc):
                s_v.wait(sc)
                s_a.inc(sc.activation(out=hexp[:], in_=hexp[:], func=AF.Exp))

            @blk.vector
            def _(v):
                s_a.wait(v)
                s_v.chain(v, v.tensor_mul(hinit_all[:], hexp[:], hinit_all[:]))
                s_v.inc(v.tensor_add(
                    hinit_all[:].rearrange("p (a c) -> p a c", a=NDT),
                    hinit_all[:].rearrange("p (a c) -> p a c", a=NDT),
                    smbuf[:, :, 0:16]))


    s_ept = p.sem("s_ept")
    s_cor = p.sem("s_cor")
    s_ctl = p.sem("s_ctl")
    s_cl = p.sem("s_cl")
    ept0, cor0, ctl0, cl0 = s_ept.n, s_cor.n, s_ctl.n, s_cl.n
    ga1 = s_ga.n
    gd_yf = {}

    for bs in range(NBS):
        @blk.sync
        def _(s, bs=bs):
            s_t.wait(s)
            s_gd.wait(s, gd_stg[('xpd', bs)])
            s_ld.dma(s.dma_start(out=xp[:, :], in_=xpD[bs]))

        for n in range(NSTATE):
            @blk.tensor
            def _(t, bs=bs, n=n):
                s_v.wait(t)
                if n == 0:
                    s_ld.wait(t)
                ins = t.matmul(ps_m[:, :],
                               selm[:, 128 * (NSTATE + n):128 * (NSTATE + n + 1)],
                               xp[:, :], start=True, stop=True)
                s_t.inc(ins)

            @blk.vector
            def _(v, n=n):
                s_t.wait(v)
                s_v.inc(v.tensor_copy(Cb[:, n, :], ps_m[:, :]))

        for dt in range(NDT):
            h = bs * NDT + dt

            @blk.sync
            def _(s, bs=bs, dt=dt, h=h):
                if h == 0:
                    s_cc.wait(s)
                    s_st.wait(s)
                if h >= 2:
                    s_ept.wait(s, ept0 + 16 * (h - 1))
                    s_ctl.wait(s, ctl0 + h - 1)
                s_cl.dma(s.dma_start(out=cstile[:, h % 2, :],
                                     in_=csD[bs, dt * 128:(dt + 1) * 128, :]))
                s_cl.dma(s.dma_start(out=ytile[:, h % 2, :],
                                     in_=ypD[bs, dt * 128:(dt + 1) * 128, :]))
                s_cl.dma(s.dma_start(out=xcr[:, h % 2, :],
                                     in_=xcD[bs, dt * 128:(dt + 1) * 128, :]))
                s_cl.dma(s.dma_start(out=sz_t[:, h % 2, :],
                                     in_=zD[bs, dt * 128:(dt + 1) * 128, :]))

            @blk.scalar
            def _(sce, bs=bs, dt=dt, h=h):
                s_cl.wait(sce, cl0 + 64 * h + 16)
                for n in range(NSTATE):
                    u = 16 * h + n
                    if u >= 2:
                        s_cor.wait(sce, cor0 + u - 1)
                    s_ept.inc(sce.activation(out=dAt[:, u % 2, :],
                                             in_=cstile[:, h % 2, :], func=AF.Exp,
                                             scale=acol[:, dt, n:n + 1]))

            @blk.vector
            def _(v, bs=bs, dt=dt, h=h):
                for n in range(NSTATE):
                    u = 16 * h + n
                    s_ept.wait(v, ept0 + u + 1)
                    if n == 0:
                        s_cl.wait(v, cl0 + 64 * h + 32)
                    s_v.chain(v, v.scalar_tensor_tensor(
                        out=corr[:], in0=dAt[:, u % 2, :],
                        scalar=hmine[:, bs, dt, n:n + 1],
                        in1=Cb[:, n, :], op0=ALU.mult, op1=ALU.mult))
                    s_cor.inc(v.tensor_add(ytile[:, h % 2, :],
                                           ytile[:, h % 2, :], corr[:]))
                s_cl.wait(v, cl0 + 64 * (h + 1))
                if h >= 2:
                    s_gd.wait(v, gd_yf[h - 2])
                s_v.chain(v, v.scalar_tensor_tensor(
                    out=ytile[:, h % 2, :], in0=xcr[:, h % 2, :],
                    scalar=dpar_c[:, dt:dt + 1],
                    in1=ytile[:, h % 2, :], op0=ALU.mult, op1=ALU.add))
                s_ctl.inc(v.tensor_mul(yf_t[:, h % 2, :], ytile[:, h % 2, :],
                                       sz_t[:, h % 2, :]))

            @blk.gpsimd
            def _(gp, bs=bs, dt=dt, h=h):
                s_ctl.wait(gp, ctl0 + h + 1)
                s_gd.dma(gp.dma_start(out=yfD[bs, dt * 128:(dt + 1) * 128, :],
                                      in_=yf_t[:, h % 2, :]))
                gd_yf[h] = s_gd.n

    for bs in range(NBS):
        for rnd in range(2):
            for dt in range(NDT):
                @blk.sync
                def _(s, bs=bs, rnd=rnd, dt=dt):
                    s_t.wait(s)
                    if bs == 0 and rnd == 0 and dt == 0:
                        s_gd.wait(s)
                    s_ld.dma(s.dma_start(out=yf_t[:, dt % 2, :],
                                         in_=yfD[bs, dt * 128:(dt + 1) * 128, :]))

                @blk.tensor
                def _(t, bs=bs, rnd=rnd, dt=dt):
                    s_ld.wait(t)
                    if dt == 0:
                        s_v.wait(t)
                    for dm4 in range(4):
                        dm = rnd * 4 + dm4
                        ins = t.matmul(ps_o[:, dm4, :], woT[:, dt, 128 * dm:128 * (dm + 1)],
                                       yf_t[:, dt % 2, :], start=(dt == 0),
                                       stop=(dt == NDT - 1))
                    s_t.inc(ins)

            @blk.vector
            def _(v, bs=bs, rnd=rnd):
                s_t.wait(v)
                for dm4 in range(4):
                    ins = v.tensor_copy(outT[:, rnd * 4 + dm4, :], ps_o[:, dm4, :])
                s_v.inc(ins)

        for ti in range(4):
            @blk.sync
            def _(s, bs=bs, ti=ti):
                s_out.wait(s)
                row0 = bs * SL + ti * 128
                s_ld.dma(s.dma_start(out=xt2[:], in_=x_in[row0:row0 + 128, :]))

            for half in range(2):
                @blk.tensor
                def _(t, ti=ti, half=half):
                    s_v.wait(t)
                    for i4 in range(4):
                        i = half * 4 + i4
                        ins = t.transpose(ps_tf[:, 128 * i4:128 * (i4 + 1)] if False
                                          else ps_t[:, 128 * i4:128 * (i4 + 1)],
                                          outT[:, i, 128 * ti:128 * (ti + 1)], idb[:])
                    s_t.inc(ins)

                @blk.vector
                def _(v, ti=ti, half=half):
                    s_t.wait(v)
                    s_ld.wait(v)
                    for i4 in range(4):
                        i = half * 4 + i4
                        ins = v.tensor_add(xt2[:, 128 * i:128 * (i + 1)],
                                           xt2[:, 128 * i:128 * (i + 1)],
                                           ps_t[:, 128 * i4:128 * (i4 + 1)])
                    s_v.inc(ins)

            @blk.scalar
            def _(sc, bs=bs, ti=ti):
                s_v.wait(sc)
                row0 = bs * SL + ti * 128
                s_out.dma(sc.dma_start(out=y_out[row0:row0 + 128, :], in_=xt2[:]))

    if ext:
        return

    @blk.gpsimd
    def _(g):
        s_out.wait(g)

    return p.finish()


_L1ATTN_NC = None


def run_l1attn(xshards, W_in, conv_w, conv_b, W_x, w_dt, b_dt, A_log, D_param, W_out):
    global _L1ATTN_NC
    if _L1ATTN_NC is None:
        _L1ATTN_NC = build_l1attn()
    in_maps = []
    for c in range(NC):
        cbidx, smask = l1attn_tables(c)
        in_maps.append(dict(
            x=xshards[c], w_in=W_in, conv_w=conv_w.reshape(DIN, 4),
            conv_b=conv_b.reshape(1, DIN), w_x=W_x, w_dt=w_dt.reshape(1, DIN),
            b_dt=b_dt.reshape(1, DIN), a_log=A_log, d_par=D_param.reshape(1, DIN),
            w_out=W_out, cbidx=cbidx, smask=smask))
    res = run_bass_kernel_spmd(_L1ATTN_NC, in_maps, core_ids=list(range(NC)))
    return [np.asarray(res.results[c]["y_ma"]) for c in range(NC)]



NE = 8
KSEL = 8192
FWIN = 256
NTOK = NC * TPC
FF = 4096


def l1ffn_tables(core):
    p = np.arange(128)
    myslot_g = (core * 128 + p).astype(np.int32).reshape(128, 1)
    mycomb = np.zeros((128, NE), dtype=np.int32)
    for e in range(NE):
        mycomb[:, e] = e * 1024 + core * 128 + p
    return myslot_g, mycomb


def build_l1ffn(debug=False, ext=None):
    p = ext["p"] if ext else Prog()
    nc = p.nc

    x_in = ext["x_in"] if ext else p.dram("x", [TPC, D], F32, kind="ExternalInput")
    wg = p.dram("wg", [D, NE], F32, kind="ExternalInput")
    wg_my = p.dram("wg_my", [D, 1], F32, kind="ExternalInput")
    w1 = p.dram("w1", [D, FF], F32, kind="ExternalInput")
    w2 = p.dram("w2", [FF, D], F32, kind="ExternalInput")
    b1 = p.dram("b1", [1, FF], F32, kind="ExternalInput")
    b2 = p.dram("b2", [1, D], F32, kind="ExternalInput")
    msg_idx = p.dram("msg_idx", [128, 1], I32, kind="ExternalInput")
    mc_idx = p.dram("mc_idx", [128, NE], I32, kind="ExternalInput")
    y_out = ext["y_out"] if ext else p.dram("y_mo", [TPC, D], F32, kind="ExternalOutput")

    lnD = p.dram("lnD", [NBS, 1 + SL, D], BF16)
    lnfD = p.dram("lnfD", [TPC, D], F32)
    ln_ag_in = p.dram("ln_ag_in", [TPC, D], BF16)
    agLN = p.dram("agLN", [NTOK, D], BF16)
    s_ag_in = p.dram("s_ag_in", [TPC, NE], F32)
    agS = p.dram("agS", [NTOK, NE], F32)
    slotD = p.dram("slotD", [NE, NTOK], I32)
    myslotD = p.dram("myslotD", [NTOK], I32)
    myslotC = p.dram("myslotC", [NE, TPC], I32)
    XeD = p.dram("XeD", [KSEL, D], BF16)
    y_ag_in = p.dram("y_ag_in", [KSEL, D], BF16)
    agY = p.dram("agY", [NE * KSEL, D], BF16)

    if debug:
        dbg_slot = p.dram("dbg_slot", [NE, NTOK], I32, kind="ExternalOutput")
        dbg_s = p.dram("dbg_s", [NTOK, NE], F32, kind="ExternalOutput")
        dbg_xe = p.dram("dbg_xe", [KSEL, D], BF16, kind="ExternalOutput")
        dbg_yag = p.dram("dbg_yag", [KSEL, D], F32, kind="ExternalOutput")
        dbg_msl = p.dram("dbg_msl", [NTOK], I32, kind="ExternalOutput")
        dbg_bis = p.dram("dbg_bis", [1, 8 * NE], F32, kind="ExternalOutput")
        dbg_hib = p.dram("dbg_hib", [128, NE], F32, kind="ExternalOutput")
        dbg_cntp = p.dram("dbg_cntp", [128, NE], F32, kind="ExternalOutput")
        dbg_selg = p.dram("dbg_selg", [128, 256], F32, kind="ExternalOutput")
        dbg_slotf = p.dram("dbg_slotf", [128, 256], F32, kind="ExternalOutput")
        dbg_pcs = p.dram("dbg_pcs", [128, 256], F32, kind="ExternalOutput")
        dbg_sloti = p.dram("dbg_sloti", [128, 256], I32, kind="ExternalOutput")
    xt = p.sb("xt", [128, 2, D], F32)
    sq = p.sb("sq", [128, 2, D], F32)
    stat = p.sb("stat", [128, 8], F32)
    ln16 = p.sb("ln16", [128, 2, D], BF16)
    wgsb = p.sb("wgsb", [128, 8, NE], F32)
    wgmy = p.sb("wgmy", [128, 8, 1], BF16)
    b1c = p.sb("b1c", [128, 32], F32)
    b2c = p.sb("b2c", [128, 8], F32)
    msg_sb = p.sb("msg_sb", [128, 1], I32)
    mc_sb = p.sb("mc_sb", [128, NE], I32)
    bis = p.sb("bis", [1, 8 * NE], F32)
    bisd = p.sb("bisd", [1, NE], I32)
    cntp = p.sb("cntp", [128, NE], F32)
    midb = p.sb("midb", [128, NE], F32)
    lob = p.sb("lob", [128, NE], F32)
    hib = p.sb("hib", [128, NE], F32)
    ones1 = p.sb("ones1", [128, 1], F32)
    onesc = p.sb("onesc", [1, 128], F32)
    lstrict = p.sb("lstrict", [128, 128], F32)
    tots = p.sb("tots", [128, 1], F32)
    excl = p.sb("excl", [128, 1], F32)
    rneed_b = p.sb("rneed_b", [128, 1], F32)
    idxcol = p.sb("idxcol", [128, 4, 1], I32)
    xbl = p.sb("xbl", [128, 4, D], BF16)
    xt2 = p.sb("xt2", [128, 2, D], F32)
    idb = p.sb("idb", [128, 128], BF16)
    idf = p.sb("idf", [128, 128], F32)

    ga = ExitStack()

    def gA(name, shape, dtype):
        return p.sb_into(ga, name, shape, dtype)

    lnf = gA("lnf", [128, 2, D], F32)
    lnTwf = gA("lnTwf", [128, 8, SL], F32)
    stile = gA("stile", [8, 2, SL], F32)
    stok = gA("stok", [128, 2, 8], F32)
    ssb = gA("ssb", [128, 256, NE], F32)
    junk = gA("junk", [128, 256], F32)
    onesr = gA("onesr", [128, 256], F32)
    huget = gA("huget", [128, 256], F32)
    selg = gA("selg", [128, 256], F32)
    tiem = gA("tiem", [128, 256], F32)
    pcs = gA("pcs", [128, 256], F32)
    sranks = gA("sranks", [128, 256], F32)
    slotf = gA("slotf", [128, 256], F32)
    sloti = gA("sloti", [128, 256], I32)
    seli = gA("seli", [128, 256], I32)
    msg_cols = gA("msg_cols", [128, 256], I32)

    ps_t = p.ps("ps_t", [128, 512], BF16)
    ps_tf = p.ps("ps_tf", [128, 512], F32)
    ps_m = p.ps("ps_m", [128, 512], F32)
    ps_m2 = p.ps("ps_m2", [128, 512], F32)
    ps_m3 = p.ps("ps_m3", [128, 512], F32)

    s_ld = p.sem("s_ld")
    s_st = p.sem("s_st")
    s_g = p.sem("s_g")
    s_gd = p.sem("s_gd")
    s_v = p.sem("s_v")
    s_a = p.sem("s_a")
    s_t = p.sem("s_t")
    s_cc = p.sem("s_cc")
    s_out = p.sem("s_out")

    blk = p.block()
    emit_identities(blk, s_g, idb, idf)

    @blk.gpsimd
    def _(g):
        s_g.chain(g, g.memset(lstrict[:], 0.0))
        s_g.inc(g.affine_select(out=lstrict[:], in_=lstrict[:],
                                compare_op=ALU.is_ge, fill=1.0, base=0,
                                pattern=[[-1, 128]], channel_multiplier=1))
        for j in range(8):
            s_gd.dma(g.dma_start(out=wgmy[:, j, :], in_=wg_my[128 * j:128 * (j + 1), :]))

    @blk.sync
    def _(s):
        s_ld.dma(s.dma_start(out=msg_sb[:], in_=msg_idx))
        s_ld.dma(s.dma_start(out=mc_sb[:], in_=mc_idx))
        for j in range(8):
            s_ld.dma(s.dma_start(out=wgsb[:, j, :], in_=wg[128 * j:128 * (j + 1), :]))
        for j in range(32):
            load_col(s, s_ld, b1c[:, j:j + 1], b1[0:1, 128 * j:128 * (j + 1)])
        for j in range(8):
            load_col(s, s_ld, b2c[:, j:j + 1], b2[0:1, 128 * j:128 * (j + 1)])

    @blk.vector
    def _(v):
        v.memset(ones1[:], 1.0)
        v.memset(onesc[:], 1.0)
        v.memset(onesr[:], 1.0)
        s_v.inc(v.memset(huget[:], float(HUGE)))

    for k in range(TPC // 128):
        bs, ti = divmod(k, 4)
        pb = k % 2

        @blk.sync
        def _(s, k=k, pb=pb):
            s_v.wait(s)
            s_ld.dma(s.dma_start(out=xt[:, pb, :], in_=x_in[k * 128:(k + 1) * 128, :]))

        @blk.vector
        def _(v, pb=pb):
            s_ld.wait(v)
            s_a.wait(v)
            s_v.chain(v, v.tensor_reduce(out=stat[:, 0:1], in_=xt[:, pb, :],
                                         axis=AX.X, op=ALU.add))
            s_v.chain(v, v.tensor_scalar_mul(stat[:, 2:3], stat[:, 0:1], 1.0 / D))
            s_v.inc(v.tensor_scalar(out=sq[:, pb, :], in0=xt[:, pb, :],
                                    scalar1=stat[:, 2:3], scalar2=None,
                                    op0=ALU.subtract))

        @blk.scalar
        def _(sc, pb=pb):
            s_v.wait(sc)
            s_a.inc(sc.activation(out=sq[:, pb, :], in_=sq[:, pb, :], func=AF.Square,
                                  accum_out=stat[:, 1:2]))

        @blk.vector
        def _(v):
            s_a.wait(v)
            s_v.inc(v.tensor_scalar(out=stat[:, 3:4], in0=stat[:, 1:2],
                                    scalar1=1.0 / D, scalar2=EPS,
                                    op0=ALU.mult, op1=ALU.add))

        @blk.scalar
        def _(sc):
            s_v.wait(sc)
            s_a.inc(sc.activation(out=stat[:, 4:5], in_=stat[:, 3:4], func=AF.Sqrt))

        @blk.vector
        def _(v, pb=pb):
            s_a.wait(v)
            s_st.wait(v)
            s_v.chain(v, v.reciprocal(stat[:, 5:6], stat[:, 4:5]))
            s_v.chain(v, v.tensor_scalar(out=sq[:, pb, :], in0=xt[:, pb, :],
                                         scalar1=stat[:, 2:3], scalar2=None,
                                         op0=ALU.subtract))
            s_v.chain(v, v.tensor_scalar(out=lnf[:, pb, :], in0=sq[:, pb, :],
                                         scalar1=stat[:, 5:6], scalar2=None,
                                         op0=ALU.mult))
            s_v.inc(v.tensor_copy(ln16[:, pb, :], lnf[:, pb, :]))

        @blk.scalar
        def _(sc, k=k, pb=pb):
            s_v.wait(sc)
            s_st.dma(sc.dma_start(out=ln_ag_in[k * 128:(k + 1) * 128, :],
                                  in_=ln16[:, pb, :]))
            s_st.dma(sc.dma_start(out=lnfD[k * 128:(k + 1) * 128, :],
                                  in_=lnf[:, pb, :]))

    for bs in range(NBS):
        for ti in range(4):
            @blk.sync
            def _(s, bs=bs, ti=ti):
                s_v.wait(s)
                s_a.wait(s)
                if bs == 0 and ti == 0:
                    s_st.wait(s)
                row0 = bs * SL + ti * 128
                s_ld.dma(s.dma_start(out=lnf[:, 0, :], in_=lnfD[row0:row0 + 128, :]))

            for jj in range(2):
                @blk.tensor
                def _(t, jj=jj):
                    s_ld.wait(t)
                    s_v.wait(t)
                    s_g.wait(t)
                    for j4 in range(4):
                        j = jj * 4 + j4
                        ins = t.transpose(ps_tf[:, 128 * j4:128 * (j4 + 1)],
                                          lnf[:, 0, 128 * j:128 * (j + 1)], idf[:])
                    s_t.inc(ins)

                @blk.vector
                def _(v, ti=ti, jj=jj):
                    s_t.wait(v)
                    for j4 in range(4):
                        j = jj * 4 + j4
                        ins = v.tensor_copy(lnTwf[:, j, 128 * ti:128 * (ti + 1)],
                                            ps_tf[:, 128 * j4:128 * (j4 + 1)])
                    s_v.inc(ins)

        @blk.tensor
        def _(t, bs=bs):
            s_v.wait(t)
            for j in range(8):
                ins = t.matmul(ps_m[0:8, :], wgsb[:, j, :], lnTwf[:, j, :],
                               start=(j == 0), stop=(j == 7))
            s_t.inc(ins)

        @blk.vector
        def _(v, bs=bs):
            s_t.wait(v)
            s_a.wait(v)
            s_v.inc(v.tensor_copy(stile[:, bs % 2, :], ps_m[0:8, :]))

        for ti in range(4):
            @blk.tensor
            def _(t, bs=bs, ti=ti):
                s_v.wait(t)
                s_a.wait(t)
                ins = t.transpose(ps_tf[:, 0:8],
                                  stile[0:8, bs % 2, 128 * ti:128 * (ti + 1)],
                                  idf[0:8, 0:8])
                s_t.inc(ins)

            @blk.vector
            def _(v, bs=bs, ti=ti):
                s_t.wait(v)
                s_st.wait(v)
                s_v.inc(v.tensor_copy(stok[:, ti % 2, :], ps_tf[:, 0:8]))

            @blk.scalar
            def _(sc, bs=bs, ti=ti):
                s_v.wait(sc)
                row0 = bs * SL + ti * 128
                s_st.dma(sc.dma_start(out=s_ag_in[row0:row0 + 128, :],
                                      in_=stok[:, ti % 2, :]))

    @blk.gpsimd
    def _(g):
        s_st.wait(g)
        s_cc.inc(g.collective_compute("AllGather", ALU.bypass,
                                      replica_groups=[list(range(NC))],
                                      ins=[ln_ag_in], outs=[agLN]))
        s_cc.inc(g.collective_compute("AllGather", ALU.bypass,
                                      replica_groups=[list(range(NC))],
                                      ins=[s_ag_in], outs=[agS]))
        s_cc.wait(g)

    @blk.sync
    def _(s):
        s_cc.wait(s)
        s_ld.dma(s.dma_start(out=ssb[:],
                             in_=agS.rearrange("(p r) e -> p r e", p=128)))

    BIS_ITERS = 34
    def bcol(name, e):
        i = {"lo": 0, "hi": 1, "mid": 2, "cnt": 3, "dir": 4, "rneed": 5,
             "c1": 6}[name]
        return bis[0:1, i * NE + e:i * NE + e + 1]

    @blk.vector
    def _(v):
        s_ld.wait(v)
        v.memset(bis[0:1, 0 * NE:1 * NE], -100.0)
        s_v.inc(v.memset(bis[0:1, 1 * NE:2 * NE], 100.0))

    for it in range(BIS_ITERS):
        @blk.vector
        def _(v, it=it):
            if it > 0:
                s_t.wait(v)
                s_v.chain(v, v.tensor_copy(bis[0:1, 3 * NE:4 * NE], ps_m[0:1, 0:NE]))
                s_v.chain(v, v.tensor_scalar(out=bisd[0:1, :],
                                             in0=bis[0:1, 3 * NE:4 * NE],
                                             scalar1=float(KSEL), scalar2=None,
                                             op0=ALU.is_ge))
                s_v.chain(v, v.select(bis[0:1, 0 * NE:1 * NE], bisd[0:1, :],
                                      bis[0:1, 2 * NE:3 * NE], bis[0:1, 0 * NE:1 * NE]))
                s_v.chain(v, v.select(bis[0:1, 1 * NE:2 * NE], bisd[0:1, :],
                                      bis[0:1, 1 * NE:2 * NE], bis[0:1, 2 * NE:3 * NE]))
            s_v.chain(v, v.tensor_add(bis[0:1, 2 * NE:3 * NE], bis[0:1, 0 * NE:1 * NE],
                                      bis[0:1, 1 * NE:2 * NE]))
            s_v.inc(v.tensor_scalar_mul(bis[0:1, 2 * NE:3 * NE],
                                        bis[0:1, 2 * NE:3 * NE], 0.5))

        @blk.tensor
        def _(t, it=it):
            s_v.wait(t)
            ins = t.matmul(ps_m2[:, 0:NE], onesc[:, :], bis[0:1, 2 * NE:3 * NE],
                           start=True, stop=True)
            s_t.inc(ins)

        @blk.vector
        def _(v, it=it):
            s_t.wait(v)
            s_v.chain(v, v.tensor_copy(midb[:], ps_m2[:, 0:NE]))
            for e in range(NE):
                s_v.chain(v, v.tensor_scalar(
                    out=junk[:], in0=ssb[:, :, e], scalar1=midb[:, e:e + 1],
                    scalar2=0.0, op0=ALU.is_gt, op1=ALU.add,
                    accum_out=cntp[:, e:e + 1]))
            s_v.inc(v.engine_nop())

        @blk.tensor
        def _(t, it=it):
            s_v.wait(t)
            ins = t.matmul(ps_m[0:1, 0:NE], ones1[:, 0:1],
                           cntp[:, :], start=True, stop=True)
            s_t.inc(ins)

    @blk.vector
    def _(v):
        s_t.wait(v)
        s_v.chain(v, v.tensor_copy(bis[0:1, 3 * NE:4 * NE], ps_m[0:1, 0:NE]))
        s_v.chain(v, v.tensor_scalar(out=bisd[0:1, :],
                                     in0=bis[0:1, 3 * NE:4 * NE],
                                     scalar1=float(KSEL), scalar2=None,
                                     op0=ALU.is_ge))
        s_v.chain(v, v.select(bis[0:1, 0 * NE:1 * NE], bisd[0:1, :],
                              bis[0:1, 2 * NE:3 * NE], bis[0:1, 0 * NE:1 * NE]))
        s_v.inc(v.select(bis[0:1, 1 * NE:2 * NE], bisd[0:1, :],
                         bis[0:1, 1 * NE:2 * NE], bis[0:1, 2 * NE:3 * NE]))

    @blk.tensor
    def _(t):
        s_v.wait(t)
        ins = t.matmul(ps_m2[:, 0:NE], onesc[:, :], bis[0:1, 0 * NE:1 * NE],
                       start=True, stop=True)
        s_t.inc(ins)
        ins = t.matmul(ps_tf[:, 0:NE], onesc[:, :], bis[0:1, 1 * NE:2 * NE],
                       start=True, stop=True)
        s_t.inc(ins)

    @blk.vector
    def _(v):
        s_t.wait(v)
        v.tensor_copy(lob[:], ps_m2[:, 0:NE])
        s_v.chain(v, v.tensor_copy(hib[:], ps_tf[:, 0:NE]))
        for e in range(NE):
            s_v.chain(v, v.tensor_scalar(
                out=junk[:], in0=ssb[:, :, e], scalar1=hib[:, e:e + 1],
                scalar2=0.0, op0=ALU.is_gt, op1=ALU.add,
                accum_out=cntp[:, e:e + 1]))
        s_v.inc(v.engine_nop())

    @blk.tensor
    def _(t):
        s_v.wait(t)
        ins = t.matmul(ps_m[0:1, 0:NE], ones1[:, 0:1], cntp[:, :],
                       start=True, stop=True)
        s_t.inc(ins)

    @blk.vector
    def _(v):
        s_t.wait(v)
        s_v.chain(v, v.tensor_copy(bis[0:1, 6 * NE:7 * NE], ps_m[0:1, 0:NE]))
        s_v.chain(v, v.tensor_scalar(out=bis[0:1, 5 * NE:6 * NE],
                                     in0=bis[0:1, 6 * NE:7 * NE],
                                     scalar1=-1.0, scalar2=float(KSEL),
                                     op0=ALU.mult, op1=ALU.add))
        s_v.inc(v.engine_nop())

    for e in range(NE):
        @blk.tensor
        def _(t, e=e):
            s_v.wait(t)
            ins = t.matmul(ps_m2[:, 0:1], onesc[:, :], bcol("rneed", e),
                           start=True, stop=True)
            s_t.inc(ins)

        @blk.vector
        def _(v, e=e):
            s_t.wait(v)
            s_v.chain(v, v.tensor_copy(rneed_b[:], ps_m2[:, 0:1]))
            s_v.chain(v, v.tensor_scalar(out=selg[:], in0=ssb[:, :, e],
                                         scalar1=hib[:, e:e + 1], scalar2=None,
                                         op0=ALU.is_gt))
            s_v.chain(v, v.tensor_scalar(out=tiem[:], in0=ssb[:, :, e],
                                         scalar1=lob[:, e:e + 1], scalar2=None,
                                         op0=ALU.is_gt))
            s_v.chain(v, v.tensor_sub(tiem[:], tiem[:], selg[:]))
            s_v.chain(v, v.tensor_tensor_scan(out=pcs[:], data0=onesr[:],
                                              data1=tiem[:], initial=0.0,
                                              op0=ALU.mult, op1=ALU.add))
            s_v.inc(v.tensor_copy(tots[:], pcs[:, 255:256]))

        @blk.tensor
        def _(t, e=e):
            s_v.wait(t)
            ins = t.matmul(ps_m2[:, 0:1], lstrict[:, :], tots[:, 0:1],
                           start=True, stop=True)
            s_t.inc(ins)

        @blk.vector
        def _(v, e=e):
            s_t.wait(v)
            s_v.chain(v, v.tensor_copy(excl[:], ps_m2[:, 0:1]))
            s_v.chain(v, v.tensor_scalar(out=sranks[:], in0=pcs[:],
                                         scalar1=excl[:, 0:1], scalar2=-1.0,
                                         op0=ALU.add, op1=ALU.add))
            s_v.chain(v, v.tensor_scalar(out=sranks[:], in0=sranks[:],
                                         scalar1=rneed_b[:, 0:1], scalar2=None,
                                         op0=ALU.is_lt))
            s_v.chain(v, v.tensor_mul(sranks[:], sranks[:], tiem[:]))
            s_v.chain(v, v.tensor_add(selg[:], selg[:], sranks[:]))
            s_v.chain(v, v.tensor_tensor_scan(out=pcs[:], data0=onesr[:],
                                              data1=selg[:], initial=0.0,
                                              op0=ALU.mult, op1=ALU.add))
            s_v.inc(v.tensor_copy(tots[:], pcs[:, 255:256]))

        @blk.tensor
        def _(t, e=e):
            s_v.wait(t)
            ins = t.matmul(ps_m2[:, 0:1], lstrict[:, :], tots[:, 0:1],
                           start=True, stop=True)
            s_t.inc(ins)

        @blk.vector
        def _(v, e=e):
            s_t.wait(v)
            s_v.chain(v, v.tensor_copy(excl[:], ps_m2[:, 0:1]))
            s_v.chain(v, v.tensor_scalar(out=slotf[:], in0=pcs[:],
                                         scalar1=excl[:, 0:1], scalar2=-1.0,
                                         op0=ALU.add, op1=ALU.add))
            s_v.chain(v, v.tensor_scalar(out=sranks[:], in0=slotf[:],
                                         scalar1=-float(HUGE), scalar2=None,
                                         op0=ALU.add))
            s_v.chain(v, v.tensor_mul(sranks[:], sranks[:], selg[:]))
            s_v.chain(v, v.tensor_scalar(out=slotf[:], in0=sranks[:],
                                         scalar1=float(HUGE), scalar2=None,
                                         op0=ALU.add))
            s_v.inc(v.tensor_copy(sloti[:], slotf[:]))

        @blk.scalar
        def _(sc, e=e):
            s_v.wait(sc)
            s_st.dma(sc.dma_start(out=slotD[e].rearrange("(p r) -> p r", p=128),
                                  in_=sloti[:]))

    @blk.gpsimd
    def _(g):
        s_st.wait(g)
        s_ld.wait(g)
        breg = g.to_reg(NE * 128 - 1)
        bregs2 = g.to_reg(NE * 1024 - 1)
        s_gd.dma(g.indirect_dma_start(
            out=msg_cols[:], out_offset=None,
            in_=slotD.rearrange("a b -> (a b)").unsqueeze(1) if False
            else slotD.rearrange("e (q c) -> (e q) c", c=256),
            in_offset=IndirectOffsetOnAxis(msg_sb[:, 0:1], axis=0),
            bounds_check=breg, oob_is_err=False))
        s_gd.wait(g)
        s_gd.dma(g.dma_start(out=myslotD.rearrange("(p r) -> p r", p=128),
                             in_=msg_cols[:]))
        for e in range(NE):
            s_gd.dma(g.indirect_dma_start(
                out=msg_cols[:, 32 * e:32 * (e + 1)], out_offset=None,
                in_=slotD.rearrange("e (q c) -> (e q) c", c=32),
                in_offset=IndirectOffsetOnAxis(mc_sb[:, e:e + 1], axis=0),
                bounds_check=bregs2, oob_is_err=False))
        s_gd.wait(g)
        for e in range(NE):
            s_gd.dma(g.dma_start(out=myslotC[e].rearrange("(p r) -> p r", p=128),
                                 in_=msg_cols[:, 32 * e:32 * (e + 1)]))
        s_gd.wait(g)

    @blk.gpsimd
    def _(g):
        breg3 = g.to_reg(KSEL - 1)
        nblk = NTOK // 128
        ld, sc = {}, {}
        for pb2 in range(min(4, nblk)):
            s_gd.dma(g.dma_start(
                out=idxcol[:, pb2 % 4, :],
                in_=myslotD[pb2 * 128:(pb2 + 1) * 128].unsqueeze(1)))
            s_gd.dma(g.dma_start(out=xbl[:, pb2 % 4, :],
                                 in_=agLN[pb2 * 128:(pb2 + 1) * 128, :]))
            ld[pb2] = s_gd.n
        for pb2 in range(nblk):
            g.wait_ge(s_gd.h, ld[pb2])
            s_gd.dma(g.indirect_dma_start(
                out=XeD, out_offset=IndirectOffsetOnAxis(idxcol[:, pb2 % 4, :], axis=0),
                in_=xbl[:, pb2 % 4, :], in_offset=None,
                bounds_check=breg3, oob_is_err=False))
            sc[pb2] = s_gd.n
            prev = pb2 - 1
            nxt = prev + 4
            if prev >= 0 and nxt < nblk:
                g.wait_ge(s_gd.h, sc[prev])
                s_gd.dma(g.dma_start(
                    out=idxcol[:, nxt % 4, :],
                    in_=myslotD[nxt * 128:(nxt + 1) * 128].unsqueeze(1)))
                s_gd.dma(g.dma_start(out=xbl[:, nxt % 4, :],
                                     in_=agLN[nxt * 128:(nxt + 1) * 128, :]))
                ld[nxt] = s_gd.n
        s_gd.wait(g)

    if debug:
        @blk.gpsimd
        def _(g):
            s_out.wait(g) if False else None
            s_gd.dma(g.dma_start(out=dbg_slot, in_=slotD))
            s_gd.dma(g.dma_start(out=dbg_s, in_=agS))
            s_gd.dma(g.dma_start(out=dbg_xe, in_=XeD))
            s_gd.dma(g.dma_start(out=dbg_msl, in_=myslotD))
            s_gd.dma(g.dma_start(out=dbg_bis, in_=bis[:]))
            s_gd.dma(g.dma_start(out=dbg_hib, in_=hib[:]))
            s_gd.dma(g.dma_start(out=dbg_cntp, in_=cntp[:]))
            s_gd.dma(g.dma_start(out=dbg_selg, in_=selg[:]))
            s_gd.dma(g.dma_start(out=dbg_slotf, in_=slotf[:]))
            s_gd.dma(g.dma_start(out=dbg_pcs, in_=pcs[:]))
            s_gd.dma(g.dma_start(out=dbg_sloti, in_=sloti[:]))
            s_gd.wait(g)
        ga.close()
        return p.finish()

    ga.close()

    gb = ExitStack()
    w1sb = p.sb_into(gb, "w1sb", [128, 8, FF], BF16)
    w2sb = p.sb_into(gb, "w2sb", [128, 32, D], BF16)
    xeT = p.sb_into(gb, "xeT", [128, 8, FWIN], BF16)
    xet_tok = p.sb_into(gb, "xet_tok", [128, 2, D], BF16)
    hidT = p.sb_into(gb, "hidT", [128, 32, FWIN], BF16)
    grow = p.sb_into(gb, "grow", [1, FWIN], F32)
    gbs = p.sb_into(gb, "gbs", [128, FWIN], F32)
    otmp = p.sb_into(gb, "otmp", [128, FWIN], F32)
    oT = p.sb_into(gb, "oT", [128, 8, FWIN], BF16)
    otok = p.sb_into(gb, "otok", [128, 2, D], BF16)

    @blk.gpsimd
    def _(g):
        for j in range(8):
            s_gd.dma(g.dma_start(out=w1sb[:, j, :], in_=w1[128 * j:128 * (j + 1), :]))
        for j in range(32):
            s_gd.dma(g.dma_start(out=w2sb[:, j, :], in_=w2[128 * j:128 * (j + 1), :]))

    for w in range(KSEL // FWIN):
        for ti in range(2):
            @blk.sync
            def _(s, w=w, ti=ti):
                s_v.wait(s)
                s_a.wait(s)
                if w == 0 and ti == 0:
                    s_gd.wait(s)
                row0 = w * FWIN + ti * 128
                s_ld.dma(s.dma_start(out=xet_tok[:, ti % 2, :],
                                     in_=XeD[row0:row0 + 128, :]))

            for jj in range(2):
                @blk.tensor
                def _(t, ti=ti, jj=jj):
                    s_ld.wait(t)
                    s_v.wait(t)
                    s_g.wait(t)
                    for j4 in range(4):
                        j = jj * 4 + j4
                        ins = t.transpose(ps_t[:, 128 * j4:128 * (j4 + 1)],
                                          xet_tok[:, ti % 2, 128 * j:128 * (j + 1)],
                                          idb[:])
                    s_t.inc(ins)

                @blk.vector
                def _(v, ti=ti, jj=jj):
                    s_t.wait(v)
                    for j4 in range(4):
                        j = jj * 4 + j4
                        ins = v.tensor_copy(xeT[:, j, 128 * ti:128 * (ti + 1)],
                                            ps_t[:, 128 * j4:128 * (j4 + 1)])
                    s_v.inc(ins)

        @blk.tensor
        def _(t, w=w):
            s_v.wait(t)
            s_gd.wait(t)
            for j in range(8):
                ins = t.matmul(ps_m2[0:1, 0:FWIN], wgmy[:, j, :], xeT[:, j, :],
                               start=(j == 0), stop=(j == 7))
            s_t.inc(ins)

        @blk.scalar
        def _(sc, w=w):
            s_t.wait(sc)
            s_a.inc(sc.activation(out=grow[:], in_=ps_m2[0:1, 0:FWIN], func=AF.Sigmoid))

        a_hid0 = s_a.n

        @blk.tensor
        def _(t, w=w):
            s_a.wait(t, a_hid0)
            ins = t.matmul(ps_m2[:, 0:FWIN], onesc[:, :], grow[:, :],
                           start=True, stop=True)
            s_t.inc(ins)

        @blk.vector
        def _(v, w=w):
            s_t.wait(v)
            s_v.inc(v.tensor_copy(gbs[:], ps_m2[:, 0:FWIN]))

        v_w2 = s_v.n

        for fb in range(32):
            pm = ps_m if fb % 2 == 0 else ps_m3

            @blk.tensor
            def _(t, w=w, fb=fb, pm=pm):
                s_a.wait(t, a_hid0 + fb - 1 if fb >= 2 else a_hid0 - 1)
                for j in range(8):
                    ins = t.matmul(pm[:, 0:FWIN], w1sb[:, j, 128 * fb:128 * (fb + 1)],
                                   xeT[:, j, :], start=(j == 0), stop=(j == 7))
                s_t.inc(ins)

            @blk.scalar
            def _(sc, w=w, fb=fb, pm=pm):
                s_t.wait(sc)
                s_a.inc(sc.activation(out=hidT[:, fb, :], in_=pm[:, 0:FWIN],
                                      func=AF.Gelu, bias=b1c[:, fb:fb + 1]))

        a_w2 = s_a.n

        for hb in range(8):
            pm = ps_m if hb % 2 == 0 else ps_m3

            @blk.tensor
            def _(t, w=w, hb=hb, pm=pm):
                if hb == 0:
                    s_a.wait(t, a_w2)
                    s_v.wait(t, v_w2)
                elif hb >= 2:
                    s_v.wait(t, v_w2 + 2 * (hb - 1))
                for j in range(32):
                    ins = t.matmul(pm[:, 0:FWIN], w2sb[:, j, 128 * hb:128 * (hb + 1)],
                                   hidT[:, j, :], start=(j == 0), stop=(j == 31))
                s_t.inc(ins)

            @blk.vector
            def _(v, w=w, hb=hb, pm=pm):
                s_t.wait(v)
                s_v.chain(v, v.tensor_scalar(out=otmp[:], in0=pm[:, 0:FWIN],
                                             scalar1=b2c[:, hb:hb + 1], scalar2=None,
                                             op0=ALU.add))
                s_v.inc(v.tensor_mul(oT[:, hb, :], otmp[:], gbs[:]))

        for ti in range(2):
            for half in range(2):
                @blk.tensor
                def _(t, w=w, ti=ti, half=half):
                    s_v.wait(t)
                    s_gd.wait(t)
                    for i4 in range(4):
                        i = half * 4 + i4
                        ins = t.transpose(ps_t[:, 128 * i4:128 * (i4 + 1)],
                                          oT[:, i, 128 * ti:128 * (ti + 1)], idb[:])
                    s_t.inc(ins)

                @blk.vector
                def _(v, ti=ti, half=half):
                    s_t.wait(v)
                    for i4 in range(4):
                        i = half * 4 + i4
                        ins = v.tensor_copy(otok[:, ti % 2, 128 * i:128 * (i + 1)],
                                            ps_t[:, 128 * i4:128 * (i4 + 1)])
                    s_v.inc(ins)

            @blk.gpsimd
            def _(g, w=w, ti=ti):
                s_v.wait(g)
                row0 = w * FWIN + ti * 128
                s_gd.dma(g.dma_start(out=y_ag_in[row0:row0 + 128, :],
                                     in_=otok[:, ti % 2, :]))

    gb.close()
    accbuf = p.sb("accbuf", [128, NE, D], BF16)
    idxall = p.sb("idxall", [128, 4, NE], I32)
    ebias = p.sb("ebias", [128, NE], I32)

    @blk.gpsimd
    def _(g):
        s_st.wait(g)
        s_cc.inc(g.collective_compute("AllGather", ALU.bypass,
                                      replica_groups=[list(range(NC))],
                                      ins=[y_ag_in], outs=[agY]))
        s_cc.wait(g)

    bregs_c = {}
    nkt = TPC // 128
    idx_cnt, gth_cnt, v_cnt = {}, {}, {}

    def emit_idx_loads(g, k):
        for e in range(NE):
            s_gd.dma(g.dma_start(out=idxall[:, k % 4, e:e + 1],
                                 in_=myslotC[e, k * 128:(k + 1) * 128].unsqueeze(1)))
        idx_cnt[k] = s_gd.n

    @blk.gpsimd
    def _(g):
        s_cc.wait(g)
        bregs_c["y"] = g.to_reg(NE * KSEL - 1)
        s_g.chain(g, g.iota(ebias[:], pattern=[[KSEL, NE]], base=0,
                            channel_multiplier=0))
        emit_idx_loads(g, 0)
        if nkt > 1:
            emit_idx_loads(g, 1)

    @blk.vector
    def _(v):
        s_gd.wait(v)
        s_st.wait(v)
        s_v.inc(v.memset(accbuf[:].rearrange("p a b -> p (a b)"), 0.0))

    for k in range(nkt):
        pb = k % 2

        @blk.gpsimd
        def _(g, k=k, pb=pb):
            g.wait_ge(s_gd.h, idx_cnt[k])
            s_g.chain(g, g.tensor_tensor(out=idxall[:, k % 4, :],
                                         in0=idxall[:, k % 4, :],
                                         in1=ebias[:], op=ALU.add))
            if k >= 1:
                s_v.wait(g, v_cnt[k - 1])
            else:
                s_v.wait(g)
            for e in range(NE):
                s_gd.dma(g.indirect_dma_start(
                    out=accbuf[:, e, :], out_offset=None, in_=agY,
                    in_offset=IndirectOffsetOnAxis(idxall[:, k % 4, e:e + 1],
                                                   axis=0),
                    bounds_check=bregs_c["y"], oob_is_err=False))
            gth_cnt[k] = s_gd.n
            if k + 2 < nkt:
                emit_idx_loads(g, k + 2)

        @blk.sync
        def _(s, k=k, pb=pb):
            s_out.wait(s)
            s_ld.dma(s.dma_start(out=xt2[:, pb, :], in_=x_in[k * 128:(k + 1) * 128, :]))

        @blk.vector
        def _(v, k=k, pb=pb):
            s_ld.wait(v)
            v.wait_ge(s_gd.h, gth_cnt[k])
            for st in (1, 2, 4):
                for e in range(0, NE, 2 * st):
                    s_v.chain(v, v.tensor_add(accbuf[:, e, :], accbuf[:, e, :],
                                              accbuf[:, e + st, :]))
            s_v.chain(v, v.tensor_copy(sq[:, pb, :], accbuf[:, 0, :]))
            s_v.chain(v, v.memset(accbuf[:].rearrange("p a b -> p (a b)"), 0.0))
            v_cnt[k] = s_v.n
            ins = v.scalar_tensor_tensor(out=xt2[:, pb, :], in0=sq[:, pb, :],
                                         scalar=0.5, in1=xt2[:, pb, :],
                                         op0=ALU.mult, op1=ALU.add)
            s_v.inc(ins)

        @blk.scalar
        def _(sc, k=k, pb=pb):
            s_v.wait(sc)
            s_out.dma(sc.dma_start(out=y_out[k * 128:(k + 1) * 128, :],
                                   in_=xt2[:, pb, :]))

    if ext:
        return

    @blk.gpsimd
    def _(g):
        s_out.wait(g)
        s_gd.wait(g)

    return p.finish()


_L1FFN_NC = None


def run_l1ffn(xshards, Wg, W1e_list, b1e_list, W2e_list, b2e_list):
    global _L1FFN_NC
    if _L1FFN_NC is None:
        _L1FFN_NC = build_l1ffn()
    in_maps = []
    for c in range(NC):
        msg, mcomb = l1ffn_tables(c)
        in_maps.append(dict(
            x=xshards[c], wg=Wg, wg_my=np.ascontiguousarray(Wg[:, c:c + 1]),
            w1=W1e_list[c], b1=b1e_list[c].reshape(1, FF),
            w2=W2e_list[c], b2=b2e_list[c].reshape(1, D),
            msg_idx=msg, mc_idx=mcomb))
    res = run_bass_kernel_spmd(_L1FFN_NC, in_maps, core_ids=list(range(NC)))
    return [np.asarray(res.results[c]["y_mo"]) for c in range(NC)]




def kernel_unfused(L0_attn_x, L0_attn_w_q, L0_attn_w_k, L0_attn_w_v,
           L0_ffn_W_r, L0_ffn_W_k, L0_ffn_W_v, L0_ffn_mu_r, L0_ffn_mu_k,
           L1_attn_W_in, L1_attn_conv_w, L1_attn_conv_b, L1_attn_W_x,
           L1_attn_w_dt, L1_attn_b_dt, L1_attn_A_log, L1_attn_D_param,
           L1_attn_W_out, L1_ffn_Wg, L1_ffn_W1, L1_ffn_b1, L1_ffn_W2,
           L1_ffn_b2):
    f = np.asarray
    x = np.ascontiguousarray(f(L0_attn_x), dtype=np.float32)
    sh = shard_x(x)
    w1l = [np.ascontiguousarray(f(L1_ffn_W1)[c]) for c in range(NE)]
    b1l = [np.ascontiguousarray(f(L1_ffn_b1)[c]) for c in range(NE)]
    w2l = [np.ascontiguousarray(f(L1_ffn_W2)[c]) for c in range(NE)]
    b2l = [np.ascontiguousarray(f(L1_ffn_b2)[c]) for c in range(NE)]

    sh = run_l0ffn(sh, f(L0_ffn_W_r), f(L0_ffn_W_k), f(L0_ffn_W_v),
                   f(L0_ffn_mu_r), f(L0_ffn_mu_k))
    sh = run_l0attn(sh, f(L0_attn_w_q), f(L0_attn_w_k), f(L0_attn_w_v))
    sh = run_l0ffn(sh, f(L0_ffn_W_r), f(L0_ffn_W_k), f(L0_ffn_W_v),
                   f(L0_ffn_mu_r), f(L0_ffn_mu_k))
    sh = run_l1ffn(sh, f(L1_ffn_Wg), w1l, b1l, w2l, b2l)
    sh = run_l1attn(sh, f(L1_attn_W_in), f(L1_attn_conv_w), f(L1_attn_conv_b),
                    f(L1_attn_W_x), f(L1_attn_w_dt), f(L1_attn_b_dt),
                    f(L1_attn_A_log), f(L1_attn_D_param), f(L1_attn_W_out))
    sh = run_l1ffn(sh, f(L1_ffn_Wg), w1l, b1l, w2l, b2l)
    return unshard_x(sh)



def emit_barrier(p, blk):
    sems = [p.sem(n) for n in ("s_ld", "s_st", "s_g", "s_gd", "s_v", "s_a",
                               "s_t", "s_cc", "s_out")]
    sbar = p.sem("s_bar")

    @blk.gpsimd
    def _(g):
        for sm in sems:
            sm.wait(g)
        sbar.inc(g.nop())

    for sec in (blk.sync, blk.vector, blk.scalar, blk.tensor):
        @sec
        def _(eng):
            sbar.wait(eng)


def build_fused():
    p = Prog()
    x_in = p.dram("x", [TPC, D], F32, kind="ExternalInput")
    y_out = p.dram("y", [TPC, D], F32, kind="ExternalOutput")
    xI0 = p.dram("xI0", [TPC, D], F32)
    xI1 = p.dram("xI1", [TPC, D], F32)
    blk = p.block()

    chain_io = [(x_in, xI0), (xI0, xI1), (xI1, xI0), (xI0, xI1), (xI1, xI0),
                (xI0, y_out)]
    builders = [build_l0ffn, build_l0attn, build_l0ffn, build_l1ffn,
                build_l1attn, build_l1ffn]
    for (xi, yo), bld in zip(chain_io, builders):
        p.push_scope()
        bld(ext=dict(p=p, x_in=xi, y_out=yo))
        p.pop_scope()
        emit_barrier(p, blk)

    s_out = p.sem("s_out")

    @blk.gpsimd
    def _(g):
        s_out.wait(g)

    return p.finish()


_FUSED_NC = None


def kernel_fused(L0_attn_x, L0_attn_w_q, L0_attn_w_k, L0_attn_w_v,
                 L0_ffn_W_r, L0_ffn_W_k, L0_ffn_W_v, L0_ffn_mu_r, L0_ffn_mu_k,
                 L1_attn_W_in, L1_attn_conv_w, L1_attn_conv_b, L1_attn_W_x,
                 L1_attn_w_dt, L1_attn_b_dt, L1_attn_A_log, L1_attn_D_param,
                 L1_attn_W_out, L1_ffn_Wg, L1_ffn_W1, L1_ffn_b1, L1_ffn_W2,
                 L1_ffn_b2):
    global _FUSED_NC
    if _FUSED_NC is None:
        _FUSED_NC = build_fused()
    f = np.asarray
    x = np.ascontiguousarray(f(L0_attn_x), dtype=np.float32)
    sh = shard_x(x)
    mu = np.stack([f(L0_ffn_mu_r), f(L0_ffn_mu_k)]).astype(np.float32)
    Wg = f(L1_ffn_Wg)
    in_maps = []
    for c in range(NC):
        tb = attn_tables(c)
        cbidx, smask = l1attn_tables(c)
        msg, mcomb = l1ffn_tables(c)
        im = dict(x=sh[c])
        in_maps.append(im)
        im.update(dict(
            w_r=f(L0_ffn_W_r), w_k=f(L0_ffn_W_k), w_v=f(L0_ffn_W_v),
            mu_rk=mu, bidx=l0ffn_bidx(c),
            wq_at=f(L0_attn_w_q), wk_at=f(L0_attn_w_k), wv_at=f(L0_attn_w_v),
        ))
        im.update(dict(
            qidx=tb["qidx"], kidx=tb["kidx"], vidx=tb["vidx"], thr=tb["thr"],
            rf=tb["rf"], rb=tb["rb"], ysc=tb["ysc"],
            w_in=f(L1_attn_W_in), conv_w=f(L1_attn_conv_w).reshape(DIN, 4),
            conv_b=f(L1_attn_conv_b).reshape(1, DIN), w_x=f(L1_attn_W_x),
            w_dt=f(L1_attn_w_dt).reshape(1, DIN),
            b_dt=f(L1_attn_b_dt).reshape(1, DIN), a_log=f(L1_attn_A_log),
            d_par=f(L1_attn_D_param).reshape(1, DIN), w_out=f(L1_attn_W_out),
            cbidx=cbidx, smask=smask,
            wg=Wg, wg_my=np.ascontiguousarray(Wg[:, c:c + 1]),
            w1=np.ascontiguousarray(f(L1_ffn_W1)[c]),
            b1=np.ascontiguousarray(f(L1_ffn_b1)[c]).reshape(1, FF),
            w2=np.ascontiguousarray(f(L1_ffn_W2)[c]),
            b2=np.ascontiguousarray(f(L1_ffn_b2)[c]).reshape(1, D),
            msg_idx=msg, mc_idx=mcomb))
    res = run_bass_kernel_spmd(_FUSED_NC, in_maps, core_ids=list(range(NC)))
    return unshard_x([np.asarray(res.results[c]["y"]) for c in range(NC)])


kernel = kernel_fused
